# version 01
# baseline (speedup 1.0000x reference)
import math
import contextlib
import numpy as np
import ml_dtypes
import concourse.bass as bass
import concourse.mybir as mybir
from concourse.bass_utils import run_bass_kernel_spmd

F32 = mybir.dt.float32
BF16 = mybir.dt.bfloat16
AF = mybir.ActivationFunctionType
ALU = mybir.AluOpType
AX = mybir.AxisListType

B, T, D = 16, 2048, 2048
H, KVH, DH = 16, 4, 128
GRP = H // KVH
HID = 8192
NCORES = 8
SEQ_PER_CORE = B // NCORES
CHUNK = 512
TT = CHUNK // 128
DK = D // 128
NKT = T // 128
MASK_NEG = -30000.0
FFN_EPS = float(np.finfo(np.float32).eps)


class Sem:
    def __init__(self, handle):
        self.h = handle
        self.v = 0

    def inc(self, n=1):
        self.v += n
        return self.v


class Rec:
    def __init__(self, name):
        self.name = name
        self.ops = []

    def wait(self, sem, val=None):
        self.ops.append(("w", sem.h, sem.v if val is None else val))

    def call(self, meth, *args, **kwargs):
        self.ops.append(("c", meth, args, kwargs))

    def op(self, meth, *args, inc=None, incn=1, **kwargs):
        v = inc.inc(incn) if inc is not None else None
        self.ops.append(("o", meth, args, kwargs,
                         None if inc is None else (inc.h, incn)))
        return v

    def dma(self, out, in_, inc, incn=16):
        v = inc.inc(incn)
        self.ops.append(("o", "dma_start", (), dict(out=out, in_=in_),
                         (inc.h, incn)))
        return v

    def replay(self, eng):
        for it in self.ops:
            k = it[0]
            if k == "w":
                if it[2] > 0:
                    eng.wait_ge(it[1], it[2])
            elif k == "c":
                getattr(eng, it[1])(*it[2], **it[3])
            else:
                r = getattr(eng, it[1])(*it[2], **it[3])
                if it[4] is not None:
                    r.then_inc(it[4][0], it[4][1])


def build_program(n_chunks=2 * (T // CHUNK), dbg=False):
    nc = bass.Bass("TRN2", target_bir_lowering=False,
                   detect_race_conditions=False)

    x_in = nc.dram_tensor("x", [SEQ_PER_CORE, T, D], F32, kind="ExternalInput").ap()
    wq_in = nc.dram_tensor("wq", [D, H * DH], F32, kind="ExternalInput").ap()
    wk_in = nc.dram_tensor("wk", [D, KVH * DH], F32, kind="ExternalInput").ap()
    wv_in = nc.dram_tensor("wv", [D, KVH * DH], F32, kind="ExternalInput").ap()
    wg_in = nc.dram_tensor("wg", [D, D], F32, kind="ExternalInput").ap()
    scaler_in = nc.dram_tensor("scaler", [1, H], F32, kind="ExternalInput").ap()
    gnw_in = nc.dram_tensor("gnw", [1, DH], F32, kind="ExternalInput").ap()
    wgu_in = nc.dram_tensor("wgu", [2 * HID, D], F32, kind="ExternalInput").ap()
    wpr_in = nc.dram_tensor("wpr", [D, HID], F32, kind="ExternalInput").ap()
    out_d = nc.dram_tensor("out", [SEQ_PER_CORE, T, D], F32, kind="ExternalOutput").ap()
    if dbg:
        dbg_sg = nc.dram_tensor("dbg_sg", [128, TT * D], BF16, kind="ExternalOutput").ap()
        dbg_gg = nc.dram_tensor("dbg_gg", [128, TT * D], BF16, kind="ExternalOutput").ap()
        dbg_xnT = nc.dram_tensor("dbg_xnT", [128, DK * CHUNK], BF16, kind="ExternalOutput").ap()
        dbg_kT = nc.dram_tensor("dbg_kT", [128, KVH * T], BF16, kind="ExternalOutput").ap()
        dbg_v = nc.dram_tensor("dbg_v", [128, KVH * NKT * DH], BF16, kind="ExternalOutput").ap()

    theta = 10000.0
    inv_freq = 1.0 / theta ** (np.arange(0, DH, 2, dtype=np.float64) / DH)
    freqs = np.outer(np.arange(T, dtype=np.float64), inv_freq)
    cos_tiled = np.cos(freqs).astype(np.float32).reshape(NKT, 128, 64) \
        .transpose(1, 0, 2).reshape(128, -1)
    sin_tiled = np.sin(freqs).astype(np.float32).reshape(NKT, 128, 64) \
        .transpose(1, 0, 2).reshape(128, -1)
    lp_np = (np.log(np.arange(1, T + 1, dtype=np.float64)) / math.sqrt(DH)).astype(np.float32)
    lp_tiled = np.ascontiguousarray(lp_np.reshape(NKT, 128).T)
    qi = np.arange(128)[:, None]
    ki = np.arange(128)[None, :]
    m_causal = np.where(ki <= qi, 0.0, MASK_NEG).astype(np.float32)
    m_window = np.where(ki >= qi, 0.0, MASK_NEG).astype(np.float32)

    c_cos = nc.inline_tensor(cos_tiled, name="c_cos")
    c_sin = nc.inline_tensor(sin_tiled, name="c_sin")
    c_lp = nc.inline_tensor(lp_tiled, name="c_lp")
    c_mc = nc.inline_tensor(m_causal, name="c_mc")
    c_mw = nc.inline_tensor(m_window, name="c_mw")
    c_iden = nc.inline_tensor(np.eye(128, dtype=ml_dtypes.bfloat16), name="c_iden")
    c_idenf = nc.inline_tensor(np.eye(128, dtype=np.float32), name="c_idenf")
    c_onec = nc.inline_tensor(np.ones((128, 1), dtype=ml_dtypes.bfloat16), name="c_onec")
    c_oner = nc.inline_tensor(np.ones((1, 128), dtype=np.float32), name="c_oner")

    wq_s = nc.dram_tensor("wq_s", [D, H * DH], BF16, kind="Internal")
    wk_s = nc.dram_tensor("wk_s", [D, KVH * DH], BF16, kind="Internal")
    wv_s = nc.dram_tensor("wv_s", [D, KVH * DH], BF16, kind="Internal")
    wg_s = nc.dram_tensor("wg_s", [D, D], BF16, kind="Internal")
    gut_s = nc.dram_tensor("gut_s", [2 * HID, D], BF16, kind="Internal")
    prt_s = nc.dram_tensor("prt_s", [HID, D], BF16, kind="Internal")

    ctx = contextlib.ExitStack()
    sb = lambda n, s, d=F32: ctx.enter_context(nc.sbuf_tensor(n, s, d))
    mksem = lambda n: Sem(ctx.enter_context(nc.semaphore(n)))

    xnT = sb("xnT", [128, DK * CHUNK], BF16)
    ggf = sb("ggf", [128, TT * D], BF16)
    sgb = sb("sgb", [128, TT * D], BF16)
    kTc = sb("kTc", [128, KVH * T], BF16)
    vcc = sb("vcc", [128, KVH * NKT * DH], BF16)
    hpc = sb("hpc", [128, 16 * CHUNK], BF16)
    xo = [sb(f"xo{i}", [128, D]) for i in range(2)]
    xn_f = sb("xn_f", [128, D])
    fx = sb("fx", [128, D])
    xn_b = sb("xn_b", [128, D], BF16)
    qf = sb("qf", [128, 512])
    rt1 = sb("rt1", [128, 64]); rt2 = sb("rt2", [128, 64])
    qgb = sb("qgb", [128, GRP * 512], BF16)
    ktm = sb("ktm", [128, KVH * DH])
    sqt = sb("sqt", [128, GRP * DH])
    red16 = sb("red16", [128, 16])
    rstd16 = sb("rstd16", [128, 16])
    lnt16 = sb("lnt16", [128, 16])
    p_bf = [sb(f"p_bf{i}", [128, 1152], BF16) for i in range(2)]
    p_dg = sb("p_dg", [128, 128])
    pT_sb = [sb(f"pT_sb{i}", [128, 1152], BF16) for i in range(2)]
    y_grp = sb("y_grp", [128, GRP * DH])
    y2_t = sb("y2_t", [128, GRP * DH], BF16)
    nmx = [sb(f"nmx{i}", [128, 1]) for i in range(2)]
    lsum = [sb(f"lsum{i}", [128, 1]) for i in range(2)]
    ilv = [sb(f"ilv{i}", [128, 1]) for i in range(2)]
    lntA = [sb(f"lntA{i}", [128, 1]) for i in range(2)]
    sums2 = sb("sums2", [128, 2])
    meanv = sb("meanv", [128, 1]); negmean = sb("negmean", [128, 1])
    m2v = sb("m2v", [128, 1]); negm2 = sb("negm2", [128, 1]); varv = sb("varv", [128, 1])
    lnt1 = sb("lnt1", [128, 1]); rstd1 = sb("rstd1", [128, 1])
    rstd_t = sb("rstd_t", [128, TT]); lnt_t = sb("lnt_t", [128, TT])
    h2t = sb("h2t", [128, CHUNK], BF16)
    silut = sb("silut", [128, CHUNK])
    prodt = sb("prodt", [128, D], BF16)
    cosb = sb("cosb", [128, NKT * 64])
    sinb = sb("sinb", [128, NKT * 64])
    lpb = sb("lpb", [128, NKT])
    mcb = sb("mcb", [128, 128]); mwb = sb("mwb", [128, 128])
    idnb = sb("idnb", [128, 128], BF16)
    idnf = sb("idnf", [128, 128])
    onec = sb("onec", [128, 1], BF16)
    oner = sb("oner", [1, 128])
    gnb = sb("gnb", [128, DH], BF16)
    sclb = sb("sclb", [128, H])
    gn_row = sb("gn_row", [1, DH]); scl_row = sb("scl_row", [1, H])
    eps6 = sb("eps6", [128, 1]); eps5 = sb("eps5", [128, 1]); epsF = sb("epsF", [128, 1])
    neghalf = sb("neghalf", [128, 1])
    tiny = sb("tiny", [128, 8])
    wsl = [sb(f"wsl{i}", [128, DK * 256], BF16) for i in range(2)]
    gsl = [sb(f"gsl{i}", [128, D], BF16) for i in range(2)]
    psl = [sb(f"psl{i}", [128, 512], BF16) for i in range(2)]
    st_am = sb("st_am", [128, 32]); st_am3 = sb("st_am3", [128, 32])
    st_sc = sb("st_sc", [128, 32], BF16)
    st_ln = sb("st_ln", [128, 32]); st_rs = sb("st_rs", [128, 32])
    stf = xo
    st_A = xn_f
    st_B = fx
    stb = xn_b
    st_qb = prodt
    st_what = (ggf, 0)
    st_dd = (ggf, 2048)
    st_q = [(sgb, 0), (sgb, 2048)]
    st_tb = [(sgb, 4096), (sgb, 6144)]

    def carve(spec, w=D):
        t, off = spec
        return t[:, off:off + w]

    psp = contextlib.ExitStack()
    psA = psp.enter_context(nc.psum_tensor("psA", [128, 1152], F32))
    psB = psp.enter_context(nc.psum_tensor("psB", [128, 1152], F32))
    ptA = psp.enter_context(nc.psum_tensor("ptA", [128, 1024], BF16))
    aux = psp.enter_context(nc.psum_tensor("aux", [128, 512], F32))

    def pm(i):
        return (psA if i < 2 else psB)[:, (i % 2) * 512:(i % 2) * 512 + 512]

    s_ld = mksem("s_ld")
    s_x = [mksem(f"s_x{i}") for i in range(2)]
    s_xf = [mksem(f"s_xf{i}") for i in range(2)]
    s_wa = [mksem(f"s_wa{i}") for i in range(2)]
    s_waf = [mksem(f"s_waf{i}") for i in range(2)]
    s_wg = [mksem(f"s_wg{i}") for i in range(2)]
    s_wgf = [mksem(f"s_wgf{i}") for i in range(2)]
    s_wp = [mksem(f"s_wp{i}") for i in range(2)]
    s_wpf = [mksem(f"s_wpf{i}") for i in range(2)]
    s_sf = [mksem(f"s_sf{i}") for i in range(2)]
    s_sff = [mksem(f"s_sff{i}") for i in range(2)]
    s_sq = [mksem(f"s_sq{i}") for i in range(2)]
    s_sqf = [mksem(f"s_sqf{i}") for i in range(2)]
    s_stb = [mksem(f"s_stb{i}") for i in range(2)]
    s_stbf = [mksem(f"s_stbf{i}") for i in range(2)]
    s_scr = mksem("s_scr")
    s_pe = mksem("s_pe")
    s_ve = mksem("s_ve")
    s_se = mksem("s_se")
    s_out = mksem("s_out")
    s_prT = mksem("s_prT")
    s_prM = [mksem(f"s_prM{i}") for i in range(4)]
    s_prS = [mksem(f"s_prS{i}") for i in range(2)]
    s_prX = mksem("s_prX")

    SY, SC, VE, TE, GP = Rec("sync"), Rec("scalar"), Rec("vector"), Rec("tensor"), Rec("gpsimd")

    for dst, src in ((cosb, c_cos), (sinb, c_sin), (lpb, c_lp), (mcb, c_mc),
                     (mwb, c_mw), (idnb, c_iden), (idnf, c_idenf),
                     (onec, c_onec), (oner, c_oner)):
        SY.dma(dst[:], src.ap(), s_ld)
    SY.dma(gn_row[:], gnw_in, s_ld)
    SY.dma(scl_row[:], scaler_in, s_ld)

    VE.call("memset", eps6[:], 1e-6)
    VE.call("memset", eps5[:], 1e-5)
    VE.call("memset", epsF[:], FFN_EPS)
    VE.call("memset", neghalf[:], -0.5)

    TE.wait(s_ld)
    TE.call("matmul", aux[:, 0:DH], lhsT=oner[:], rhs=gn_row[:], start=True, stop=True)
    pev = TE.op("matmul", aux[:, 128:128 + H], lhsT=oner[:], rhs=scl_row[:],
                start=True, stop=True, inc=s_pe)
    SC.wait(s_pe, pev)
    SC.call("copy", out=gnb[:], in_=aux[:, 0:DH])
    sev0 = SC.op("copy", out=sclb[:], in_=aux[:, 128:128 + H], inc=s_se)
    SC.op("copy", out=tiny[:1, 0:2], in_=sclb[:1, 0:2], inc=s_prX)

    jobs = []
    for (src, dst, dout) in ((wq_in, wq_s, H * DH), (wk_in, wk_s, KVH * DH),
                             (wv_in, wv_s, KVH * DH), (wg_in, wg_s, D)):
        for rb in range(D // 128):
            jobs.append(("cast", src, dst, rb, dout))
    for rb in range(2 * HID // 128):
        jobs.append(("qgu", wgu_in, gut_s, rb, D))
    for rb in range(D // 128):
        for cb in range(HID // D):
            jobs.append(("qpr", wpr_in, prt_s, (rb, cb), D))

    g64 = lambda ap, w: ap[:, 0:w].rearrange("p (g e) -> p g e", e=64)
    g1 = lambda ap, w: ap[:, 0:w // 64].rearrange("p (g o) -> p g o", o=1)

    for ji, (mode, src, dst, rb, w) in enumerate(jobs):
        sl = ji % 2
        if ji >= 2:
            SY.wait(s_sff[sl])
        if mode == "qpr":
            r, cbb = rb
            src_ap = src[r * 128:(r + 1) * 128, cbb * D:(cbb + 1) * D]
        else:
            src_ap = src[rb * 128:(rb + 1) * 128, 0:w]
        ldv = SY.dma(stf[sl][:, 0:w], src_ap, s_sf[sl])

        if mode == "cast":
            SC.wait(s_sf[sl], ldv)
            if ji >= 2:
                SC.wait(s_sqf[sl])
            qv = SC.op("copy", out=carve(st_q[sl], w), in_=stf[sl][:, 0:w],
                       inc=s_sq[sl])
            SC.op("copy", out=tiny[:1, 0:2], in_=st_ln[:1, 0:2], inc=s_sff[sl])
            SY.wait(s_sq[sl], qv)
            scv = SY.dma(dst.ap()[rb * 128:(rb + 1) * 128, 0:w],
                         carve(st_q[sl], w), s_scr)
            SY.wait(s_scr, scv)
            SY.call("sem_inc", s_sqf[sl].h, 1)
            s_sqf[sl].inc()
        else:
            SC.wait(s_sf[sl], ldv)
            sev = SC.op("copy", out=stb[:, 0:w], in_=stf[sl][:, 0:w], inc=s_se)
            SC.op("copy", out=tiny[:1, 0:2], in_=st_ln[:1, 0:2], inc=s_sff[sl])
            VE.wait(s_se, sev)
            VE.call("tensor_reduce", g1(st_am[:], w), g64(stb[:], w), axis=AX.X,
                    op=ALU.add, apply_absolute_value=True)
            VE.call("drain")
            VE.call("tensor_scalar", st_am3[:, 0:w // 64], st_am[:, 0:w // 64],
                    scalar1=1.0 / 64.0, scalar2=1e-8, op0=ALU.mult, op1=ALU.max)
            VE.call("drain")
            vev = VE.op("tensor_copy", st_sc[:, 0:w // 64], st_am3[:, 0:w // 64],
                        inc=s_ve)
            SC.wait(s_ve, vev)
            SC.call("activation", out=st_ln[:, 0:w // 64], in_=st_sc[:, 0:w // 64],
                    func=AF.Ln)
            sev = SC.op("activation", out=st_rs[:, 0:w // 64], in_=st_ln[:, 0:w // 64],
                        func=AF.Exp, scale=-1.0, inc=s_se)
            VE.wait(s_se, sev)
            vev = VE.op("tensor_mul", g64(st_qb[:], w), g64(stb[:], w),
                        g1(st_rs[:], w).broadcast_to([128, w // 64, 64]), inc=s_ve)
            SC.wait(s_ve, vev)
            SC.call("activation", out=st_A[:, 0:w], in_=st_qb[:, 0:w], func=AF.Abs)
            SC.call("activation", out=st_A[:, 0:w], in_=st_A[:, 0:w], func=AF.Relu,
                    bias=neghalf[:, 0:1])
            SC.call("activation", out=st_A[:, 0:w], in_=st_A[:, 0:w], func=AF.Sign)
            sev = SC.op("activation", out=st_B[:, 0:w], in_=st_qb[:, 0:w],
                        func=AF.Sign, inc=s_se)
            VE.wait(s_se, sev)
            if ji >= 2:
                VE.wait(s_sqf[sl])
            VE.call("tensor_mul", st_B[:, 0:w], st_B[:, 0:w], st_A[:, 0:w])
            VE.call("tensor_mul", g64(carve(st_what), w), g64(st_B[:], w),
                    g1(st_sc[:], w).broadcast_to([128, w // 64, 64]))
            VE.call("tensor_sub", carve(st_dd, w), carve(st_what, w), stb[:, 0:w])
            qv = VE.op("tensor_add", carve(st_q[sl], w), stb[:, 0:w],
                       carve(st_dd, w), inc=s_sq[sl])
            TE.wait(s_sq[sl], qv)
            tsl = sl
            for hhf in range(2):
                TE.wait(s_prT)
                for k in range(8):
                    kk = hhf * 8 + k
                    if k == 7:
                        pev = TE.op("transpose", ptA[:, k * 128:(k + 1) * 128],
                                    in_=carve(st_q[sl], w)[:, kk * 128:(kk + 1) * 128],
                                    identity=idnb[:], inc=s_pe)
                    else:
                        TE.call("transpose", ptA[:, k * 128:(k + 1) * 128],
                                in_=carve(st_q[sl], w)[:, kk * 128:(kk + 1) * 128],
                                identity=idnb[:])
                SC.wait(s_pe, pev)
                if hhf == 0 and ji >= 2:
                    SC.wait(s_stbf[tsl])
                ev = SC.op("copy",
                           out=carve(st_tb[tsl], w)[:, hhf * 1024:(hhf + 1) * 1024],
                           in_=ptA[:], inc=s_stb[tsl])
                SC.op("copy", out=tiny[:1, 0:2], in_=st_ln[:1, 0:2], inc=s_prT)
            SC.call("sem_inc", s_sqf[sl].h, 1)
            s_sqf[sl].inc()
            SY.wait(s_stb[tsl])
            if mode == "qgu":
                scv = SY.dma(dst.ap()[rb * 128:(rb + 1) * 128, 0:D],
                             carve(st_tb[tsl], D), s_scr)
            else:
                r, cbb = rb
                dv = dst.ap()[cbb * D:(cbb + 1) * D, r * 128:(r + 1) * 128] \
                    .rearrange("(k i) j -> i k j", i=128)
                sv = carve(st_tb[tsl], D).rearrange("p (k j) -> p k j", k=16)
                scv = SY.dma(dv, sv, s_scr)
            SY.wait(s_scr, scv)
            SY.call("sem_inc", s_stbf[tsl].h, 1)
            s_stbf[tsl].inc()

    SY.wait(s_scr)
    SY.call("nop")


    airot = 0
    sqt_free_v = 0
    comb_done_v = 0
    ln_apply_v = 0

    def pe_wait_releases():
        for s in (s_prT, s_prS[0], s_prS[1], s_prX, *s_prM):
            TE.wait(s)

    GP.wait(s_sff[0])
    GP.wait(s_sff[1])

    for c in range(n_chunks):
        seq = c // (T // CHUNK)
        half = c % (T // CHUNK)
        tok0 = half * CHUNK
        gt0 = tok0 // 128

        for tt in range(TT):
            sl = tt % 2
            GP.wait(s_xf[sl])
            xv = GP.dma(xo[sl][:],
                        x_in[seq, tok0 + tt * 128: tok0 + (tt + 1) * 128, :], s_x[sl])
            SC.wait(s_x[sl], xv)
            if ln_apply_v:
                SC.wait(s_ve, ln_apply_v)
            SC.call("activation", out=xn_f[:], in_=xo[sl][:], func=AF.Identity,
                    accum_out=sums2[:, 0:1])
            SC.call("activation", out=xn_f[:], in_=xo[sl][:], func=AF.Square,
                    accum_out=sums2[:, 1:2])
            SC.call("mul", out=meanv[:], in_=sums2[:, 0:1], mul=1.0 / D)
            SC.call("mul", out=negmean[:], in_=meanv[:], mul=-1.0)
            SC.call("activation", out=m2v[:], in_=meanv[:], func=AF.Square)
            SC.call("mul", out=negm2[:], in_=m2v[:], mul=-1.0)
            SC.call("drain")
            SC.call("activation", out=varv[:], in_=sums2[:, 1:2], func=AF.Identity,
                    scale=1.0 / D, bias=negm2[:, 0:1])
            SC.call("drain")
            SC.call("activation", out=lnt1[:], in_=varv[:], func=AF.Ln, bias=eps5[:, 0:1])
            sev = SC.op("activation", out=rstd1[:], in_=lnt1[:], func=AF.Exp,
                        scale=-0.5, inc=s_se)
            VE.wait(s_se, sev)
            vev = VE.op("tensor_scalar", xn_f[:], xo[sl][:], scalar1=negmean[:, 0:1],
                        scalar2=rstd1[:, 0:1], op0=ALU.add, op1=ALU.mult, inc=s_ve)
            ln_apply_v = vev
            VE.op("tensor_copy", tiny[:1, 6:8], xn_f[:1, 0:2], inc=s_xf[sl])
            SC.wait(s_ve, vev)
            sev = SC.op("copy", out=xn_b[:], in_=xn_f[:], inc=s_se)
            TE.wait(s_se, sev)
            for hhf in range(2):
                TE.wait(s_prT)
                for k in range(8):
                    kk = hhf * 8 + k
                    if k == 7:
                        pev = TE.op("transpose", ptA[:, k * 128:(k + 1) * 128],
                                    in_=xn_b[:, kk * 128:(kk + 1) * 128],
                                    identity=idnb[:], inc=s_pe)
                    else:
                        TE.call("transpose", ptA[:, k * 128:(k + 1) * 128],
                                in_=xn_b[:, kk * 128:(kk + 1) * 128], identity=idnb[:])
                SC.wait(s_pe, pev)
                ov = xnT[:].rearrange("p (dk t) -> p dk t", t=CHUNK)[
                    :, hhf * 8:(hhf + 1) * 8, tt * 128:(tt + 1) * 128]
                iv = ptA[:].rearrange("p (dk t) -> p dk t", t=128)
                SC.call("copy", out=ov, in_=iv)
                SC.op("copy", out=tiny[:1, 0:2], in_=xn_b[:1, 0:2], inc=s_prT)
        sev_xnT = SC.op("copy", out=tiny[:1, 2:4], in_=xn_b[:1, 0:2], inc=s_se)

        mirot = 0
        pev_ss_prev = 0
        silut_ev_v = 0
        for p in range(4):
            for hc in range(16):
                mg, mu = mirot % 4, (mirot + 1) % 4
                mirot += 2
                wgver = {}
                for which, mi in (("g", mg), ("u", mu)):
                    hb = (0 if which == "g" else 64) + p * 16 + hc
                    sl = (2 * (p * 16 + hc) + (0 if which == "g" else 1)) % 2
                    SY.wait(s_wgf[sl])
                    wv_ = SY.dma(gsl[sl][:], gut_s.ap()[hb * 128:(hb + 1) * 128, :],
                                 s_wg[sl])
                    TE.wait(s_wg[sl], wv_)
                    if p == 0 and hc == 0 and which == "g":
                        TE.wait(s_prX)
                        TE.wait(s_se, sev_xnT)
                        pe_wait_releases()
                    TE.wait(s_prM[mi])
                    for dk in range(DK):
                        if dk == DK - 1:
                            wgver[which] = TE.op(
                                "matmul", pm(mi),
                                lhsT=gsl[sl][:, dk * 128:(dk + 1) * 128],
                                rhs=xnT[:, dk * CHUNK:(dk + 1) * CHUNK],
                                start=False, stop=True, inc=s_wgf[sl])
                        else:
                            TE.call("matmul", pm(mi),
                                    lhsT=gsl[sl][:, dk * 128:(dk + 1) * 128],
                                    rhs=xnT[:, dk * CHUNK:(dk + 1) * CHUNK],
                                    start=(dk == 0), stop=False)
                    wgver[which + "sl"] = sl
                SC.wait(s_wgf[wgver["gsl"]], wgver["g"])
                sev = SC.op("activation", out=silut[:], in_=pm(mg), func=AF.Silu,
                            inc=s_se)
                SC.op("copy", out=tiny[:1, 0:2], in_=silut[:1, 0:2], inc=s_prM[mg])
                VE.wait(s_se, sev)
                VE.wait(s_wgf[wgver["usl"]], wgver["u"])
                hv = VE.op("tensor_mul", hpc[:, hc * CHUNK:(hc + 1) * CHUNK],
                           silut[:], pm(mu), inc=s_ve)
                VE.op("tensor_copy", tiny[:1, 2:4], silut[:1, 0:2], inc=s_prM[mu])
                SC.wait(s_ve, hv)
                if pev_ss_prev:
                    SC.wait(s_pe, pev_ss_prev)
                sev = SC.op("activation", out=h2t[:],
                            in_=hpc[:, hc * CHUNK:(hc + 1) * CHUNK],
                            func=AF.Square, inc=s_se)
                TE.wait(s_se, sev)
                for tt in range(TT):
                    first = (p == 0 and hc == 0)
                    last = (p == 3 and hc == 15)
                    if tt == TT - 1:
                        pev_ss = TE.op("matmul", aux[:, tt:tt + 1],
                                       lhsT=h2t[:, tt * 128:(tt + 1) * 128],
                                       rhs=onec[:], start=first, stop=last, inc=s_pe)
                    else:
                        TE.call("matmul", aux[:, tt:tt + 1],
                                lhsT=h2t[:, tt * 128:(tt + 1) * 128], rhs=onec[:],
                                start=first, stop=last)
                pev_ss_prev = pev_ss
            if p == 3:
                SC.wait(s_pe, pev_ss)
                SC.call("activation", out=lnt_t[:], in_=aux[:, 0:TT], func=AF.Ln,
                        scale=1.0 / HID, bias=epsF[:, 0:1])
                sev_rstd = SC.op("activation", out=rstd_t[:], in_=lnt_t[:],
                                 func=AF.Exp, scale=-0.5, inc=s_se)
                SC.op("copy", out=tiny[:1, 0:2], in_=rstd_t[:1, 0:2], inc=s_prX)
            for dc in range(4):
                for tt in range(TT):
                    TE.wait(s_prM[tt])
                wpver = [None, None]
                for hc in range(16):
                    sl = hc % 2
                    SY.wait(s_wpf[sl])
                    wv_ = SY.dma(psl[sl][:],
                                 prt_s.ap()[p * 2048 + hc * 128:
                                            p * 2048 + (hc + 1) * 128,
                                            dc * 512:(dc + 1) * 512], s_wp[sl])
                    TE.wait(s_wp[sl], wv_)
                    for tt in range(TT):
                        if tt == TT - 1:
                            wpver[sl] = TE.op(
                                "matmul", pm(tt),
                                lhsT=hpc[:, hc * CHUNK + tt * 128:
                                         hc * CHUNK + (tt + 1) * 128],
                                rhs=psl[sl][:], start=(hc == 0), stop=(hc == 15),
                                inc=s_wpf[sl])
                        else:
                            TE.call("matmul", pm(tt),
                                    lhsT=hpc[:, hc * CHUNK + tt * 128:
                                             hc * CHUNK + (tt + 1) * 128],
                                    rhs=psl[sl][:], start=(hc == 0), stop=(hc == 15))
                for sl in range(2):
                    if wpver[sl] is not None:
                        SC.wait(s_wpf[sl], wpver[sl])
                        VE.wait(s_wpf[sl], wpver[sl])
                for tt in range(TT):
                    dst = ggf[:, tt * D + dc * 512: tt * D + (dc + 1) * 512]
                    if p == 0:
                        if dc == 0 and tt == 0 and comb_done_v:
                            SC.wait(s_ve, comb_done_v)
                        SC.call("copy", out=dst, in_=pm(tt))
                        SC.op("copy", out=tiny[:1, 0:2], in_=lnt_t[:1, 0:2],
                              inc=s_prM[tt])
                    elif p < 3:
                        VE.call("tensor_add", dst, dst, pm(tt))
                        VE.op("tensor_copy", tiny[:1, 0:2], silut[:1, 0:2],
                              inc=s_prM[tt])
                    else:
                        VE.wait(s_se, sev_rstd)
                        if silut_ev_v:
                            VE.wait(s_se, silut_ev_v)
                        VE.call("tensor_add", silut[:], dst, pm(tt))
                        VE.op("tensor_copy", tiny[:1, 0:2], silut[:1, 0:2],
                              inc=s_prM[tt])
                        VE.call("drain")
                        VE.call("tensor_scalar", silut[:], silut[:],
                                scalar1=rstd_t[:, tt:tt + 1], scalar2=None,
                                op0=ALU.mult)
                        vev = VE.op("tensor_copy", tiny[:1, 4:6], silut[:1, 0:2],
                                    inc=s_ve)
                        SC.wait(s_ve, vev)
                        silut_ev_v = SC.op("activation", out=dst, in_=silut[:],
                                           func=AF.Silu, inc=s_se)
        sev_gg = SC.op("copy", out=tiny[:1, 2:4], in_=lnt_t[:1, 0:2], inc=s_se)

        awrot = 0
        mirot = 0
        sqt_free_v = sqt_free_v if c > 0 else 0

        def attn_piece(src_s, colblk, width=256):
            nonlocal awrot
            sl = awrot % 2
            awrot += 1
            SY.wait(s_waf[sl])
            v = SY.dma(wsl[sl][:, 0:DK * width].rearrange("p (k n) -> p k n", n=width),
                       src_s.ap().rearrange("(k p) n -> p k n", p=128)[
                           :, :, colblk * width:(colblk + 1) * width], s_wa[sl])
            return sl, v

        def proj_tm(src_s, colblk, tgt, tt, width=256, last_inc=None):
            sl, wv_ = attn_piece(src_s, colblk, width)
            TE.wait(s_wa[sl], wv_)
            ver = None
            for dk in range(DK):
                if dk == DK - 1:
                    ver = TE.op("matmul", tgt,
                                lhsT=xnT[:, dk * CHUNK + tt * 128:
                                         dk * CHUNK + (tt + 1) * 128],
                                rhs=wsl[sl][:, dk * width:(dk + 1) * width],
                                start=False, stop=True, inc=s_waf[sl])
                else:
                    TE.call("matmul", tgt,
                            lhsT=xnT[:, dk * CHUNK + tt * 128:
                                     dk * CHUNK + (tt + 1) * 128],
                            rhs=wsl[sl][:, dk * width:(dk + 1) * width],
                            start=(dk == 0), stop=False)
            return sl, ver

        for tt in range(TT):
            gq = gt0 + tt
            mik = mirot % 4
            mirot += 1
            TE.wait(s_prM[mik])
            vers = []
            for cb2 in range(2):
                sl, ver = proj_tm(wk_s, cb2, pm(mik)[:, cb2 * 256:(cb2 + 1) * 256], tt)
                vers.append((sl, ver))
            for sl, ver in vers:
                SC.wait(s_waf[sl], ver)
            SC.call("copy", out=ktm[:], in_=pm(mik))
            if sqt_free_v:
                SC.wait(s_ve, sqt_free_v)
            sev = SC.op("activation", out=sqt[:], in_=ktm[:], func=AF.Square, inc=s_se)
            VE.wait(s_se, sev)
            VE.call("tensor_reduce",
                    red16[:, 0:4].rearrange("p (g o) -> p g o", o=1),
                    sqt[:].rearrange("p (g e) -> p g e", e=DH), axis=AX.X, op=ALU.add)
            VE.call("drain")
            vev = VE.op("tensor_copy", tiny[:1, 0:2], red16[:1, 0:2], inc=s_ve)
            SC.wait(s_ve, vev)
            SC.call("activation", out=lnt16[:, 0:4], in_=red16[:, 0:4], func=AF.Ln,
                    scale=1.0 / DH, bias=eps6[:, 0:1])
            SC.op("copy", out=tiny[:1, 0:2], in_=lnt16[:1, 0:2], inc=s_prM[mik])
            sev = SC.op("activation", out=rstd16[:, 0:4], in_=lnt16[:, 0:4],
                        func=AF.Exp, scale=-0.5, inc=s_se)
            VE.wait(s_se, sev)
            VE.call("tensor_mul",
                    ktm[:].rearrange("p (g e) -> p g e", e=DH),
                    ktm[:].rearrange("p (g e) -> p g e", e=DH),
                    rstd16[:, 0:4].rearrange("p (g o) -> p g o", o=1)
                    .broadcast_to([128, 4, DH]))
            for kv in range(KVH):
                x1 = ktm[:, kv * DH: kv * DH + 64]
                x2 = ktm[:, kv * DH + 64: kv * DH + DH]
                ct = cosb[:, gq * 64:(gq + 1) * 64]
                st = sinb[:, gq * 64:(gq + 1) * 64]
                VE.call("tensor_mul", rt1[:, 0:64], x1, st)
                VE.call("tensor_mul", rt2[:, 0:64], x2, st)
                VE.call("tensor_mul", x1, x1, ct)
                VE.call("tensor_mul", x2, x2, ct)
                VE.call("tensor_add", x1, x1, rt2[:, 0:64])
                VE.call("tensor_sub", x2, x2, rt1[:, 0:64])
            vev = VE.op("tensor_copy", tiny[:1, 0:2], ktm[:1, 0:2], inc=s_ve)
            SC.wait(s_ve, vev)
            kb = prodt[:, 0:512]
            sev = SC.op("copy", out=kb, in_=ktm[:], inc=s_se)
            TE.wait(s_se, sev)
            TE.wait(s_prT)
            for kv in range(KVH):
                if kv == KVH - 1:
                    pev = TE.op("transpose", ptA[:, kv * 128:(kv + 1) * 128],
                                in_=kb[:, kv * 128:(kv + 1) * 128],
                                identity=idnb[:], inc=s_pe)
                else:
                    TE.call("transpose", ptA[:, kv * 128:(kv + 1) * 128],
                            in_=kb[:, kv * 128:(kv + 1) * 128], identity=idnb[:])
            SC.wait(s_pe, pev)
            ovk = kTc[:].rearrange("p (kv t) -> p kv t", t=T)[
                :, :, tok0 + tt * 128: tok0 + (tt + 1) * 128]
            SC.call("copy", out=ovk,
                    in_=ptA[:, 0:512].rearrange("p (kv t) -> p kv t", t=128))
            SC.op("copy", out=tiny[:1, 0:2], in_=lnt16[:1, 0:2], inc=s_prT)
            miv = mirot % 4
            mirot += 1
            TE.wait(s_prM[miv])
            vers = []
            for cb2 in range(2):
                sl, ver = proj_tm(wv_s, cb2, pm(miv)[:, cb2 * 256:(cb2 + 1) * 256], tt)
                vers.append((sl, ver))
            for sl, ver in vers:
                SC.wait(s_waf[sl], ver)
            ovv = vcc[:].rearrange("p (kv kt e) -> p kv kt e", kv=KVH, e=DH)[
                :, :, gq, :]
            SC.call("copy", out=ovv,
                    in_=pm(miv).rearrange("p (kv e) -> p kv e", e=DH))
            SC.op("copy", out=tiny[:1, 0:2], in_=lnt16[:1, 0:2], inc=s_prM[miv])
            for cbp in range(4):
                mig = mirot % 4
                mirot += 1
                TE.wait(s_prM[mig])
                vers = []
                for cb2 in range(2):
                    sl, ver = proj_tm(wg_s, cbp * 2 + cb2,
                                      pm(mig)[:, cb2 * 256:(cb2 + 1) * 256], tt)
                    vers.append((sl, ver))
                for sl, ver in vers:
                    SC.wait(s_waf[sl], ver)
                if tt == 0 and cbp == 0 and comb_done_v:
                    SC.wait(s_ve, comb_done_v)
                SC.call("activation",
                        out=sgb[:, tt * D + cbp * 512: tt * D + (cbp + 1) * 512],
                        in_=pm(mig), func=AF.Silu)
                SC.op("copy", out=tiny[:1, 0:2], in_=lnt16[:1, 0:2], inc=s_prM[mig])

        for s_ in s_prM:
            TE.wait(s_)
        for gi in range(KVH):
            for hg in range(GRP):
                h = gi * GRP + hg
                sl, wv_ = attn_piece(wq_s, h, 128)
                TE.wait(s_wa[sl], wv_)
                TE.wait(s_prX)
                qver = None
                for tt in range(TT):
                    for dk in range(DK):
                        lastmm = (tt == TT - 1 and dk == DK - 1)
                        if lastmm:
                            qver = TE.op(
                                "matmul", aux[:, tt * 128:(tt + 1) * 128],
                                lhsT=xnT[:, dk * CHUNK + tt * 128:
                                         dk * CHUNK + (tt + 1) * 128],
                                rhs=wsl[sl][:, dk * 128:(dk + 1) * 128],
                                start=False, stop=True, inc=s_waf[sl])
                        else:
                            TE.call("matmul", aux[:, tt * 128:(tt + 1) * 128],
                                    lhsT=xnT[:, dk * CHUNK + tt * 128:
                                             dk * CHUNK + (tt + 1) * 128],
                                    rhs=wsl[sl][:, dk * 128:(dk + 1) * 128],
                                    start=(dk == 0), stop=(dk == DK - 1))
                SC.wait(s_waf[sl], qver)
                SC.call("copy", out=qf[:], in_=aux[:])
                if sqt_free_v:
                    SC.wait(s_ve, sqt_free_v)
                sev = SC.op("activation", out=sqt[:], in_=qf[:], func=AF.Square,
                            inc=s_se)
                VE.wait(s_se, sev)
                VE.call("tensor_reduce",
                        red16[:, 4:8].rearrange("p (g o) -> p g o", o=1),
                        sqt[:].rearrange("p (g e) -> p g e", e=DH), axis=AX.X,
                        op=ALU.add)
                VE.call("drain")
                vev = VE.op("tensor_copy", tiny[:1, 0:2], red16[:1, 4:6], inc=s_ve)
                SC.wait(s_ve, vev)
                SC.call("activation", out=lnt16[:, 4:8], in_=red16[:, 4:8],
                        func=AF.Ln, scale=1.0 / DH, bias=eps6[:, 0:1])
                SC.op("copy", out=tiny[:1, 0:2], in_=lnt16[:1, 4:6], inc=s_prX)
                sev = SC.op("activation", out=rstd16[:, 4:8], in_=lnt16[:, 4:8],
                            func=AF.Exp, scale=-0.5, inc=s_se)
                VE.wait(s_se, sev)
                VE.call("tensor_mul",
                        qf[:].rearrange("p (g e) -> p g e", e=DH),
                        qf[:].rearrange("p (g e) -> p g e", e=DH),
                        rstd16[:, 4:8].rearrange("p (g o) -> p g o", o=1)
                        .broadcast_to([128, 4, DH]))
                for tt in range(TT):
                    gq = gt0 + tt
                    x1 = qf[:, tt * 128: tt * 128 + 64]
                    x2 = qf[:, tt * 128 + 64: tt * 128 + 128]
                    ct = cosb[:, gq * 64:(gq + 1) * 64]
                    st = sinb[:, gq * 64:(gq + 1) * 64]
                    VE.call("tensor_mul", rt1[:, 0:64], x1, st)
                    VE.call("tensor_mul", rt2[:, 0:64], x2, st)
                    VE.call("tensor_mul", x1, x1, ct)
                    VE.call("tensor_mul", x2, x2, ct)
                    VE.call("tensor_add", x1, x1, rt2[:, 0:64])
                    VE.call("tensor_sub", x2, x2, rt1[:, 0:64])
                    VE.call("tensor_scalar", qf[:, tt * 128:(tt + 1) * 128],
                            qf[:, tt * 128:(tt + 1) * 128],
                            scalar1=sclb[:, h:h + 1],
                            scalar2=lpb[:, gq:gq + 1], op0=ALU.mult, op1=ALU.mult)
                vev = VE.op("tensor_copy", tiny[:1, 0:2], qf[:1, 0:2], inc=s_ve)
                SC.wait(s_ve, vev)
                qcb = prodt[:, 512:1024]
                sev = SC.op("copy", out=qcb, in_=qf[:], inc=s_se)
                TE.wait(s_se, sev)
                TE.wait(s_prT)
                for tt in range(TT):
                    if tt == TT - 1:
                        pev = TE.op("transpose", ptA[:, tt * 128:(tt + 1) * 128],
                                    in_=qcb[:, tt * 128:(tt + 1) * 128],
                                    identity=idnb[:], inc=s_pe)
                    else:
                        TE.call("transpose", ptA[:, tt * 128:(tt + 1) * 128],
                                in_=qcb[:, tt * 128:(tt + 1) * 128], identity=idnb[:])
                SC.wait(s_pe, pev)
                SC.call("copy", out=qgb[:, hg * 512:(hg + 1) * 512], in_=ptA[:, 0:512])
                sev_q = SC.op("copy", out=tiny[:1, 0:2], in_=lnt16[:1, 0:2], inc=s_prT)
            sev_qg = SC.op("copy", out=tiny[:1, 2:4], in_=lnt16[:1, 0:2], inc=s_se)

            for tt in range(TT):
                gq = gt0 + tt
                kt0 = max(0, gq - 8)
                nkt = gq + 1 - kt0
                for hg in range(GRP):
                    pp = airot % 2
                    buf = psA if pp == 0 else psB
                    TE.wait(s_prS[pp])
                    if hg == 0 and tt == 0:
                        TE.wait(s_se, sev_qg)
                    nspans = (nkt + 3) // 4
                    done = 0
                    for sp in range(nspans):
                        ln_ = min(4, nkt - done)
                        lastmm = (sp == nspans - 1)
                        if lastmm:
                            pev = TE.op(
                                "matmul", buf[:, done * 128:(done + ln_) * 128],
                                lhsT=qgb[:, hg * 512 + tt * 128:
                                         hg * 512 + (tt + 1) * 128],
                                rhs=kTc[:, gi * T + (kt0 + done) * 128:
                                        gi * T + (kt0 + done + ln_) * 128],
                                start=True, stop=True, inc=s_pe)
                        else:
                            TE.call("matmul", buf[:, done * 128:(done + ln_) * 128],
                                    lhsT=qgb[:, hg * 512 + tt * 128:
                                             hg * 512 + (tt + 1) * 128],
                                    rhs=kTc[:, gi * T + (kt0 + done) * 128:
                                            gi * T + (kt0 + done + ln_) * 128],
                                    start=True, stop=True)
                        done += ln_
                    VE.wait(s_pe, pev)
                    VE.call("tensor_add", buf[:, (nkt - 1) * 128: nkt * 128],
                            buf[:, (nkt - 1) * 128: nkt * 128], mcb[:])
                    if nkt == 9:
                        VE.call("tensor_add", buf[:, 0:128], buf[:, 0:128], mwb[:])
                    VE.call("reduce_max", nmx[pp][:], buf[:, 0:nkt * 128], axis=AX.X,
                            negate=True)
                    vev = VE.op("tensor_copy", tiny[:1, 0:1], nmx[pp][:1, 0:1],
                                inc=s_ve)
                    SC.wait(s_ve, vev)
                    SC.call("activation", out=p_bf[pp][:, 0:nkt * 128],
                            in_=buf[:, 0:nkt * 128], func=AF.Exp,
                            bias=nmx[pp][:, 0:1], accum_out=lsum[pp][:])
                    if nkt == 9:
                        SC.call("activation", out=p_dg[:], in_=buf[:, 1024:1152],
                                func=AF.Exp, bias=nmx[pp][:, 0:1])
                    SC.call("activation", out=lntA[pp][:], in_=lsum[pp][:], func=AF.Ln)
                    sev = SC.op("activation", out=ilv[pp][:], in_=lntA[pp][:],
                                func=AF.Exp, scale=-1.0, inc=s_se)
                    TE.wait(s_se, sev)
                    TE.wait(s_prT)
                    nbt = min(nkt, 8)
                    for j in range(nbt):
                        if j == nbt - 1:
                            pev = TE.op("transpose", ptA[:, j * 128:(j + 1) * 128],
                                        in_=p_bf[pp][:, j * 128:(j + 1) * 128],
                                        identity=idnb[:], inc=s_pe)
                        else:
                            TE.call("transpose", ptA[:, j * 128:(j + 1) * 128],
                                    in_=p_bf[pp][:, j * 128:(j + 1) * 128],
                                    identity=idnb[:])
                    if nkt == 9:
                        TE.wait(s_prX)
                        pev = TE.op("transpose", aux[:, 0:128], in_=p_dg[:],
                                    identity=idnf[:], inc=s_pe)
                    SC.wait(s_pe, pev)
                    SC.call("copy", out=pT_sb[pp][:, 0:nbt * 128], in_=ptA[:, 0:nbt * 128])
                    SC.op("copy", out=tiny[:1, 0:1], in_=lntA[pp][:1, 0:1], inc=s_prT)
                    if nkt == 9:
                        SC.call("copy", out=pT_sb[pp][:, 1024:1152], in_=aux[:, 0:128])
                        SC.op("copy", out=tiny[:1, 0:1], in_=lntA[pp][:1, 0:1],
                              inc=s_prX)
                    sev = SC.op("copy", out=tiny[:1, 2:3], in_=lntA[pp][:1, 0:1],
                                inc=s_se)
                    TE.wait(s_se, sev)
                    for j in range(nkt):
                        if j == nkt - 1:
                            pev = TE.op("matmul", buf[:, 0:128],
                                        lhsT=pT_sb[pp][:, j * 128:(j + 1) * 128],
                                        rhs=vcc[:, (gi * NKT + kt0 + j) * DH:
                                                (gi * NKT + kt0 + j + 1) * DH],
                                        start=(j == 0), stop=True, inc=s_pe)
                        else:
                            TE.call("matmul", buf[:, 0:128],
                                    lhsT=pT_sb[pp][:, j * 128:(j + 1) * 128],
                                    rhs=vcc[:, (gi * NKT + kt0 + j) * DH:
                                            (gi * NKT + kt0 + j + 1) * DH],
                                    start=(j == 0), stop=(j == nkt - 1))
                    VE.wait(s_pe, pev)
                    VE.call("tensor_scalar", y_grp[:, hg * 128:(hg + 1) * 128],
                            buf[:, 0:128], scalar1=ilv[pp][:, 0:1], scalar2=None,
                            op0=ALU.mult)
                    VE.op("tensor_copy", tiny[:1, 0:2], y_grp[:1, 0:2], inc=s_prS[pp])
                    airot += 1
                vev = VE.op("tensor_mul", y2_t[:], y_grp[:],
                            sgb[:, tt * D + gi * 512: tt * D + (gi + 1) * 512],
                            inc=s_ve)
                SC.wait(s_ve, vev)
                sev = SC.op("activation", out=sqt[:], in_=y2_t[:], func=AF.Square,
                            inc=s_se)
                VE.wait(s_se, sev)
                VE.call("tensor_reduce",
                        red16[:, 8:12].rearrange("p (g o) -> p g o", o=1),
                        sqt[:].rearrange("p (g e) -> p g e", e=DH), axis=AX.X,
                        op=ALU.add)
                VE.call("drain")
                vev = VE.op("tensor_copy", tiny[:1, 0:2], red16[:1, 8:10], inc=s_ve)
                sqt_free_v = vev
                SC.wait(s_ve, vev)
                SC.call("activation", out=lnt16[:, 8:12], in_=red16[:, 8:12],
                        func=AF.Ln, scale=1.0 / DH, bias=eps6[:, 0:1])
                sev = SC.op("activation", out=rstd16[:, 8:12], in_=lnt16[:, 8:12],
                            func=AF.Exp, scale=-0.5, inc=s_se)
                VE.wait(s_se, sev)
                VE.call("tensor_mul",
                        y2_t[:].rearrange("p (g e) -> p g e", e=DH),
                        y2_t[:].rearrange("p (g e) -> p g e", e=DH),
                        rstd16[:, 8:12].rearrange("p (g o) -> p g o", o=1)
                        .broadcast_to([128, 4, DH]))
                VE.call("tensor_mul",
                        sgb[:].rearrange("p (t g e) -> p t g e", t=TT, e=DH)[
                            :, tt, gi * GRP:(gi + 1) * GRP, :],
                        y2_t[:].rearrange("p (g e) -> p g e", e=DH),
                        gnb[:].rearrange("p (o e) -> p o e", o=1).broadcast_to([128, 4, DH]))

        VE.wait(s_se, sev_gg)
        for tt in range(TT):
            sl = tt % 2
            GP.wait(s_xf[sl])
            xv = GP.dma(xo[sl][:],
                        x_in[seq, tok0 + tt * 128: tok0 + (tt + 1) * 128, :], s_x[sl])
            VE.wait(s_x[sl], xv)
            VE.call("tensor_mul", prodt[:], sgb[:, tt * D:(tt + 1) * D],
                    ggf[:, tt * D:(tt + 1) * D])
            VE.call("drain")
            vev = VE.op("tensor_add", xo[sl][:], xo[sl][:], prodt[:], inc=s_ve)
            comb_done_v = vev
            GP.wait(s_ve, vev)
            ov = GP.dma(out_d[seq, tok0 + tt * 128: tok0 + (tt + 1) * 128, :],
                        xo[sl][:], s_out)
            GP.wait(s_out, ov)
            GP.call("sem_inc", s_xf[sl].h, 1)
            s_xf[sl].inc()

    if dbg:
        GP.dma(dbg_sg, sgb[:], s_out)
        GP.dma(dbg_gg, ggf[:], s_out)
        GP.dma(dbg_xnT, xnT[:], s_out)
        GP.dma(dbg_kT, kTc[:], s_out)
        GP.dma(dbg_v, vcc[:], s_out)
        GP.wait(s_out)

    GP.wait(s_out)

    with nc.Block() as block:
        block.sync(lambda e: SY.replay(e))
        block.scalar(lambda e: SC.replay(e))
        block.vector(lambda e: VE.replay(e))
        block.tensor(lambda e: TE.replay(e))
        block.gpsimd(lambda e: GP.replay(e))
    ctx.close()
    psp.close()
    return nc


_NC_CACHE = {}


def kernel(L0_attn_hidden_states, L0_attn_Wq, L0_attn_Wk, L0_attn_Wv, L0_attn_Wg,
           L0_attn_softmax_scaler, L0_attn_gn_weight, L0_ffn_w_gate_up,
           L0_ffn_w_proj, n_chunks=2 * (T // CHUNK)):
    x = np.ascontiguousarray(np.asarray(L0_attn_hidden_states, dtype=np.float32))
    wq = np.ascontiguousarray(np.asarray(L0_attn_Wq, dtype=np.float32))
    wk = np.ascontiguousarray(np.asarray(L0_attn_Wk, dtype=np.float32))
    wv = np.ascontiguousarray(np.asarray(L0_attn_Wv, dtype=np.float32))
    wg = np.ascontiguousarray(np.asarray(L0_attn_Wg, dtype=np.float32))
    scl = np.ascontiguousarray(np.asarray(L0_attn_softmax_scaler,
                                          dtype=np.float32)).reshape(1, H)
    gnw = np.ascontiguousarray(np.asarray(L0_attn_gn_weight,
                                          dtype=np.float32)).reshape(1, DH)
    wgu = np.ascontiguousarray(np.asarray(L0_ffn_w_gate_up, dtype=np.float32))
    wpr = np.ascontiguousarray(np.asarray(L0_ffn_w_proj, dtype=np.float32))

    key = (n_chunks, False)
    if key not in _NC_CACHE:
        _NC_CACHE[key] = build_program(n_chunks)
    nc = _NC_CACHE[key]

    in_maps = []
    for core in range(NCORES):
        xs = np.ascontiguousarray(x[core * SEQ_PER_CORE:(core + 1) * SEQ_PER_CORE])
        in_maps.append({"x": xs, "wq": wq, "wk": wk, "wv": wv, "wg": wg,
                        "scaler": scl, "gnw": gnw, "wgu": wgu, "wpr": wpr})
    res = run_bass_kernel_spmd(nc, in_maps, core_ids=list(range(NCORES)))
    out = np.concatenate([r["out"] for r in res.results], axis=0)
    return out.reshape(B, T, D)


# version 02
# speedup vs baseline: 1.0052x; 1.0052x over previous
import math
import contextlib
import numpy as np
import ml_dtypes
import concourse.bass as bass
import concourse.mybir as mybir
from concourse.bass_utils import run_bass_kernel_spmd

F32 = mybir.dt.float32
BF16 = mybir.dt.bfloat16
AF = mybir.ActivationFunctionType
ALU = mybir.AluOpType
AX = mybir.AxisListType

B, T, D = 16, 2048, 2048
H, KVH, DH = 16, 4, 128
GRP = H // KVH
HID = 8192
NCORES = 8
SEQ_PER_CORE = B // NCORES
CHUNK = 512
TT = CHUNK // 128
DK = D // 128
NKT = T // 128
MASK_NEG = -30000.0
FFN_EPS = float(np.finfo(np.float32).eps)


class Sem:
    def __init__(self, handle):
        self.h = handle
        self.v = 0

    def inc(self, n=1):
        self.v += n
        return self.v


class Rec:
    def __init__(self, name):
        self.name = name
        self.ops = []

    def wait(self, sem, val=None):
        self.ops.append(("w", sem.h, sem.v if val is None else val))

    def call(self, meth, *args, **kwargs):
        self.ops.append(("c", meth, args, kwargs))

    def op(self, meth, *args, inc=None, incn=1, **kwargs):
        v = inc.inc(incn) if inc is not None else None
        self.ops.append(("o", meth, args, kwargs,
                         None if inc is None else (inc.h, incn)))
        return v

    def dma(self, out, in_, inc, incn=16):
        v = inc.inc(incn)
        self.ops.append(("o", "dma_start", (), dict(out=out, in_=in_),
                         (inc.h, incn)))
        return v

    def replay(self, eng):
        for it in self.ops:
            k = it[0]
            if k == "w":
                if it[2] > 0:
                    eng.wait_ge(it[1], it[2])
            elif k == "c":
                getattr(eng, it[1])(*it[2], **it[3])
            else:
                r = getattr(eng, it[1])(*it[2], **it[3])
                if it[4] is not None:
                    r.then_inc(it[4][0], it[4][1])


def build_program(n_chunks=2 * (T // CHUNK), dbg=False):
    nc = bass.Bass("TRN2", target_bir_lowering=False,
                   detect_race_conditions=False)

    x_in = nc.dram_tensor("x", [SEQ_PER_CORE, T, D], F32, kind="ExternalInput").ap()
    wq_in = nc.dram_tensor("wq", [D, H * DH], F32, kind="ExternalInput").ap()
    wk_in = nc.dram_tensor("wk", [D, KVH * DH], F32, kind="ExternalInput").ap()
    wv_in = nc.dram_tensor("wv", [D, KVH * DH], F32, kind="ExternalInput").ap()
    wg_in = nc.dram_tensor("wg", [D, D], F32, kind="ExternalInput").ap()
    scaler_in = nc.dram_tensor("scaler", [1, H], F32, kind="ExternalInput").ap()
    gnw_in = nc.dram_tensor("gnw", [1, DH], F32, kind="ExternalInput").ap()
    wgu_in = nc.dram_tensor("wgu", [2 * HID, D], F32, kind="ExternalInput").ap()
    wpr_in = nc.dram_tensor("wpr", [D, HID], F32, kind="ExternalInput").ap()
    out_d = nc.dram_tensor("out", [SEQ_PER_CORE, T, D], F32, kind="ExternalOutput").ap()
    if dbg:
        dbg_sg = nc.dram_tensor("dbg_sg", [128, TT * D], BF16, kind="ExternalOutput").ap()
        dbg_gg = nc.dram_tensor("dbg_gg", [128, TT * D], BF16, kind="ExternalOutput").ap()
        dbg_xnT = nc.dram_tensor("dbg_xnT", [128, DK * CHUNK], BF16, kind="ExternalOutput").ap()
        dbg_kT = nc.dram_tensor("dbg_kT", [128, KVH * T], BF16, kind="ExternalOutput").ap()
        dbg_v = nc.dram_tensor("dbg_v", [128, KVH * NKT * DH], BF16, kind="ExternalOutput").ap()

    theta = 10000.0
    inv_freq = 1.0 / theta ** (np.arange(0, DH, 2, dtype=np.float64) / DH)
    freqs = np.outer(np.arange(T, dtype=np.float64), inv_freq)
    cos_tiled = np.cos(freqs).astype(np.float32).reshape(NKT, 128, 64) \
        .transpose(1, 0, 2).reshape(128, -1)
    sin_tiled = np.sin(freqs).astype(np.float32).reshape(NKT, 128, 64) \
        .transpose(1, 0, 2).reshape(128, -1)
    lp_np = (np.log(np.arange(1, T + 1, dtype=np.float64)) / math.sqrt(DH)).astype(np.float32)
    lp_tiled = np.ascontiguousarray(lp_np.reshape(NKT, 128).T)
    qi = np.arange(128)[:, None]
    ki = np.arange(128)[None, :]
    m_causal = np.where(ki <= qi, 0.0, MASK_NEG).astype(np.float32)
    m_window = np.where(ki >= qi, 0.0, MASK_NEG).astype(np.float32)

    c_cos = nc.inline_tensor(cos_tiled, name="c_cos")
    c_sin = nc.inline_tensor(sin_tiled, name="c_sin")
    c_lp = nc.inline_tensor(lp_tiled, name="c_lp")
    c_mc = nc.inline_tensor(m_causal, name="c_mc")
    c_mw = nc.inline_tensor(m_window, name="c_mw")
    c_iden = nc.inline_tensor(np.eye(128, dtype=ml_dtypes.bfloat16), name="c_iden")
    c_idenf = nc.inline_tensor(np.eye(128, dtype=np.float32), name="c_idenf")
    c_onec = nc.inline_tensor(np.ones((128, 1), dtype=ml_dtypes.bfloat16), name="c_onec")
    c_oner = nc.inline_tensor(np.ones((1, 128), dtype=np.float32), name="c_oner")

    wq_s = nc.dram_tensor("wq_s", [D, H * DH], BF16, kind="Internal")
    wk_s = nc.dram_tensor("wk_s", [D, KVH * DH], BF16, kind="Internal")
    wv_s = nc.dram_tensor("wv_s", [D, KVH * DH], BF16, kind="Internal")
    wg_s = nc.dram_tensor("wg_s", [D, D], BF16, kind="Internal")
    gut_s = nc.dram_tensor("gut_s", [2 * HID, D], BF16, kind="Internal")
    prt_s = nc.dram_tensor("prt_s", [HID, D], BF16, kind="Internal")

    ctx = contextlib.ExitStack()
    sb = lambda n, s, d=F32: ctx.enter_context(nc.sbuf_tensor(n, s, d))
    mksem = lambda n: Sem(ctx.enter_context(nc.semaphore(n)))

    xnT = sb("xnT", [128, DK * CHUNK], BF16)
    ggf = sb("ggf", [128, TT * D], BF16)
    sgb = sb("sgb", [128, TT * D], BF16)
    kTc = sb("kTc", [128, KVH * T], BF16)
    vcc = sb("vcc", [128, KVH * NKT * DH], BF16)
    hpc = sb("hpc", [128, 16 * CHUNK], BF16)
    xo = [sb(f"xo{i}", [128, D]) for i in range(2)]
    xn_f = sb("xn_f", [128, D])
    fx = sb("fx", [128, D])
    xn_b = sb("xn_b", [128, D], BF16)
    qf = sb("qf", [128, 512])
    rt1 = sb("rt1", [128, 64]); rt2 = sb("rt2", [128, 64])
    qgb = sb("qgb", [128, GRP * 512], BF16)
    ktm = sb("ktm", [128, KVH * DH])
    sqt = sb("sqt", [128, GRP * DH])
    red16 = sb("red16", [128, 16])
    rstd16 = sb("rstd16", [128, 16])
    lnt16 = sb("lnt16", [128, 16])
    p_bf = [sb(f"p_bf{i}", [128, 1152], BF16) for i in range(2)]
    p_dg = sb("p_dg", [128, 128])
    pT_sb = [sb(f"pT_sb{i}", [128, 1152], BF16) for i in range(2)]
    y_grp = sb("y_grp", [128, GRP * DH])
    y2_t = sb("y2_t", [128, GRP * DH], BF16)
    nmx = [sb(f"nmx{i}", [128, 1]) for i in range(2)]
    lsum = [sb(f"lsum{i}", [128, 1]) for i in range(2)]
    ilv = [sb(f"ilv{i}", [128, 1]) for i in range(2)]
    lntA = [sb(f"lntA{i}", [128, 1]) for i in range(2)]
    sums2 = sb("sums2", [128, 2])
    meanv = sb("meanv", [128, 1]); negmean = sb("negmean", [128, 1])
    m2v = sb("m2v", [128, 1]); negm2 = sb("negm2", [128, 1]); varv = sb("varv", [128, 1])
    lnt1 = sb("lnt1", [128, 1]); rstd1 = sb("rstd1", [128, 1])
    rstd_t = sb("rstd_t", [128, TT]); lnt_t = sb("lnt_t", [128, TT])
    h2t = sb("h2t", [128, CHUNK], BF16)
    silut = sb("silut", [128, CHUNK])
    prodt = sb("prodt", [128, D], BF16)
    cosb = sb("cosb", [128, NKT * 64])
    sinb = sb("sinb", [128, NKT * 64])
    lpb = sb("lpb", [128, NKT])
    mcb = sb("mcb", [128, 128]); mwb = sb("mwb", [128, 128])
    idnb = sb("idnb", [128, 128], BF16)
    idnf = sb("idnf", [128, 128])
    onec = sb("onec", [128, 1], BF16)
    oner = sb("oner", [1, 128])
    gnb = sb("gnb", [128, DH], BF16)
    sclb = sb("sclb", [128, H])
    gn_row = sb("gn_row", [1, DH]); scl_row = sb("scl_row", [1, H])
    eps6 = sb("eps6", [128, 1]); eps5 = sb("eps5", [128, 1]); epsF = sb("epsF", [128, 1])
    neghalf = sb("neghalf", [128, 1])
    tiny = sb("tiny", [128, 8])
    wsl = [sb(f"wsl{i}", [128, DK * 256], BF16) for i in range(2)]
    gsl = [sb(f"gsl{i}", [128, D], BF16) for i in range(2)]
    psl = [sb(f"psl{i}", [128, 512], BF16) for i in range(2)]
    st_am = sb("st_am", [128, 32]); st_am3 = sb("st_am3", [128, 32])
    st_sc = sb("st_sc", [128, 32], BF16)
    st_ln = sb("st_ln", [128, 32]); st_rs = sb("st_rs", [128, 32])
    stf = xo
    st_A = xn_f
    st_B = fx
    stb = xn_b
    st_qb = prodt
    st_what = (ggf, 0)
    st_dd = (ggf, 2048)
    st_q = [(sgb, 0), (sgb, 2048)]
    st_tb = [(sgb, 4096), (sgb, 6144)]

    def carve(spec, w=D):
        t, off = spec
        return t[:, off:off + w]

    psp = contextlib.ExitStack()
    psA = psp.enter_context(nc.psum_tensor("psA", [128, 1152], F32))
    psB = psp.enter_context(nc.psum_tensor("psB", [128, 1152], F32))
    ptA = psp.enter_context(nc.psum_tensor("ptA", [128, 1024], BF16))
    aux = psp.enter_context(nc.psum_tensor("aux", [128, 512], F32))

    def pm(i):
        return (psA if i < 2 else psB)[:, (i % 2) * 512:(i % 2) * 512 + 512]

    s_ld = mksem("s_ld")
    s_x = [mksem(f"s_x{i}") for i in range(2)]
    s_xf = [mksem(f"s_xf{i}") for i in range(2)]
    s_wa = [mksem(f"s_wa{i}") for i in range(2)]
    s_waf = [mksem(f"s_waf{i}") for i in range(2)]
    s_wg = [mksem(f"s_wg{i}") for i in range(2)]
    s_wgf = [mksem(f"s_wgf{i}") for i in range(2)]
    s_wp = [mksem(f"s_wp{i}") for i in range(2)]
    s_wpf = [mksem(f"s_wpf{i}") for i in range(2)]
    s_sf = [mksem(f"s_sf{i}") for i in range(2)]
    s_sff = [mksem(f"s_sff{i}") for i in range(2)]
    s_sq = [mksem(f"s_sq{i}") for i in range(2)]
    s_sqf = [mksem(f"s_sqf{i}") for i in range(2)]
    s_stb = [mksem(f"s_stb{i}") for i in range(2)]
    s_stbf = [mksem(f"s_stbf{i}") for i in range(2)]
    s_scr = mksem("s_scr")
    s_pe = mksem("s_pe")
    s_ve = mksem("s_ve")
    s_se = mksem("s_se")
    s_out = mksem("s_out")
    s_prT = mksem("s_prT")
    s_prM = [mksem(f"s_prM{i}") for i in range(4)]
    s_prS = [mksem(f"s_prS{i}") for i in range(2)]
    s_prX = mksem("s_prX")

    SY, SC, VE, TE, GP = Rec("sync"), Rec("scalar"), Rec("vector"), Rec("tensor"), Rec("gpsimd")

    for dst, src in ((cosb, c_cos), (sinb, c_sin), (lpb, c_lp), (mcb, c_mc),
                     (mwb, c_mw), (idnb, c_iden), (idnf, c_idenf),
                     (onec, c_onec), (oner, c_oner)):
        SY.dma(dst[:], src.ap(), s_ld)
    SY.dma(gn_row[:], gnw_in, s_ld)
    SY.dma(scl_row[:], scaler_in, s_ld)

    VE.call("memset", tiny[:], 0.0)
    VE.call("memset", eps6[:], 1e-6)
    VE.call("memset", eps5[:], 1e-5)
    VE.call("memset", epsF[:], FFN_EPS)
    VE.call("memset", neghalf[:], -0.5)

    TE.wait(s_ld)
    TE.call("matmul", aux[:, 0:DH], lhsT=oner[:], rhs=gn_row[:], start=True, stop=True)
    pev = TE.op("matmul", aux[:, 128:128 + H], lhsT=oner[:], rhs=scl_row[:],
                start=True, stop=True, inc=s_pe)
    SC.wait(s_pe, pev)
    SC.call("copy", out=gnb[:], in_=aux[:, 0:DH])
    sev0 = SC.op("copy", out=sclb[:], in_=aux[:, 128:128 + H], inc=s_se)
    SC.op("copy", out=tiny[:1, 0:2], in_=tiny[:1, 4:6], inc=s_prX)

    jobs = []
    for (src, dst, dout) in ((wq_in, wq_s, H * DH), (wk_in, wk_s, KVH * DH),
                             (wv_in, wv_s, KVH * DH), (wg_in, wg_s, D)):
        for rb in range(D // 128):
            jobs.append(("cast", src, dst, rb, dout))
    for rb in range(2 * HID // 128):
        jobs.append(("qgu", wgu_in, gut_s, rb, D))
    for rb in range(D // 128):
        for cb in range(HID // D):
            jobs.append(("qpr", wpr_in, prt_s, (rb, cb), D))

    g64 = lambda ap, w: ap[:, 0:w].rearrange("p (g e) -> p g e", e=64)
    g1 = lambda ap, w: ap[:, 0:w // 64].rearrange("p (g o) -> p g o", o=1)

    for ji, (mode, src, dst, rb, w) in enumerate(jobs):
        sl = ji % 2
        if ji >= 2:
            SY.wait(s_sff[sl])
        if mode == "qpr":
            r, cbb = rb
            src_ap = src[r * 128:(r + 1) * 128, cbb * D:(cbb + 1) * D]
        else:
            src_ap = src[rb * 128:(rb + 1) * 128, 0:w]
        ldv = SY.dma(stf[sl][:, 0:w], src_ap, s_sf[sl])

        if mode == "cast":
            SC.wait(s_sf[sl], ldv)
            if ji >= 2:
                SC.wait(s_sqf[sl])
            qv = SC.op("copy", out=carve(st_q[sl], w), in_=stf[sl][:, 0:w],
                       inc=s_sq[sl])
            SC.op("copy", out=tiny[:1, 0:2], in_=tiny[:1, 4:6], inc=s_sff[sl])
            SY.wait(s_sq[sl], qv)
            scv = SY.dma(dst.ap()[rb * 128:(rb + 1) * 128, 0:w],
                         carve(st_q[sl], w), s_scr)
            SY.wait(s_scr, scv)
            SY.call("sem_inc", s_sqf[sl].h, 1)
            s_sqf[sl].inc()
        else:
            SC.wait(s_sf[sl], ldv)
            sev = SC.op("copy", out=stb[:, 0:w], in_=stf[sl][:, 0:w], inc=s_se)
            SC.op("copy", out=tiny[:1, 0:2], in_=tiny[:1, 4:6], inc=s_sff[sl])
            VE.wait(s_se, sev)
            VE.call("tensor_reduce", g1(st_am[:], w), g64(stb[:], w), axis=AX.X,
                    op=ALU.add, apply_absolute_value=True)
            VE.call("drain")
            VE.call("tensor_scalar", st_am3[:, 0:w // 64], st_am[:, 0:w // 64],
                    scalar1=1.0 / 64.0, scalar2=1e-8, op0=ALU.mult, op1=ALU.max)
            VE.call("drain")
            vev = VE.op("tensor_copy", st_sc[:, 0:w // 64], st_am3[:, 0:w // 64],
                        inc=s_ve)
            SC.wait(s_ve, vev)
            SC.call("activation", out=st_ln[:, 0:w // 64], in_=st_sc[:, 0:w // 64],
                    func=AF.Ln)
            sev = SC.op("activation", out=st_rs[:, 0:w // 64], in_=st_ln[:, 0:w // 64],
                        func=AF.Exp, scale=-1.0, inc=s_se)
            VE.wait(s_se, sev)
            vev = VE.op("tensor_mul", g64(st_qb[:], w), g64(stb[:], w),
                        g1(st_rs[:], w).broadcast_to([128, w // 64, 64]), inc=s_ve)
            SC.wait(s_ve, vev)
            SC.call("activation", out=st_A[:, 0:w], in_=st_qb[:, 0:w], func=AF.Abs)
            SC.call("activation", out=st_A[:, 0:w], in_=st_A[:, 0:w], func=AF.Relu,
                    bias=neghalf[:, 0:1])
            SC.call("activation", out=st_A[:, 0:w], in_=st_A[:, 0:w], func=AF.Sign)
            sev = SC.op("activation", out=st_B[:, 0:w], in_=st_qb[:, 0:w],
                        func=AF.Sign, inc=s_se)
            VE.wait(s_se, sev)
            if ji >= 2:
                VE.wait(s_sqf[sl])
            VE.call("tensor_mul", st_B[:, 0:w], st_B[:, 0:w], st_A[:, 0:w])
            VE.call("tensor_mul", g64(carve(st_what), w), g64(st_B[:], w),
                    g1(st_sc[:], w).broadcast_to([128, w // 64, 64]))
            VE.call("tensor_sub", carve(st_dd, w), carve(st_what, w), stb[:, 0:w])
            qv = VE.op("tensor_add", carve(st_q[sl], w), stb[:, 0:w],
                       carve(st_dd, w), inc=s_sq[sl])
            TE.wait(s_sq[sl], qv)
            tsl = sl
            for hhf in range(2):
                TE.wait(s_prT)
                for k in range(8):
                    kk = hhf * 8 + k
                    if k == 7:
                        pev = TE.op("transpose", ptA[:, k * 128:(k + 1) * 128],
                                    in_=carve(st_q[sl], w)[:, kk * 128:(kk + 1) * 128],
                                    identity=idnb[:], inc=s_pe)
                    else:
                        TE.call("transpose", ptA[:, k * 128:(k + 1) * 128],
                                in_=carve(st_q[sl], w)[:, kk * 128:(kk + 1) * 128],
                                identity=idnb[:])
                SC.wait(s_pe, pev)
                if hhf == 0 and ji >= 2:
                    SC.wait(s_stbf[tsl])
                ev = SC.op("copy",
                           out=carve(st_tb[tsl], w)[:, hhf * 1024:(hhf + 1) * 1024],
                           in_=ptA[:], inc=s_stb[tsl])
                SC.op("copy", out=tiny[:1, 0:2], in_=tiny[:1, 4:6], inc=s_prT)
            SC.call("sem_inc", s_sqf[sl].h, 1)
            s_sqf[sl].inc()
            SY.wait(s_stb[tsl])
            if mode == "qgu":
                scv = SY.dma(dst.ap()[rb * 128:(rb + 1) * 128, 0:D],
                             carve(st_tb[tsl], D), s_scr)
            else:
                r, cbb = rb
                dv = dst.ap()[cbb * D:(cbb + 1) * D, r * 128:(r + 1) * 128] \
                    .rearrange("(k i) j -> i k j", i=128)
                sv = carve(st_tb[tsl], D).rearrange("p (k j) -> p k j", k=16)
                scv = SY.dma(dv, sv, s_scr)
            SY.wait(s_scr, scv)
            SY.call("sem_inc", s_stbf[tsl].h, 1)
            s_stbf[tsl].inc()

    SY.wait(s_scr)
    SY.call("nop")


    airot = 0
    sqt_free_v = 0
    comb_done_v = 0
    ln_apply_v = 0

    def pe_wait_releases():
        for s in (s_prT, s_prS[0], s_prS[1], s_prX, *s_prM):
            TE.wait(s)

    GP.wait(s_sff[0])
    GP.wait(s_sff[1])

    for c in range(n_chunks):
        seq = c // (T // CHUNK)
        half = c % (T // CHUNK)
        tok0 = half * CHUNK
        gt0 = tok0 // 128

        for tt in range(TT):
            sl = tt % 2
            GP.wait(s_xf[sl])
            xv = GP.dma(xo[sl][:],
                        x_in[seq, tok0 + tt * 128: tok0 + (tt + 1) * 128, :], s_x[sl])
            SC.wait(s_x[sl], xv)
            if ln_apply_v:
                SC.wait(s_ve, ln_apply_v)
            SC.call("activation", out=xn_f[:], in_=xo[sl][:], func=AF.Identity,
                    accum_out=sums2[:, 0:1])
            SC.call("activation", out=xn_f[:], in_=xo[sl][:], func=AF.Square,
                    accum_out=sums2[:, 1:2])
            SC.call("mul", out=meanv[:], in_=sums2[:, 0:1], mul=1.0 / D)
            SC.call("mul", out=negmean[:], in_=meanv[:], mul=-1.0)
            SC.call("activation", out=m2v[:], in_=meanv[:], func=AF.Square)
            SC.call("mul", out=negm2[:], in_=m2v[:], mul=-1.0)
            SC.call("drain")
            SC.call("activation", out=varv[:], in_=sums2[:, 1:2], func=AF.Identity,
                    scale=1.0 / D, bias=negm2[:, 0:1])
            SC.call("drain")
            SC.call("activation", out=lnt1[:], in_=varv[:], func=AF.Ln, bias=eps5[:, 0:1])
            sev = SC.op("activation", out=rstd1[:], in_=lnt1[:], func=AF.Exp,
                        scale=-0.5, inc=s_se)
            VE.wait(s_se, sev)
            vev = VE.op("tensor_scalar", xn_f[:], xo[sl][:], scalar1=negmean[:, 0:1],
                        scalar2=rstd1[:, 0:1], op0=ALU.add, op1=ALU.mult, inc=s_ve)
            ln_apply_v = vev
            VE.op("tensor_copy", tiny[:1, 6:8], xn_f[:1, 0:2], inc=s_xf[sl])
            SC.wait(s_ve, vev)
            sev = SC.op("copy", out=xn_b[:], in_=xn_f[:], inc=s_se)
            TE.wait(s_se, sev)
            for hhf in range(2):
                TE.wait(s_prT)
                for k in range(8):
                    kk = hhf * 8 + k
                    if k == 7:
                        pev = TE.op("transpose", ptA[:, k * 128:(k + 1) * 128],
                                    in_=xn_b[:, kk * 128:(kk + 1) * 128],
                                    identity=idnb[:], inc=s_pe)
                    else:
                        TE.call("transpose", ptA[:, k * 128:(k + 1) * 128],
                                in_=xn_b[:, kk * 128:(kk + 1) * 128], identity=idnb[:])
                SC.wait(s_pe, pev)
                ov = xnT[:].rearrange("p (dk t) -> p dk t", t=CHUNK)[
                    :, hhf * 8:(hhf + 1) * 8, tt * 128:(tt + 1) * 128]
                iv = ptA[:].rearrange("p (dk t) -> p dk t", t=128)
                SC.call("copy", out=ov, in_=iv)
                SC.op("copy", out=tiny[:1, 0:2], in_=tiny[:1, 4:6], inc=s_prT)
        sev_xnT = SC.op("copy", out=tiny[:1, 2:4], in_=tiny[:1, 4:6], inc=s_se)

        mirot = 0
        pev_ss_prev = 0
        silut_ev_v = 0
        for p in range(4):
            for hc in range(16):
                mg, mu = mirot % 4, (mirot + 1) % 4
                mirot += 2
                wgver = {}
                for which, mi in (("g", mg), ("u", mu)):
                    hb = (0 if which == "g" else 64) + p * 16 + hc
                    sl = (2 * (p * 16 + hc) + (0 if which == "g" else 1)) % 2
                    SY.wait(s_wgf[sl])
                    wv_ = SY.dma(gsl[sl][:], gut_s.ap()[hb * 128:(hb + 1) * 128, :],
                                 s_wg[sl])
                    TE.wait(s_wg[sl], wv_)
                    if p == 0 and hc == 0 and which == "g":
                        TE.wait(s_prX)
                        TE.wait(s_se, sev_xnT)
                        pe_wait_releases()
                    TE.wait(s_prM[mi])
                    for dk in range(DK):
                        if dk == DK - 1:
                            wgver[which] = TE.op(
                                "matmul", pm(mi),
                                lhsT=gsl[sl][:, dk * 128:(dk + 1) * 128],
                                rhs=xnT[:, dk * CHUNK:(dk + 1) * CHUNK],
                                start=False, stop=True, inc=s_wgf[sl])
                        else:
                            TE.call("matmul", pm(mi),
                                    lhsT=gsl[sl][:, dk * 128:(dk + 1) * 128],
                                    rhs=xnT[:, dk * CHUNK:(dk + 1) * CHUNK],
                                    start=(dk == 0), stop=False)
                    wgver[which + "sl"] = sl
                SC.wait(s_wgf[wgver["gsl"]], wgver["g"])
                sev = SC.op("activation", out=silut[:], in_=pm(mg), func=AF.Silu,
                            inc=s_se)
                SC.op("copy", out=tiny[:1, 0:2], in_=tiny[:1, 4:6], inc=s_prM[mg])
                VE.wait(s_se, sev)
                VE.wait(s_wgf[wgver["usl"]], wgver["u"])
                hv = VE.op("tensor_mul", hpc[:, hc * CHUNK:(hc + 1) * CHUNK],
                           silut[:], pm(mu), inc=s_ve)
                VE.op("tensor_copy", tiny[:1, 2:4], silut[:1, 0:2], inc=s_prM[mu])
                SC.wait(s_ve, hv)
                if pev_ss_prev:
                    SC.wait(s_pe, pev_ss_prev)
                sev = SC.op("activation", out=h2t[:],
                            in_=hpc[:, hc * CHUNK:(hc + 1) * CHUNK],
                            func=AF.Square, inc=s_se)
                TE.wait(s_se, sev)
                for tt in range(TT):
                    first = (p == 0 and hc == 0)
                    last = (p == 3 and hc == 15)
                    if tt == TT - 1:
                        pev_ss = TE.op("matmul", aux[:, tt:tt + 1],
                                       lhsT=h2t[:, tt * 128:(tt + 1) * 128],
                                       rhs=onec[:], start=first, stop=last, inc=s_pe)
                    else:
                        TE.call("matmul", aux[:, tt:tt + 1],
                                lhsT=h2t[:, tt * 128:(tt + 1) * 128], rhs=onec[:],
                                start=first, stop=last)
                pev_ss_prev = pev_ss
            if p == 3:
                SC.wait(s_pe, pev_ss)
                SC.call("activation", out=lnt_t[:], in_=aux[:, 0:TT], func=AF.Ln,
                        scale=1.0 / HID, bias=epsF[:, 0:1])
                sev_rstd = SC.op("activation", out=rstd_t[:], in_=lnt_t[:],
                                 func=AF.Exp, scale=-0.5, inc=s_se)
                SC.op("copy", out=tiny[:1, 0:2], in_=tiny[:1, 4:6], inc=s_prX)
            for dc in range(4):
                for tt in range(TT):
                    TE.wait(s_prM[tt])
                wpver = [None, None]
                for hc in range(16):
                    sl = hc % 2
                    SY.wait(s_wpf[sl])
                    wv_ = SY.dma(psl[sl][:],
                                 prt_s.ap()[p * 2048 + hc * 128:
                                            p * 2048 + (hc + 1) * 128,
                                            dc * 512:(dc + 1) * 512], s_wp[sl])
                    TE.wait(s_wp[sl], wv_)
                    for tt in range(TT):
                        if tt == TT - 1:
                            wpver[sl] = TE.op(
                                "matmul", pm(tt),
                                lhsT=hpc[:, hc * CHUNK + tt * 128:
                                         hc * CHUNK + (tt + 1) * 128],
                                rhs=psl[sl][:], start=(hc == 0), stop=(hc == 15),
                                inc=s_wpf[sl])
                        else:
                            TE.call("matmul", pm(tt),
                                    lhsT=hpc[:, hc * CHUNK + tt * 128:
                                             hc * CHUNK + (tt + 1) * 128],
                                    rhs=psl[sl][:], start=(hc == 0), stop=(hc == 15))
                for sl in range(2):
                    if wpver[sl] is not None:
                        SC.wait(s_wpf[sl], wpver[sl])
                        VE.wait(s_wpf[sl], wpver[sl])
                for tt in range(TT):
                    dst = ggf[:, tt * D + dc * 512: tt * D + (dc + 1) * 512]
                    if p == 0:
                        if dc == 0 and tt == 0 and comb_done_v:
                            SC.wait(s_ve, comb_done_v)
                        SC.call("copy", out=dst, in_=pm(tt))
                        SC.op("copy", out=tiny[:1, 0:2], in_=tiny[:1, 4:6],
                              inc=s_prM[tt])
                    elif p < 3:
                        VE.call("tensor_add", dst, dst, pm(tt))
                        VE.op("tensor_copy", tiny[:1, 0:2], silut[:1, 0:2],
                              inc=s_prM[tt])
                    else:
                        VE.wait(s_se, sev_rstd)
                        if silut_ev_v:
                            VE.wait(s_se, silut_ev_v)
                        VE.call("tensor_add", silut[:], dst, pm(tt))
                        VE.op("tensor_copy", tiny[:1, 0:2], silut[:1, 0:2],
                              inc=s_prM[tt])
                        VE.call("drain")
                        VE.call("tensor_scalar", silut[:], silut[:],
                                scalar1=rstd_t[:, tt:tt + 1], scalar2=None,
                                op0=ALU.mult)
                        vev = VE.op("tensor_copy", tiny[:1, 4:6], silut[:1, 0:2],
                                    inc=s_ve)
                        SC.wait(s_ve, vev)
                        silut_ev_v = SC.op("activation", out=dst, in_=silut[:],
                                           func=AF.Silu, inc=s_se)
        sev_gg = SC.op("copy", out=tiny[:1, 2:4], in_=tiny[:1, 4:6], inc=s_se)

        awrot = 0
        mirot = 0
        sqt_free_v = sqt_free_v if c > 0 else 0

        def attn_piece(src_s, colblk, width=256):
            nonlocal awrot
            sl = awrot % 2
            awrot += 1
            SY.wait(s_waf[sl])
            v = SY.dma(wsl[sl][:, 0:DK * width].rearrange("p (k n) -> p k n", n=width),
                       src_s.ap().rearrange("(k p) n -> p k n", p=128)[
                           :, :, colblk * width:(colblk + 1) * width], s_wa[sl])
            return sl, v

        def proj_tm(src_s, colblk, tgt, tt, width=256, last_inc=None):
            sl, wv_ = attn_piece(src_s, colblk, width)
            TE.wait(s_wa[sl], wv_)
            ver = None
            for dk in range(DK):
                if dk == DK - 1:
                    ver = TE.op("matmul", tgt,
                                lhsT=xnT[:, dk * CHUNK + tt * 128:
                                         dk * CHUNK + (tt + 1) * 128],
                                rhs=wsl[sl][:, dk * width:(dk + 1) * width],
                                start=False, stop=True, inc=s_waf[sl])
                else:
                    TE.call("matmul", tgt,
                            lhsT=xnT[:, dk * CHUNK + tt * 128:
                                     dk * CHUNK + (tt + 1) * 128],
                            rhs=wsl[sl][:, dk * width:(dk + 1) * width],
                            start=(dk == 0), stop=False)
            return sl, ver

        for tt in range(TT):
            gq = gt0 + tt
            mik = mirot % 4
            mirot += 1
            TE.wait(s_prM[mik])
            vers = []
            for cb2 in range(2):
                sl, ver = proj_tm(wk_s, cb2, pm(mik)[:, cb2 * 256:(cb2 + 1) * 256], tt)
                vers.append((sl, ver))
            for sl, ver in vers:
                SC.wait(s_waf[sl], ver)
            SC.call("copy", out=ktm[:], in_=pm(mik))
            if sqt_free_v:
                SC.wait(s_ve, sqt_free_v)
            sev = SC.op("activation", out=sqt[:], in_=ktm[:], func=AF.Square, inc=s_se)
            VE.wait(s_se, sev)
            VE.call("tensor_reduce",
                    red16[:, 0:4].rearrange("p (g o) -> p g o", o=1),
                    sqt[:].rearrange("p (g e) -> p g e", e=DH), axis=AX.X, op=ALU.add)
            VE.call("drain")
            vev = VE.op("tensor_copy", tiny[:1, 0:2], red16[:1, 0:2], inc=s_ve)
            SC.wait(s_ve, vev)
            SC.call("activation", out=lnt16[:, 0:4], in_=red16[:, 0:4], func=AF.Ln,
                    scale=1.0 / DH, bias=eps6[:, 0:1])
            SC.op("copy", out=tiny[:1, 0:2], in_=tiny[:1, 4:6], inc=s_prM[mik])
            sev = SC.op("activation", out=rstd16[:, 0:4], in_=lnt16[:, 0:4],
                        func=AF.Exp, scale=-0.5, inc=s_se)
            VE.wait(s_se, sev)
            VE.call("tensor_mul",
                    ktm[:].rearrange("p (g e) -> p g e", e=DH),
                    ktm[:].rearrange("p (g e) -> p g e", e=DH),
                    rstd16[:, 0:4].rearrange("p (g o) -> p g o", o=1)
                    .broadcast_to([128, 4, DH]))
            for kv in range(KVH):
                x1 = ktm[:, kv * DH: kv * DH + 64]
                x2 = ktm[:, kv * DH + 64: kv * DH + DH]
                ct = cosb[:, gq * 64:(gq + 1) * 64]
                st = sinb[:, gq * 64:(gq + 1) * 64]
                VE.call("tensor_mul", rt1[:, 0:64], x1, st)
                VE.call("tensor_mul", rt2[:, 0:64], x2, st)
                VE.call("tensor_mul", x1, x1, ct)
                VE.call("tensor_mul", x2, x2, ct)
                VE.call("tensor_add", x1, x1, rt2[:, 0:64])
                VE.call("tensor_sub", x2, x2, rt1[:, 0:64])
            vev = VE.op("tensor_copy", tiny[:1, 0:2], ktm[:1, 0:2], inc=s_ve)
            SC.wait(s_ve, vev)
            kb = prodt[:, 0:512]
            sev = SC.op("copy", out=kb, in_=ktm[:], inc=s_se)
            TE.wait(s_se, sev)
            TE.wait(s_prT)
            for kv in range(KVH):
                if kv == KVH - 1:
                    pev = TE.op("transpose", ptA[:, kv * 128:(kv + 1) * 128],
                                in_=kb[:, kv * 128:(kv + 1) * 128],
                                identity=idnb[:], inc=s_pe)
                else:
                    TE.call("transpose", ptA[:, kv * 128:(kv + 1) * 128],
                            in_=kb[:, kv * 128:(kv + 1) * 128], identity=idnb[:])
            SC.wait(s_pe, pev)
            ovk = kTc[:].rearrange("p (kv t) -> p kv t", t=T)[
                :, :, tok0 + tt * 128: tok0 + (tt + 1) * 128]
            SC.call("copy", out=ovk,
                    in_=ptA[:, 0:512].rearrange("p (kv t) -> p kv t", t=128))
            SC.op("copy", out=tiny[:1, 0:2], in_=tiny[:1, 4:6], inc=s_prT)
            miv = mirot % 4
            mirot += 1
            TE.wait(s_prM[miv])
            vers = []
            for cb2 in range(2):
                sl, ver = proj_tm(wv_s, cb2, pm(miv)[:, cb2 * 256:(cb2 + 1) * 256], tt)
                vers.append((sl, ver))
            for sl, ver in vers:
                SC.wait(s_waf[sl], ver)
            ovv = vcc[:].rearrange("p (kv kt e) -> p kv kt e", kv=KVH, e=DH)[
                :, :, gq, :]
            SC.call("copy", out=ovv,
                    in_=pm(miv).rearrange("p (kv e) -> p kv e", e=DH))
            SC.op("copy", out=tiny[:1, 0:2], in_=tiny[:1, 4:6], inc=s_prM[miv])
            for cbp in range(4):
                mig = mirot % 4
                mirot += 1
                TE.wait(s_prM[mig])
                vers = []
                for cb2 in range(2):
                    sl, ver = proj_tm(wg_s, cbp * 2 + cb2,
                                      pm(mig)[:, cb2 * 256:(cb2 + 1) * 256], tt)
                    vers.append((sl, ver))
                for sl, ver in vers:
                    SC.wait(s_waf[sl], ver)
                if tt == 0 and cbp == 0 and comb_done_v:
                    SC.wait(s_ve, comb_done_v)
                SC.call("activation",
                        out=sgb[:, tt * D + cbp * 512: tt * D + (cbp + 1) * 512],
                        in_=pm(mig), func=AF.Silu)
                SC.op("copy", out=tiny[:1, 0:2], in_=tiny[:1, 4:6], inc=s_prM[mig])

        for s_ in s_prM:
            TE.wait(s_)
        for gi in range(KVH):
            for hg in range(GRP):
                h = gi * GRP + hg
                sl, wv_ = attn_piece(wq_s, h, 128)
                TE.wait(s_wa[sl], wv_)
                TE.wait(s_prX)
                qver = None
                for tt in range(TT):
                    for dk in range(DK):
                        lastmm = (tt == TT - 1 and dk == DK - 1)
                        if lastmm:
                            qver = TE.op(
                                "matmul", aux[:, tt * 128:(tt + 1) * 128],
                                lhsT=xnT[:, dk * CHUNK + tt * 128:
                                         dk * CHUNK + (tt + 1) * 128],
                                rhs=wsl[sl][:, dk * 128:(dk + 1) * 128],
                                start=False, stop=True, inc=s_waf[sl])
                        else:
                            TE.call("matmul", aux[:, tt * 128:(tt + 1) * 128],
                                    lhsT=xnT[:, dk * CHUNK + tt * 128:
                                             dk * CHUNK + (tt + 1) * 128],
                                    rhs=wsl[sl][:, dk * 128:(dk + 1) * 128],
                                    start=(dk == 0), stop=(dk == DK - 1))
                SC.wait(s_waf[sl], qver)
                SC.call("copy", out=qf[:], in_=aux[:])
                if sqt_free_v:
                    SC.wait(s_ve, sqt_free_v)
                sev = SC.op("activation", out=sqt[:], in_=qf[:], func=AF.Square,
                            inc=s_se)
                VE.wait(s_se, sev)
                VE.call("tensor_reduce",
                        red16[:, 4:8].rearrange("p (g o) -> p g o", o=1),
                        sqt[:].rearrange("p (g e) -> p g e", e=DH), axis=AX.X,
                        op=ALU.add)
                VE.call("drain")
                vev = VE.op("tensor_copy", tiny[:1, 0:2], red16[:1, 4:6], inc=s_ve)
                SC.wait(s_ve, vev)
                SC.call("activation", out=lnt16[:, 4:8], in_=red16[:, 4:8],
                        func=AF.Ln, scale=1.0 / DH, bias=eps6[:, 0:1])
                SC.op("copy", out=tiny[:1, 0:2], in_=tiny[:1, 4:6], inc=s_prX)
                sev = SC.op("activation", out=rstd16[:, 4:8], in_=lnt16[:, 4:8],
                            func=AF.Exp, scale=-0.5, inc=s_se)
                VE.wait(s_se, sev)
                VE.call("tensor_mul",
                        qf[:].rearrange("p (g e) -> p g e", e=DH),
                        qf[:].rearrange("p (g e) -> p g e", e=DH),
                        rstd16[:, 4:8].rearrange("p (g o) -> p g o", o=1)
                        .broadcast_to([128, 4, DH]))
                for tt in range(TT):
                    gq = gt0 + tt
                    x1 = qf[:, tt * 128: tt * 128 + 64]
                    x2 = qf[:, tt * 128 + 64: tt * 128 + 128]
                    ct = cosb[:, gq * 64:(gq + 1) * 64]
                    st = sinb[:, gq * 64:(gq + 1) * 64]
                    VE.call("tensor_mul", rt1[:, 0:64], x1, st)
                    VE.call("tensor_mul", rt2[:, 0:64], x2, st)
                    VE.call("tensor_mul", x1, x1, ct)
                    VE.call("tensor_mul", x2, x2, ct)
                    VE.call("tensor_add", x1, x1, rt2[:, 0:64])
                    VE.call("tensor_sub", x2, x2, rt1[:, 0:64])
                    VE.call("tensor_scalar", qf[:, tt * 128:(tt + 1) * 128],
                            qf[:, tt * 128:(tt + 1) * 128],
                            scalar1=sclb[:, h:h + 1],
                            scalar2=lpb[:, gq:gq + 1], op0=ALU.mult, op1=ALU.mult)
                vev = VE.op("tensor_copy", tiny[:1, 0:2], qf[:1, 0:2], inc=s_ve)
                SC.wait(s_ve, vev)
                qcb = prodt[:, 512:1024]
                sev = SC.op("copy", out=qcb, in_=qf[:], inc=s_se)
                TE.wait(s_se, sev)
                TE.wait(s_prT)
                for tt in range(TT):
                    if tt == TT - 1:
                        pev = TE.op("transpose", ptA[:, tt * 128:(tt + 1) * 128],
                                    in_=qcb[:, tt * 128:(tt + 1) * 128],
                                    identity=idnb[:], inc=s_pe)
                    else:
                        TE.call("transpose", ptA[:, tt * 128:(tt + 1) * 128],
                                in_=qcb[:, tt * 128:(tt + 1) * 128], identity=idnb[:])
                SC.wait(s_pe, pev)
                SC.call("copy", out=qgb[:, hg * 512:(hg + 1) * 512], in_=ptA[:, 0:512])
                sev_q = SC.op("copy", out=tiny[:1, 0:2], in_=tiny[:1, 4:6], inc=s_prT)
            sev_qg = SC.op("copy", out=tiny[:1, 2:4], in_=tiny[:1, 4:6], inc=s_se)

            for tt in range(TT):
                gq = gt0 + tt
                kt0 = max(0, gq - 8)
                nkt = gq + 1 - kt0
                for hg in range(GRP):
                    pp = airot % 2
                    buf = psA if pp == 0 else psB
                    TE.wait(s_prS[pp])
                    if hg == 0 and tt == 0:
                        TE.wait(s_se, sev_qg)
                    nspans = (nkt + 3) // 4
                    done = 0
                    for sp in range(nspans):
                        ln_ = min(4, nkt - done)
                        lastmm = (sp == nspans - 1)
                        if lastmm:
                            pev = TE.op(
                                "matmul", buf[:, done * 128:(done + ln_) * 128],
                                lhsT=qgb[:, hg * 512 + tt * 128:
                                         hg * 512 + (tt + 1) * 128],
                                rhs=kTc[:, gi * T + (kt0 + done) * 128:
                                        gi * T + (kt0 + done + ln_) * 128],
                                start=True, stop=True, inc=s_pe)
                        else:
                            TE.call("matmul", buf[:, done * 128:(done + ln_) * 128],
                                    lhsT=qgb[:, hg * 512 + tt * 128:
                                             hg * 512 + (tt + 1) * 128],
                                    rhs=kTc[:, gi * T + (kt0 + done) * 128:
                                            gi * T + (kt0 + done + ln_) * 128],
                                    start=True, stop=True)
                        done += ln_
                    VE.wait(s_pe, pev)
                    VE.call("tensor_add", buf[:, (nkt - 1) * 128: nkt * 128],
                            buf[:, (nkt - 1) * 128: nkt * 128], mcb[:])
                    if nkt == 9:
                        VE.call("tensor_add", buf[:, 0:128], buf[:, 0:128], mwb[:])
                    VE.call("reduce_max", nmx[pp][:], buf[:, 0:nkt * 128], axis=AX.X,
                            negate=True)
                    vev = VE.op("tensor_copy", tiny[:1, 0:1], nmx[pp][:1, 0:1],
                                inc=s_ve)
                    SC.wait(s_ve, vev)
                    SC.call("activation", out=p_bf[pp][:, 0:nkt * 128],
                            in_=buf[:, 0:nkt * 128], func=AF.Exp,
                            bias=nmx[pp][:, 0:1], accum_out=lsum[pp][:])
                    if nkt == 9:
                        SC.call("activation", out=p_dg[:], in_=buf[:, 1024:1152],
                                func=AF.Exp, bias=nmx[pp][:, 0:1])
                    SC.call("activation", out=lntA[pp][:], in_=lsum[pp][:], func=AF.Ln)
                    sev = SC.op("activation", out=ilv[pp][:], in_=lntA[pp][:],
                                func=AF.Exp, scale=-1.0, inc=s_se)
                    TE.wait(s_se, sev)
                    TE.wait(s_prT)
                    nbt = min(nkt, 8)
                    for j in range(nbt):
                        if j == nbt - 1:
                            pev = TE.op("transpose", ptA[:, j * 128:(j + 1) * 128],
                                        in_=p_bf[pp][:, j * 128:(j + 1) * 128],
                                        identity=idnb[:], inc=s_pe)
                        else:
                            TE.call("transpose", ptA[:, j * 128:(j + 1) * 128],
                                    in_=p_bf[pp][:, j * 128:(j + 1) * 128],
                                    identity=idnb[:])
                    if nkt == 9:
                        TE.wait(s_prX)
                        pev = TE.op("transpose", aux[:, 0:128], in_=p_dg[:],
                                    identity=idnf[:], inc=s_pe)
                    SC.wait(s_pe, pev)
                    SC.call("copy", out=pT_sb[pp][:, 0:nbt * 128], in_=ptA[:, 0:nbt * 128])
                    SC.op("copy", out=tiny[:1, 0:1], in_=tiny[:1, 4:5], inc=s_prT)
                    if nkt == 9:
                        SC.call("copy", out=pT_sb[pp][:, 1024:1152], in_=aux[:, 0:128])
                        SC.op("copy", out=tiny[:1, 0:1], in_=tiny[:1, 4:5],
                              inc=s_prX)
                    sev = SC.op("copy", out=tiny[:1, 2:3], in_=tiny[:1, 4:5],
                                inc=s_se)
                    TE.wait(s_se, sev)
                    for j in range(nkt):
                        if j == nkt - 1:
                            pev = TE.op("matmul", buf[:, 0:128],
                                        lhsT=pT_sb[pp][:, j * 128:(j + 1) * 128],
                                        rhs=vcc[:, (gi * NKT + kt0 + j) * DH:
                                                (gi * NKT + kt0 + j + 1) * DH],
                                        start=(j == 0), stop=True, inc=s_pe)
                        else:
                            TE.call("matmul", buf[:, 0:128],
                                    lhsT=pT_sb[pp][:, j * 128:(j + 1) * 128],
                                    rhs=vcc[:, (gi * NKT + kt0 + j) * DH:
                                            (gi * NKT + kt0 + j + 1) * DH],
                                    start=(j == 0), stop=(j == nkt - 1))
                    VE.wait(s_pe, pev)
                    VE.call("tensor_scalar", y_grp[:, hg * 128:(hg + 1) * 128],
                            buf[:, 0:128], scalar1=ilv[pp][:, 0:1], scalar2=None,
                            op0=ALU.mult)
                    VE.op("tensor_copy", tiny[:1, 0:2], y_grp[:1, 0:2], inc=s_prS[pp])
                    airot += 1
                vev = VE.op("tensor_mul", y2_t[:], y_grp[:],
                            sgb[:, tt * D + gi * 512: tt * D + (gi + 1) * 512],
                            inc=s_ve)
                SC.wait(s_ve, vev)
                sev = SC.op("activation", out=sqt[:], in_=y2_t[:], func=AF.Square,
                            inc=s_se)
                VE.wait(s_se, sev)
                VE.call("tensor_reduce",
                        red16[:, 8:12].rearrange("p (g o) -> p g o", o=1),
                        sqt[:].rearrange("p (g e) -> p g e", e=DH), axis=AX.X,
                        op=ALU.add)
                VE.call("drain")
                vev = VE.op("tensor_copy", tiny[:1, 0:2], red16[:1, 8:10], inc=s_ve)
                sqt_free_v = vev
                SC.wait(s_ve, vev)
                SC.call("activation", out=lnt16[:, 8:12], in_=red16[:, 8:12],
                        func=AF.Ln, scale=1.0 / DH, bias=eps6[:, 0:1])
                sev = SC.op("activation", out=rstd16[:, 8:12], in_=lnt16[:, 8:12],
                            func=AF.Exp, scale=-0.5, inc=s_se)
                VE.wait(s_se, sev)
                VE.call("tensor_mul",
                        y2_t[:].rearrange("p (g e) -> p g e", e=DH),
                        y2_t[:].rearrange("p (g e) -> p g e", e=DH),
                        rstd16[:, 8:12].rearrange("p (g o) -> p g o", o=1)
                        .broadcast_to([128, 4, DH]))
                VE.call("tensor_mul",
                        sgb[:].rearrange("p (t g e) -> p t g e", t=TT, e=DH)[
                            :, tt, gi * GRP:(gi + 1) * GRP, :],
                        y2_t[:].rearrange("p (g e) -> p g e", e=DH),
                        gnb[:].rearrange("p (o e) -> p o e", o=1).broadcast_to([128, 4, DH]))

        VE.wait(s_se, sev_gg)
        for tt in range(TT):
            sl = tt % 2
            GP.wait(s_xf[sl])
            xv = GP.dma(xo[sl][:],
                        x_in[seq, tok0 + tt * 128: tok0 + (tt + 1) * 128, :], s_x[sl])
            VE.wait(s_x[sl], xv)
            VE.call("tensor_mul", prodt[:], sgb[:, tt * D:(tt + 1) * D],
                    ggf[:, tt * D:(tt + 1) * D])
            VE.call("drain")
            vev = VE.op("tensor_add", xo[sl][:], xo[sl][:], prodt[:], inc=s_ve)
            comb_done_v = vev
            GP.wait(s_ve, vev)
            ov = GP.dma(out_d[seq, tok0 + tt * 128: tok0 + (tt + 1) * 128, :],
                        xo[sl][:], s_out)
            GP.wait(s_out, ov)
            GP.call("sem_inc", s_xf[sl].h, 1)
            s_xf[sl].inc()

    if dbg:
        GP.dma(dbg_sg, sgb[:], s_out)
        GP.dma(dbg_gg, ggf[:], s_out)
        GP.dma(dbg_xnT, xnT[:], s_out)
        GP.dma(dbg_kT, kTc[:], s_out)
        GP.dma(dbg_v, vcc[:], s_out)
        GP.wait(s_out)

    GP.wait(s_out)

    with nc.Block() as block:
        block.sync(lambda e: SY.replay(e))
        block.scalar(lambda e: SC.replay(e))
        block.vector(lambda e: VE.replay(e))
        block.tensor(lambda e: TE.replay(e))
        block.gpsimd(lambda e: GP.replay(e))
    ctx.close()
    psp.close()
    return nc


_NC_CACHE = {}


def kernel(L0_attn_hidden_states, L0_attn_Wq, L0_attn_Wk, L0_attn_Wv, L0_attn_Wg,
           L0_attn_softmax_scaler, L0_attn_gn_weight, L0_ffn_w_gate_up,
           L0_ffn_w_proj, n_chunks=2 * (T // CHUNK)):
    x = np.ascontiguousarray(np.asarray(L0_attn_hidden_states, dtype=np.float32))
    wq = np.ascontiguousarray(np.asarray(L0_attn_Wq, dtype=np.float32))
    wk = np.ascontiguousarray(np.asarray(L0_attn_Wk, dtype=np.float32))
    wv = np.ascontiguousarray(np.asarray(L0_attn_Wv, dtype=np.float32))
    wg = np.ascontiguousarray(np.asarray(L0_attn_Wg, dtype=np.float32))
    scl = np.ascontiguousarray(np.asarray(L0_attn_softmax_scaler,
                                          dtype=np.float32)).reshape(1, H)
    gnw = np.ascontiguousarray(np.asarray(L0_attn_gn_weight,
                                          dtype=np.float32)).reshape(1, DH)
    wgu = np.ascontiguousarray(np.asarray(L0_ffn_w_gate_up, dtype=np.float32))
    wpr = np.ascontiguousarray(np.asarray(L0_ffn_w_proj, dtype=np.float32))

    key = (n_chunks, False)
    if key not in _NC_CACHE:
        _NC_CACHE[key] = build_program(n_chunks)
    nc = _NC_CACHE[key]

    in_maps = []
    for core in range(NCORES):
        xs = np.ascontiguousarray(x[core * SEQ_PER_CORE:(core + 1) * SEQ_PER_CORE])
        in_maps.append({"x": xs, "wq": wq, "wk": wk, "wv": wv, "wg": wg,
                        "scaler": scl, "gnw": gnw, "wgu": wgu, "wpr": wpr})
    res = run_bass_kernel_spmd(nc, in_maps, core_ids=list(range(NCORES)))
    out = np.concatenate([r["out"] for r in res.results], axis=0)
    return out.reshape(B, T, D)


# version 03 (final)
# speedup vs baseline: 1.0103x; 1.0051x over previous
import math
import contextlib
import numpy as np
import ml_dtypes
import concourse.bass as bass
import concourse.mybir as mybir
from concourse.bass_utils import run_bass_kernel_spmd

F32 = mybir.dt.float32
BF16 = mybir.dt.bfloat16
AF = mybir.ActivationFunctionType
ALU = mybir.AluOpType
AX = mybir.AxisListType

B, T, D = 16, 2048, 2048
H, KVH, DH = 16, 4, 128
GRP = H // KVH
HID = 8192
NCORES = 8
SEQ_PER_CORE = B // NCORES
CHUNK = 512
TT = CHUNK // 128
DK = D // 128
NKT = T // 128
MASK_NEG = -30000.0
FFN_EPS = float(np.finfo(np.float32).eps)


class Sem:
    def __init__(self, handle):
        self.h = handle
        self.v = 0

    def inc(self, n=1):
        self.v += n
        return self.v


class Rec:
    def __init__(self, name):
        self.name = name
        self.ops = []

    def wait(self, sem, val=None):
        self.ops.append(("w", sem.h, sem.v if val is None else val))

    def call(self, meth, *args, **kwargs):
        self.ops.append(("c", meth, args, kwargs))

    def op(self, meth, *args, inc=None, incn=1, **kwargs):
        v = inc.inc(incn) if inc is not None else None
        self.ops.append(("o", meth, args, kwargs,
                         None if inc is None else (inc.h, incn)))
        return v

    def dma(self, out, in_, inc, incn=16):
        v = inc.inc(incn)
        self.ops.append(("o", "dma_start", (), dict(out=out, in_=in_),
                         (inc.h, incn)))
        return v

    def replay(self, eng):
        for it in self.ops:
            k = it[0]
            if k == "w":
                if it[2] > 0:
                    eng.wait_ge(it[1], it[2])
            elif k == "c":
                getattr(eng, it[1])(*it[2], **it[3])
            else:
                r = getattr(eng, it[1])(*it[2], **it[3])
                if it[4] is not None:
                    r.then_inc(it[4][0], it[4][1])


def build_program(n_chunks=2 * (T // CHUNK), dbg=False):
    nc = bass.Bass("TRN2", target_bir_lowering=False,
                   detect_race_conditions=False)

    x_in = nc.dram_tensor("x", [SEQ_PER_CORE, T, D], F32, kind="ExternalInput").ap()
    wq_in = nc.dram_tensor("wq", [D, H * DH], F32, kind="ExternalInput").ap()
    wk_in = nc.dram_tensor("wk", [D, KVH * DH], F32, kind="ExternalInput").ap()
    wv_in = nc.dram_tensor("wv", [D, KVH * DH], F32, kind="ExternalInput").ap()
    wg_in = nc.dram_tensor("wg", [D, D], F32, kind="ExternalInput").ap()
    scaler_in = nc.dram_tensor("scaler", [1, H], F32, kind="ExternalInput").ap()
    gnw_in = nc.dram_tensor("gnw", [1, DH], F32, kind="ExternalInput").ap()
    wgu_in = nc.dram_tensor("wgu", [2 * HID, D], F32, kind="ExternalInput").ap()
    wpr_in = nc.dram_tensor("wpr", [D, HID], F32, kind="ExternalInput").ap()
    out_d = nc.dram_tensor("out", [SEQ_PER_CORE, T, D], F32, kind="ExternalOutput").ap()
    if dbg:
        dbg_sg = nc.dram_tensor("dbg_sg", [128, TT * D], BF16, kind="ExternalOutput").ap()
        dbg_gg = nc.dram_tensor("dbg_gg", [128, TT * D], BF16, kind="ExternalOutput").ap()
        dbg_xnT = nc.dram_tensor("dbg_xnT", [128, DK * CHUNK], BF16, kind="ExternalOutput").ap()
        dbg_kT = nc.dram_tensor("dbg_kT", [128, KVH * T], BF16, kind="ExternalOutput").ap()
        dbg_v = nc.dram_tensor("dbg_v", [128, KVH * NKT * DH], BF16, kind="ExternalOutput").ap()

    theta = 10000.0
    inv_freq = 1.0 / theta ** (np.arange(0, DH, 2, dtype=np.float64) / DH)
    freqs = np.outer(np.arange(T, dtype=np.float64), inv_freq)
    cos_tiled = np.cos(freqs).astype(np.float32).reshape(NKT, 128, 64) \
        .transpose(1, 0, 2).reshape(128, -1)
    sin_tiled = np.sin(freqs).astype(np.float32).reshape(NKT, 128, 64) \
        .transpose(1, 0, 2).reshape(128, -1)
    lp_np = (np.log(np.arange(1, T + 1, dtype=np.float64)) / math.sqrt(DH)).astype(np.float32)
    lp_tiled = np.ascontiguousarray(lp_np.reshape(NKT, 128).T)
    qi = np.arange(128)[:, None]
    ki = np.arange(128)[None, :]
    m_causal = np.where(ki <= qi, 0.0, MASK_NEG).astype(np.float32)
    m_window = np.where(ki >= qi, 0.0, MASK_NEG).astype(np.float32)

    c_cos = nc.inline_tensor(cos_tiled, name="c_cos")
    c_sin = nc.inline_tensor(sin_tiled, name="c_sin")
    c_lp = nc.inline_tensor(lp_tiled, name="c_lp")
    c_mc = nc.inline_tensor(m_causal, name="c_mc")
    c_mw = nc.inline_tensor(m_window, name="c_mw")
    c_iden = nc.inline_tensor(np.eye(128, dtype=ml_dtypes.bfloat16), name="c_iden")
    c_idenf = nc.inline_tensor(np.eye(128, dtype=np.float32), name="c_idenf")
    c_onec = nc.inline_tensor(np.ones((128, 1), dtype=ml_dtypes.bfloat16), name="c_onec")
    c_oner = nc.inline_tensor(np.ones((1, 128), dtype=np.float32), name="c_oner")

    wq_s = nc.dram_tensor("wq_s", [D, H * DH], BF16, kind="Internal")
    wk_s = nc.dram_tensor("wk_s", [D, KVH * DH], BF16, kind="Internal")
    wv_s = nc.dram_tensor("wv_s", [D, KVH * DH], BF16, kind="Internal")
    wg_s = nc.dram_tensor("wg_s", [D, D], BF16, kind="Internal")
    gut_s = nc.dram_tensor("gut_s", [2 * HID, D], BF16, kind="Internal")
    prt_s = nc.dram_tensor("prt_s", [HID, D], BF16, kind="Internal")

    ctx = contextlib.ExitStack()
    sb = lambda n, s, d=F32: ctx.enter_context(nc.sbuf_tensor(n, s, d))
    mksem = lambda n: Sem(ctx.enter_context(nc.semaphore(n)))

    xnT = sb("xnT", [128, DK * CHUNK], BF16)
    ggf = sb("ggf", [128, TT * D], BF16)
    sgb = sb("sgb", [128, TT * D], BF16)
    kTc = sb("kTc", [128, KVH * T], BF16)
    vcc = sb("vcc", [128, KVH * NKT * DH], BF16)
    hpc = sb("hpc", [128, 16 * CHUNK], BF16)
    xo = [sb(f"xo{i}", [128, D]) for i in range(2)]
    xn_f = sb("xn_f", [128, D])
    fx = sb("fx", [128, D])
    xn_b = sb("xn_b", [128, D], BF16)
    qf = sb("qf", [128, 512])
    rt1 = sb("rt1", [128, 64]); rt2 = sb("rt2", [128, 64])
    qgb = sb("qgb", [128, GRP * 512], BF16)
    ktm = sb("ktm", [128, KVH * DH])
    sqt = sb("sqt", [128, GRP * DH])
    red16 = sb("red16", [128, 16])
    rstd16 = sb("rstd16", [128, 16])
    lnt16 = sb("lnt16", [128, 16])
    p_bf = [sb(f"p_bf{i}", [128, 1152], BF16) for i in range(2)]
    p_dg = sb("p_dg", [128, 128])
    pT_sb = [sb(f"pT_sb{i}", [128, 1152], BF16) for i in range(2)]
    y_grp = sb("y_grp", [128, GRP * DH])
    y2_t = sb("y2_t", [128, GRP * DH], BF16)
    nmx = [sb(f"nmx{i}", [128, 1]) for i in range(2)]
    lsum = [sb(f"lsum{i}", [128, 1]) for i in range(2)]
    ilv = [sb(f"ilv{i}", [128, 1]) for i in range(2)]
    lntA = [sb(f"lntA{i}", [128, 1]) for i in range(2)]
    sums2 = sb("sums2", [128, 2])
    meanv = sb("meanv", [128, 1]); negmean = sb("negmean", [128, 1])
    m2v = sb("m2v", [128, 1]); negm2 = sb("negm2", [128, 1]); varv = sb("varv", [128, 1])
    lnt1 = sb("lnt1", [128, 1]); rstd1 = sb("rstd1", [128, 1])
    rstd_t = sb("rstd_t", [128, TT]); lnt_t = sb("lnt_t", [128, TT])
    h2t = sb("h2t", [128, CHUNK], BF16)
    silut = sb("silut", [128, CHUNK])
    prodt = sb("prodt", [128, D], BF16)
    cosb = sb("cosb", [128, NKT * 64])
    sinb = sb("sinb", [128, NKT * 64])
    lpb = sb("lpb", [128, NKT])
    mcb = sb("mcb", [128, 128]); mwb = sb("mwb", [128, 128])
    idnb = sb("idnb", [128, 128], BF16)
    idnf = sb("idnf", [128, 128])
    onec = sb("onec", [128, 1], BF16)
    oner = sb("oner", [1, 128])
    gnb = sb("gnb", [128, DH], BF16)
    sclb = sb("sclb", [128, H])
    gn_row = sb("gn_row", [1, DH]); scl_row = sb("scl_row", [1, H])
    eps6 = sb("eps6", [128, 1]); eps5 = sb("eps5", [128, 1]); epsF = sb("epsF", [128, 1])
    neghalf = sb("neghalf", [128, 1])
    tiny = sb("tiny", [128, 8])
    wsl = [sb(f"wsl{i}", [128, DK * 256], BF16) for i in range(2)]
    gsl = [sb(f"gsl{i}", [128, D], BF16) for i in range(2)]
    psl = [sb(f"psl{i}", [128, 512], BF16) for i in range(2)]
    st_am = sb("st_am", [128, 32]); st_am3 = sb("st_am3", [128, 32])
    st_sc = sb("st_sc", [128, 32], BF16)
    st_ln = sb("st_ln", [128, 32]); st_rs = sb("st_rs", [128, 32])
    stf = xo
    st_A = xn_f
    st_B = fx
    stb = xn_b
    st_qb = prodt
    st_what = (ggf, 0)
    st_dd = (ggf, 2048)
    st_q = [(sgb, 0), (sgb, 2048)]
    st_tb = [(sgb, 4096), (sgb, 6144)]

    def carve(spec, w=D):
        t, off = spec
        return t[:, off:off + w]

    psp = contextlib.ExitStack()
    psA = psp.enter_context(nc.psum_tensor("psA", [128, 1152], F32))
    psB = psp.enter_context(nc.psum_tensor("psB", [128, 1152], F32))
    ptA = psp.enter_context(nc.psum_tensor("ptA", [128, 1024], BF16))
    aux = psp.enter_context(nc.psum_tensor("aux", [128, 512], F32))

    def pm(i):
        return (psA if i < 2 else psB)[:, (i % 2) * 512:(i % 2) * 512 + 512]

    s_ld = mksem("s_ld")
    s_x = [mksem(f"s_x{i}") for i in range(2)]
    s_xf = [mksem(f"s_xf{i}") for i in range(2)]
    s_wa = [mksem(f"s_wa{i}") for i in range(2)]
    s_waf = [mksem(f"s_waf{i}") for i in range(2)]
    s_wg = [mksem(f"s_wg{i}") for i in range(2)]
    s_wgf = [mksem(f"s_wgf{i}") for i in range(2)]
    s_wp = [mksem(f"s_wp{i}") for i in range(2)]
    s_wpf = [mksem(f"s_wpf{i}") for i in range(2)]
    s_sf = [mksem(f"s_sf{i}") for i in range(2)]
    s_sff = [mksem(f"s_sff{i}") for i in range(2)]
    s_sq = [mksem(f"s_sq{i}") for i in range(2)]
    s_sqf = [mksem(f"s_sqf{i}") for i in range(2)]
    s_stb = [mksem(f"s_stb{i}") for i in range(2)]
    s_stbf = [mksem(f"s_stbf{i}") for i in range(2)]
    s_scr = mksem("s_scr")
    s_pe = mksem("s_pe")
    s_ve = mksem("s_ve")
    s_se = mksem("s_se")
    s_out = mksem("s_out")
    s_prT = mksem("s_prT")
    s_prM = [mksem(f"s_prM{i}") for i in range(4)]
    s_prS = [mksem(f"s_prS{i}") for i in range(2)]
    s_prX = mksem("s_prX")

    SY, SC, VE, TE, GP = Rec("sync"), Rec("scalar"), Rec("vector"), Rec("tensor"), Rec("gpsimd")

    for dst, src in ((cosb, c_cos), (sinb, c_sin), (lpb, c_lp), (mcb, c_mc),
                     (mwb, c_mw), (idnb, c_iden), (idnf, c_idenf),
                     (onec, c_onec), (oner, c_oner)):
        SY.dma(dst[:], src.ap(), s_ld)
    SY.dma(gn_row[:], gnw_in, s_ld)
    SY.dma(scl_row[:], scaler_in, s_ld)

    VE.call("memset", tiny[:], 0.0)
    VE.call("memset", eps6[:], 1e-6)
    VE.call("memset", eps5[:], 1e-5)
    VE.call("memset", epsF[:], FFN_EPS)
    VE.call("memset", neghalf[:], -0.5)

    TE.wait(s_ld)
    TE.call("matmul", aux[:, 0:DH], lhsT=oner[:], rhs=gn_row[:], start=True, stop=True)
    pev = TE.op("matmul", aux[:, 128:128 + H], lhsT=oner[:], rhs=scl_row[:],
                start=True, stop=True, inc=s_pe)
    SC.wait(s_pe, pev)
    SC.call("copy", out=gnb[:], in_=aux[:, 0:DH])
    sev0 = SC.op("copy", out=sclb[:], in_=aux[:, 128:128 + H], inc=s_se)
    SC.op("copy", out=tiny[:1, 0:2], in_=tiny[:1, 4:6], inc=s_prX)

    jobs = []
    for (src, dst, dout) in ((wq_in, wq_s, H * DH), (wk_in, wk_s, KVH * DH),
                             (wv_in, wv_s, KVH * DH), (wg_in, wg_s, D)):
        for rb in range(D // 128):
            jobs.append(("cast", src, dst, rb, dout))
    for rb in range(2 * HID // 128):
        jobs.append(("qgu", wgu_in, gut_s, rb, D))
    for rb in range(D // 128):
        for cb in range(HID // D):
            jobs.append(("qpr", wpr_in, prt_s, (rb, cb), D))

    g64 = lambda ap, w: ap[:, 0:w].rearrange("p (g e) -> p g e", e=64)
    g1 = lambda ap, w: ap[:, 0:w // 64].rearrange("p (g o) -> p g o", o=1)

    for ji, (mode, src, dst, rb, w) in enumerate(jobs):
        sl = ji % 2
        if ji >= 2:
            SY.wait(s_sff[sl])
        if mode == "qpr":
            r, cbb = rb
            src_ap = src[r * 128:(r + 1) * 128, cbb * D:(cbb + 1) * D]
        else:
            src_ap = src[rb * 128:(rb + 1) * 128, 0:w]
        ldv = SY.dma(stf[sl][:, 0:w], src_ap, s_sf[sl])

        if mode == "cast":
            SC.wait(s_sf[sl], ldv)
            if ji >= 2:
                SC.wait(s_sqf[sl])
            qv = SC.op("copy", out=carve(st_q[sl], w), in_=stf[sl][:, 0:w],
                       inc=s_sq[sl])
            SC.op("copy", out=tiny[:1, 0:2], in_=tiny[:1, 4:6], inc=s_sff[sl])
            SY.wait(s_sq[sl], qv)
            scv = SY.dma(dst.ap()[rb * 128:(rb + 1) * 128, 0:w],
                         carve(st_q[sl], w), s_scr)
            SY.wait(s_scr, scv)
            SY.call("sem_inc", s_sqf[sl].h, 1)
            s_sqf[sl].inc()
        else:
            SC.wait(s_sf[sl], ldv)
            sev = SC.op("copy", out=stb[:, 0:w], in_=stf[sl][:, 0:w], inc=s_se)
            SC.op("copy", out=tiny[:1, 0:2], in_=tiny[:1, 4:6], inc=s_sff[sl])
            VE.wait(s_se, sev)
            VE.call("tensor_reduce", g1(st_am[:], w), g64(stb[:], w), axis=AX.X,
                    op=ALU.add, apply_absolute_value=True)
            VE.call("drain")
            VE.call("tensor_scalar", st_am3[:, 0:w // 64], st_am[:, 0:w // 64],
                    scalar1=1.0 / 64.0, scalar2=1e-8, op0=ALU.mult, op1=ALU.max)
            VE.call("drain")
            vev = VE.op("tensor_copy", st_sc[:, 0:w // 64], st_am3[:, 0:w // 64],
                        inc=s_ve)
            SC.wait(s_ve, vev)
            SC.call("activation", out=st_ln[:, 0:w // 64], in_=st_sc[:, 0:w // 64],
                    func=AF.Ln)
            sev = SC.op("activation", out=st_rs[:, 0:w // 64], in_=st_ln[:, 0:w // 64],
                        func=AF.Exp, scale=-1.0, inc=s_se)
            VE.wait(s_se, sev)
            vev = VE.op("tensor_mul", g64(st_qb[:], w), g64(stb[:], w),
                        g1(st_rs[:], w).broadcast_to([128, w // 64, 64]), inc=s_ve)
            SC.wait(s_ve, vev)
            SC.call("activation", out=st_A[:, 0:w], in_=st_qb[:, 0:w], func=AF.Abs)
            SC.call("activation", out=st_A[:, 0:w], in_=st_A[:, 0:w], func=AF.Relu,
                    bias=neghalf[:, 0:1])
            SC.call("activation", out=st_A[:, 0:w], in_=st_A[:, 0:w], func=AF.Sign)
            sev = SC.op("activation", out=st_B[:, 0:w], in_=st_qb[:, 0:w],
                        func=AF.Sign, inc=s_se)
            VE.wait(s_se, sev)
            if ji >= 2:
                VE.wait(s_sqf[sl])
            VE.call("tensor_mul", st_B[:, 0:w], st_B[:, 0:w], st_A[:, 0:w])
            VE.call("tensor_mul", g64(carve(st_what), w), g64(st_B[:], w),
                    g1(st_sc[:], w).broadcast_to([128, w // 64, 64]))
            VE.call("tensor_sub", carve(st_dd, w), carve(st_what, w), stb[:, 0:w])
            qv = VE.op("tensor_add", carve(st_q[sl], w), stb[:, 0:w],
                       carve(st_dd, w), inc=s_sq[sl])
            TE.wait(s_sq[sl], qv)
            tsl = sl
            for hhf in range(2):
                TE.wait(s_prT)
                for k in range(8):
                    kk = hhf * 8 + k
                    if k == 7:
                        pev = TE.op("transpose", ptA[:, k * 128:(k + 1) * 128],
                                    in_=carve(st_q[sl], w)[:, kk * 128:(kk + 1) * 128],
                                    identity=idnb[:], inc=s_pe)
                    else:
                        TE.call("transpose", ptA[:, k * 128:(k + 1) * 128],
                                in_=carve(st_q[sl], w)[:, kk * 128:(kk + 1) * 128],
                                identity=idnb[:])
                SC.wait(s_pe, pev)
                if hhf == 0 and ji >= 2:
                    SC.wait(s_stbf[tsl])
                ev = SC.op("copy",
                           out=carve(st_tb[tsl], w)[:, hhf * 1024:(hhf + 1) * 1024],
                           in_=ptA[:], inc=s_stb[tsl])
                SC.op("copy", out=tiny[:1, 0:2], in_=tiny[:1, 4:6], inc=s_prT)
            SC.call("sem_inc", s_sqf[sl].h, 1)
            s_sqf[sl].inc()
            SY.wait(s_stb[tsl])
            if mode == "qgu":
                scv = SY.dma(dst.ap()[rb * 128:(rb + 1) * 128, 0:D],
                             carve(st_tb[tsl], D), s_scr)
            else:
                r, cbb = rb
                dv = dst.ap()[cbb * D:(cbb + 1) * D, r * 128:(r + 1) * 128] \
                    .rearrange("(k i) j -> i k j", i=128)
                sv = carve(st_tb[tsl], D).rearrange("p (k j) -> p k j", k=16)
                scv = SY.dma(dv, sv, s_scr)
            SY.wait(s_scr, scv)
            SY.call("sem_inc", s_stbf[tsl].h, 1)
            s_stbf[tsl].inc()

    SY.wait(s_scr)
    SY.call("nop")


    airot = 0
    sqt_free_v = 0
    comb_done_v = 0
    ln_apply_v = 0

    def pe_wait_releases():
        for s in (s_prT, s_prS[0], s_prS[1], s_prX, *s_prM):
            TE.wait(s)

    GP.wait(s_sff[0])
    GP.wait(s_sff[1])

    for c in range(n_chunks):
        seq = c // (T // CHUNK)
        half = c % (T // CHUNK)
        tok0 = half * CHUNK
        gt0 = tok0 // 128

        for tt in range(TT):
            sl = tt % 2
            GP.wait(s_xf[sl])
            xv = GP.dma(xo[sl][:],
                        x_in[seq, tok0 + tt * 128: tok0 + (tt + 1) * 128, :], s_x[sl])
            SC.wait(s_x[sl], xv)
            if ln_apply_v:
                SC.wait(s_ve, ln_apply_v)
            SC.call("activation", out=xn_f[:], in_=xo[sl][:], func=AF.Identity,
                    accum_out=sums2[:, 0:1])
            SC.call("activation", out=xn_f[:], in_=xo[sl][:], func=AF.Square,
                    accum_out=sums2[:, 1:2])
            SC.call("mul", out=meanv[:], in_=sums2[:, 0:1], mul=1.0 / D)
            SC.call("mul", out=negmean[:], in_=meanv[:], mul=-1.0)
            SC.call("activation", out=m2v[:], in_=meanv[:], func=AF.Square)
            SC.call("mul", out=negm2[:], in_=m2v[:], mul=-1.0)
            SC.call("drain")
            SC.call("activation", out=varv[:], in_=sums2[:, 1:2], func=AF.Identity,
                    scale=1.0 / D, bias=negm2[:, 0:1])
            SC.call("drain")
            SC.call("activation", out=lnt1[:], in_=varv[:], func=AF.Ln, bias=eps5[:, 0:1])
            sev = SC.op("activation", out=rstd1[:], in_=lnt1[:], func=AF.Exp,
                        scale=-0.5, inc=s_se)
            VE.wait(s_se, sev)
            vev = VE.op("tensor_scalar", xn_f[:], xo[sl][:], scalar1=negmean[:, 0:1],
                        scalar2=rstd1[:, 0:1], op0=ALU.add, op1=ALU.mult, inc=s_ve)
            ln_apply_v = vev
            VE.op("tensor_copy", tiny[:1, 6:8], xn_f[:1, 0:2], inc=s_xf[sl])
            SC.wait(s_ve, vev)
            sev = SC.op("copy", out=xn_b[:], in_=xn_f[:], inc=s_se)
            TE.wait(s_se, sev)
            for hhf in range(2):
                TE.wait(s_prT)
                for k in range(8):
                    kk = hhf * 8 + k
                    if k == 7:
                        pev = TE.op("transpose", ptA[:, k * 128:(k + 1) * 128],
                                    in_=xn_b[:, kk * 128:(kk + 1) * 128],
                                    identity=idnb[:], inc=s_pe)
                    else:
                        TE.call("transpose", ptA[:, k * 128:(k + 1) * 128],
                                in_=xn_b[:, kk * 128:(kk + 1) * 128], identity=idnb[:])
                SC.wait(s_pe, pev)
                ov = xnT[:].rearrange("p (dk t) -> p dk t", t=CHUNK)[
                    :, hhf * 8:(hhf + 1) * 8, tt * 128:(tt + 1) * 128]
                iv = ptA[:].rearrange("p (dk t) -> p dk t", t=128)
                SC.call("copy", out=ov, in_=iv)
                SC.op("copy", out=tiny[:1, 0:2], in_=tiny[:1, 4:6], inc=s_prT)
        sev_xnT = SC.op("copy", out=tiny[:1, 2:4], in_=tiny[:1, 4:6], inc=s_se)

        mirot = 0
        pev_ss_prev = 0
        silut_ev_v = 0
        for p in range(4):
            for hc in range(16):
                mg, mu = mirot % 4, (mirot + 1) % 4
                mirot += 2
                wgver = {}
                for which, mi in (("g", mg), ("u", mu)):
                    hb = (0 if which == "g" else 64) + p * 16 + hc
                    sl = (2 * (p * 16 + hc) + (0 if which == "g" else 1)) % 2
                    SY.wait(s_wgf[sl])
                    wv_ = SY.dma(gsl[sl][:], gut_s.ap()[hb * 128:(hb + 1) * 128, :],
                                 s_wg[sl])
                    TE.wait(s_wg[sl], wv_)
                    if p == 0 and hc == 0 and which == "g":
                        TE.wait(s_prX)
                        TE.wait(s_se, sev_xnT)
                        pe_wait_releases()
                    TE.wait(s_prM[mi])
                    for dk in range(DK):
                        if dk == DK - 1:
                            wgver[which] = TE.op(
                                "matmul", pm(mi),
                                lhsT=gsl[sl][:, dk * 128:(dk + 1) * 128],
                                rhs=xnT[:, dk * CHUNK:(dk + 1) * CHUNK],
                                start=False, stop=True, inc=s_wgf[sl])
                        else:
                            TE.call("matmul", pm(mi),
                                    lhsT=gsl[sl][:, dk * 128:(dk + 1) * 128],
                                    rhs=xnT[:, dk * CHUNK:(dk + 1) * CHUNK],
                                    start=(dk == 0), stop=False)
                    wgver[which + "sl"] = sl
                SC.wait(s_wgf[wgver["gsl"]], wgver["g"])
                sev = SC.op("activation", out=silut[:], in_=pm(mg), func=AF.Silu,
                            inc=s_se)
                SC.op("copy", out=tiny[:1, 0:2], in_=tiny[:1, 4:6], inc=s_prM[mg])
                VE.wait(s_se, sev)
                VE.wait(s_wgf[wgver["usl"]], wgver["u"])
                hv = VE.op("tensor_mul", hpc[:, hc * CHUNK:(hc + 1) * CHUNK],
                           silut[:], pm(mu), inc=s_ve)
                VE.op("tensor_copy", tiny[:1, 2:4], silut[:1, 0:2], inc=s_prM[mu])
                SC.wait(s_ve, hv)
                if pev_ss_prev:
                    SC.wait(s_pe, pev_ss_prev)
                sev = SC.op("activation", out=h2t[:],
                            in_=hpc[:, hc * CHUNK:(hc + 1) * CHUNK],
                            func=AF.Square, inc=s_se)
                TE.wait(s_se, sev)
                for tt in range(TT):
                    first = (p == 0 and hc == 0)
                    last = (p == 3 and hc == 15)
                    if tt == TT - 1:
                        pev_ss = TE.op("matmul", aux[:, tt:tt + 1],
                                       lhsT=h2t[:, tt * 128:(tt + 1) * 128],
                                       rhs=onec[:], start=first, stop=last, inc=s_pe)
                    else:
                        TE.call("matmul", aux[:, tt:tt + 1],
                                lhsT=h2t[:, tt * 128:(tt + 1) * 128], rhs=onec[:],
                                start=first, stop=last)
                pev_ss_prev = pev_ss
            if p == 3:
                SC.wait(s_pe, pev_ss)
                SC.call("activation", out=lnt_t[:], in_=aux[:, 0:TT], func=AF.Ln,
                        scale=1.0 / HID, bias=epsF[:, 0:1])
                sev_rstd = SC.op("activation", out=rstd_t[:], in_=lnt_t[:],
                                 func=AF.Exp, scale=-0.5, inc=s_se)
                SC.op("copy", out=tiny[:1, 0:2], in_=tiny[:1, 4:6], inc=s_prX)
            for dc in range(4):
                for tt in range(TT):
                    TE.wait(s_prM[tt])
                wpver = [None, None]
                for hc in range(16):
                    sl = hc % 2
                    SY.wait(s_wpf[sl])
                    wv_ = SY.dma(psl[sl][:],
                                 prt_s.ap()[p * 2048 + hc * 128:
                                            p * 2048 + (hc + 1) * 128,
                                            dc * 512:(dc + 1) * 512], s_wp[sl])
                    TE.wait(s_wp[sl], wv_)
                    for tt in range(TT):
                        if tt == TT - 1:
                            wpver[sl] = TE.op(
                                "matmul", pm(tt),
                                lhsT=hpc[:, hc * CHUNK + tt * 128:
                                         hc * CHUNK + (tt + 1) * 128],
                                rhs=psl[sl][:], start=(hc == 0), stop=(hc == 15),
                                inc=s_wpf[sl])
                        else:
                            TE.call("matmul", pm(tt),
                                    lhsT=hpc[:, hc * CHUNK + tt * 128:
                                             hc * CHUNK + (tt + 1) * 128],
                                    rhs=psl[sl][:], start=(hc == 0), stop=(hc == 15))
                for sl in range(2):
                    if wpver[sl] is not None:
                        SC.wait(s_wpf[sl], wpver[sl])
                        VE.wait(s_wpf[sl], wpver[sl])
                for tt in range(TT):
                    dst = ggf[:, tt * D + dc * 512: tt * D + (dc + 1) * 512]
                    if p == 0:
                        if dc == 0 and tt == 0 and comb_done_v:
                            SC.wait(s_ve, comb_done_v)
                        SC.call("copy", out=dst, in_=pm(tt))
                        SC.op("copy", out=tiny[:1, 0:2], in_=tiny[:1, 4:6],
                              inc=s_prM[tt])
                    elif p < 3:
                        VE.call("tensor_add", dst, dst, pm(tt))
                        VE.op("tensor_copy", tiny[:1, 0:2], silut[:1, 0:2],
                              inc=s_prM[tt])
                    else:
                        VE.wait(s_se, sev_rstd)
                        if silut_ev_v:
                            VE.wait(s_se, silut_ev_v)
                        VE.call("tensor_add", silut[:], dst, pm(tt))
                        VE.op("tensor_copy", tiny[:1, 0:2], silut[:1, 0:2],
                              inc=s_prM[tt])
                        VE.call("drain")
                        VE.call("tensor_scalar", silut[:], silut[:],
                                scalar1=rstd_t[:, tt:tt + 1], scalar2=None,
                                op0=ALU.mult)
                        vev = VE.op("tensor_copy", tiny[:1, 4:6], silut[:1, 0:2],
                                    inc=s_ve)
                        SC.wait(s_ve, vev)
                        silut_ev_v = SC.op("activation", out=dst, in_=silut[:],
                                           func=AF.Silu, inc=s_se)
        sev_gg = SC.op("copy", out=tiny[:1, 2:4], in_=tiny[:1, 4:6], inc=s_se)

        awrot = 0
        mirot = 0
        sqt_free_v = sqt_free_v if c > 0 else 0

        def attn_piece(src_s, colblk, width=256):
            nonlocal awrot
            sl = awrot % 2
            awrot += 1
            SY.wait(s_waf[sl])
            v = SY.dma(wsl[sl][:, 0:DK * width].rearrange("p (k n) -> p k n", n=width),
                       src_s.ap().rearrange("(k p) n -> p k n", p=128)[
                           :, :, colblk * width:(colblk + 1) * width], s_wa[sl])
            return sl, v

        def proj_tm(src_s, colblk, tgt, tt, width=256, last_inc=None):
            sl, wv_ = attn_piece(src_s, colblk, width)
            TE.wait(s_wa[sl], wv_)
            ver = None
            for dk in range(DK):
                if dk == DK - 1:
                    ver = TE.op("matmul", tgt,
                                lhsT=xnT[:, dk * CHUNK + tt * 128:
                                         dk * CHUNK + (tt + 1) * 128],
                                rhs=wsl[sl][:, dk * width:(dk + 1) * width],
                                start=False, stop=True, inc=s_waf[sl])
                else:
                    TE.call("matmul", tgt,
                            lhsT=xnT[:, dk * CHUNK + tt * 128:
                                     dk * CHUNK + (tt + 1) * 128],
                            rhs=wsl[sl][:, dk * width:(dk + 1) * width],
                            start=(dk == 0), stop=False)
            return sl, ver

        for tt in range(TT):
            gq = gt0 + tt
            mik = mirot % 4
            mirot += 1
            TE.wait(s_prM[mik])
            vers = []
            for cb2 in range(2):
                sl, ver = proj_tm(wk_s, cb2, pm(mik)[:, cb2 * 256:(cb2 + 1) * 256], tt)
                vers.append((sl, ver))
            for sl, ver in vers:
                SC.wait(s_waf[sl], ver)
            SC.call("copy", out=ktm[:], in_=pm(mik))
            if sqt_free_v:
                SC.wait(s_ve, sqt_free_v)
            sev = SC.op("activation", out=sqt[:], in_=ktm[:], func=AF.Square, inc=s_se)
            VE.wait(s_se, sev)
            VE.call("tensor_reduce",
                    red16[:, 0:4].rearrange("p (g o) -> p g o", o=1),
                    sqt[:].rearrange("p (g e) -> p g e", e=DH), axis=AX.X, op=ALU.add)
            VE.call("drain")
            vev = VE.op("tensor_copy", tiny[:1, 0:2], red16[:1, 0:2], inc=s_ve)
            SC.wait(s_ve, vev)
            SC.call("activation", out=lnt16[:, 0:4], in_=red16[:, 0:4], func=AF.Ln,
                    scale=1.0 / DH, bias=eps6[:, 0:1])
            SC.op("copy", out=tiny[:1, 0:2], in_=tiny[:1, 4:6], inc=s_prM[mik])
            sev = SC.op("activation", out=rstd16[:, 0:4], in_=lnt16[:, 0:4],
                        func=AF.Exp, scale=-0.5, inc=s_se)
            VE.wait(s_se, sev)
            VE.call("tensor_mul",
                    ktm[:].rearrange("p (g e) -> p g e", e=DH),
                    ktm[:].rearrange("p (g e) -> p g e", e=DH),
                    rstd16[:, 0:4].rearrange("p (g o) -> p g o", o=1)
                    .broadcast_to([128, 4, DH]))
            for kv in range(KVH):
                x1 = ktm[:, kv * DH: kv * DH + 64]
                x2 = ktm[:, kv * DH + 64: kv * DH + DH]
                ct = cosb[:, gq * 64:(gq + 1) * 64]
                st = sinb[:, gq * 64:(gq + 1) * 64]
                VE.call("tensor_mul", rt1[:, 0:64], x1, st)
                VE.call("tensor_mul", rt2[:, 0:64], x2, st)
                VE.call("tensor_mul", x1, x1, ct)
                VE.call("tensor_mul", x2, x2, ct)
                VE.call("tensor_add", x1, x1, rt2[:, 0:64])
                VE.call("tensor_sub", x2, x2, rt1[:, 0:64])
            vev = VE.op("tensor_copy", tiny[:1, 0:2], ktm[:1, 0:2], inc=s_ve)
            SC.wait(s_ve, vev)
            kb = prodt[:, 0:512]
            sev = SC.op("copy", out=kb, in_=ktm[:], inc=s_se)
            TE.wait(s_se, sev)
            TE.wait(s_prT)
            for kv in range(KVH):
                if kv == KVH - 1:
                    pev = TE.op("transpose", ptA[:, kv * 128:(kv + 1) * 128],
                                in_=kb[:, kv * 128:(kv + 1) * 128],
                                identity=idnb[:], inc=s_pe)
                else:
                    TE.call("transpose", ptA[:, kv * 128:(kv + 1) * 128],
                            in_=kb[:, kv * 128:(kv + 1) * 128], identity=idnb[:])
            SC.wait(s_pe, pev)
            ovk = kTc[:].rearrange("p (kv t) -> p kv t", t=T)[
                :, :, tok0 + tt * 128: tok0 + (tt + 1) * 128]
            SC.call("copy", out=ovk,
                    in_=ptA[:, 0:512].rearrange("p (kv t) -> p kv t", t=128))
            SC.op("copy", out=tiny[:1, 0:2], in_=tiny[:1, 4:6], inc=s_prT)
            miv = mirot % 4
            mirot += 1
            TE.wait(s_prM[miv])
            vers = []
            for cb2 in range(2):
                sl, ver = proj_tm(wv_s, cb2, pm(miv)[:, cb2 * 256:(cb2 + 1) * 256], tt)
                vers.append((sl, ver))
            for sl, ver in vers:
                SC.wait(s_waf[sl], ver)
            ovv = vcc[:].rearrange("p (kv kt e) -> p kv kt e", kv=KVH, e=DH)[
                :, :, gq, :]
            SC.call("copy", out=ovv,
                    in_=pm(miv).rearrange("p (kv e) -> p kv e", e=DH))
            SC.op("copy", out=tiny[:1, 0:2], in_=tiny[:1, 4:6], inc=s_prM[miv])
            for cbp in range(4):
                mig = mirot % 4
                mirot += 1
                TE.wait(s_prM[mig])
                vers = []
                for cb2 in range(2):
                    sl, ver = proj_tm(wg_s, cbp * 2 + cb2,
                                      pm(mig)[:, cb2 * 256:(cb2 + 1) * 256], tt)
                    vers.append((sl, ver))
                for sl, ver in vers:
                    SC.wait(s_waf[sl], ver)
                if tt == 0 and cbp == 0 and comb_done_v:
                    SC.wait(s_ve, comb_done_v)
                SC.call("activation",
                        out=sgb[:, tt * D + cbp * 512: tt * D + (cbp + 1) * 512],
                        in_=pm(mig), func=AF.Silu)
                SC.op("copy", out=tiny[:1, 0:2], in_=tiny[:1, 4:6], inc=s_prM[mig])

        for s_ in s_prM:
            TE.wait(s_)
        for gi in range(KVH):
            for hg in range(GRP):
                h = gi * GRP + hg
                sl, wv_ = attn_piece(wq_s, h, 128)
                TE.wait(s_wa[sl], wv_)
                TE.wait(s_prX)
                qver = None
                for tt in range(TT):
                    for dk in range(DK):
                        lastmm = (tt == TT - 1 and dk == DK - 1)
                        if lastmm:
                            qver = TE.op(
                                "matmul", aux[:, tt * 128:(tt + 1) * 128],
                                lhsT=xnT[:, dk * CHUNK + tt * 128:
                                         dk * CHUNK + (tt + 1) * 128],
                                rhs=wsl[sl][:, dk * 128:(dk + 1) * 128],
                                start=False, stop=True, inc=s_waf[sl])
                        else:
                            TE.call("matmul", aux[:, tt * 128:(tt + 1) * 128],
                                    lhsT=xnT[:, dk * CHUNK + tt * 128:
                                             dk * CHUNK + (tt + 1) * 128],
                                    rhs=wsl[sl][:, dk * 128:(dk + 1) * 128],
                                    start=(dk == 0), stop=(dk == DK - 1))
                SC.wait(s_waf[sl], qver)
                SC.call("copy", out=qf[:], in_=aux[:])
                if sqt_free_v:
                    SC.wait(s_ve, sqt_free_v)
                sev = SC.op("activation", out=sqt[:], in_=qf[:], func=AF.Square,
                            inc=s_se)
                VE.wait(s_se, sev)
                VE.call("tensor_reduce",
                        red16[:, 4:8].rearrange("p (g o) -> p g o", o=1),
                        sqt[:].rearrange("p (g e) -> p g e", e=DH), axis=AX.X,
                        op=ALU.add)
                VE.call("drain")
                vev = VE.op("tensor_copy", tiny[:1, 0:2], red16[:1, 4:6], inc=s_ve)
                SC.wait(s_ve, vev)
                SC.call("activation", out=lnt16[:, 4:8], in_=red16[:, 4:8],
                        func=AF.Ln, scale=1.0 / DH, bias=eps6[:, 0:1])
                SC.op("copy", out=tiny[:1, 0:2], in_=tiny[:1, 4:6], inc=s_prX)
                sev = SC.op("activation", out=rstd16[:, 4:8], in_=lnt16[:, 4:8],
                            func=AF.Exp, scale=-0.5, inc=s_se)
                VE.wait(s_se, sev)
                VE.call("tensor_mul",
                        qf[:].rearrange("p (g e) -> p g e", e=DH),
                        qf[:].rearrange("p (g e) -> p g e", e=DH),
                        rstd16[:, 4:8].rearrange("p (g o) -> p g o", o=1)
                        .broadcast_to([128, 4, DH]))
                for tt in range(TT):
                    gq = gt0 + tt
                    x1 = qf[:, tt * 128: tt * 128 + 64]
                    x2 = qf[:, tt * 128 + 64: tt * 128 + 128]
                    ct = cosb[:, gq * 64:(gq + 1) * 64]
                    st = sinb[:, gq * 64:(gq + 1) * 64]
                    VE.call("tensor_mul", rt1[:, 0:64], x1, st)
                    VE.call("tensor_mul", rt2[:, 0:64], x2, st)
                    VE.call("tensor_mul", x1, x1, ct)
                    VE.call("tensor_mul", x2, x2, ct)
                    VE.call("tensor_add", x1, x1, rt2[:, 0:64])
                    VE.call("tensor_sub", x2, x2, rt1[:, 0:64])
                    VE.call("tensor_scalar", qf[:, tt * 128:(tt + 1) * 128],
                            qf[:, tt * 128:(tt + 1) * 128],
                            scalar1=sclb[:, h:h + 1],
                            scalar2=lpb[:, gq:gq + 1], op0=ALU.mult, op1=ALU.mult)
                vev = VE.op("tensor_copy", tiny[:1, 0:2], qf[:1, 0:2], inc=s_ve)
                SC.wait(s_ve, vev)
                qcb = prodt[:, 512:1024]
                sev = SC.op("copy", out=qcb, in_=qf[:], inc=s_se)
                TE.wait(s_se, sev)
                TE.wait(s_prT)
                for tt in range(TT):
                    if tt == TT - 1:
                        pev = TE.op("transpose", ptA[:, tt * 128:(tt + 1) * 128],
                                    in_=qcb[:, tt * 128:(tt + 1) * 128],
                                    identity=idnb[:], inc=s_pe)
                    else:
                        TE.call("transpose", ptA[:, tt * 128:(tt + 1) * 128],
                                in_=qcb[:, tt * 128:(tt + 1) * 128], identity=idnb[:])
                SC.wait(s_pe, pev)
                SC.call("copy", out=qgb[:, hg * 512:(hg + 1) * 512], in_=ptA[:, 0:512])
                sev_q = SC.op("copy", out=tiny[:1, 0:2], in_=tiny[:1, 4:6], inc=s_prT)
            sev_qg = SC.op("copy", out=tiny[:1, 2:4], in_=tiny[:1, 4:6], inc=s_se)

            pairs = [(t_, h_) for t_ in range(TT) for h_ in range(GRP)]
            airot0 = airot
            sc_pev = {}

            def emit_scores(idx):
                tt_, hg_ = pairs[idx]
                gq_ = gt0 + tt_
                kt0_ = max(0, gq_ - 8)
                nkt_ = gq_ + 1 - kt0_
                pp_ = (airot0 + idx) % 2
                buf_ = psA if pp_ == 0 else psB
                TE.wait(s_prS[pp_])
                if idx == 0:
                    TE.wait(s_se, sev_qg)
                done_ = 0
                nspans_ = (nkt_ + 3) // 4
                for sp_ in range(nspans_):
                    ln2 = min(4, nkt_ - done_)
                    ds_ = buf_[:, done_ * 128:(done_ + ln2) * 128]
                    lh_ = qgb[:, hg_ * 512 + tt_ * 128:hg_ * 512 + (tt_ + 1) * 128]
                    rh_ = kTc[:, gi * T + (kt0_ + done_) * 128:
                              gi * T + (kt0_ + done_ + ln2) * 128]
                    if sp_ == nspans_ - 1:
                        sc_pev[idx] = TE.op("matmul", ds_, lhsT=lh_, rhs=rh_,
                                            start=True, stop=True, inc=s_pe)
                    else:
                        TE.call("matmul", ds_, lhsT=lh_, rhs=rh_,
                                start=True, stop=True)
                    done_ += ln2

            emit_scores(0)
            for idx in range(len(pairs)):
                tt, hg = pairs[idx]
                gq = gt0 + tt
                kt0 = max(0, gq - 8)
                nkt = gq + 1 - kt0
                if True:
                    pp = airot % 2
                    buf = psA if pp == 0 else psB
                    pev = sc_pev[idx]
                    VE.wait(s_pe, pev)
                    VE.call("tensor_add", buf[:, (nkt - 1) * 128: nkt * 128],
                            buf[:, (nkt - 1) * 128: nkt * 128], mcb[:])
                    if nkt == 9:
                        VE.call("tensor_add", buf[:, 0:128], buf[:, 0:128], mwb[:])
                    VE.call("reduce_max", nmx[pp][:], buf[:, 0:nkt * 128], axis=AX.X,
                            negate=True)
                    vev = VE.op("tensor_copy", tiny[:1, 0:1], nmx[pp][:1, 0:1],
                                inc=s_ve)
                    SC.wait(s_ve, vev)
                    SC.call("activation", out=p_bf[pp][:, 0:nkt * 128],
                            in_=buf[:, 0:nkt * 128], func=AF.Exp,
                            bias=nmx[pp][:, 0:1], accum_out=lsum[pp][:])
                    if nkt == 9:
                        SC.call("activation", out=p_dg[:], in_=buf[:, 1024:1152],
                                func=AF.Exp, bias=nmx[pp][:, 0:1])
                    SC.call("activation", out=lntA[pp][:], in_=lsum[pp][:], func=AF.Ln)
                    sev = SC.op("activation", out=ilv[pp][:], in_=lntA[pp][:],
                                func=AF.Exp, scale=-1.0, inc=s_se)
                    if idx + 1 < len(pairs):
                        emit_scores(idx + 1)
                    TE.wait(s_se, sev)
                    TE.wait(s_prT)
                    nbt = min(nkt, 8)
                    for j in range(nbt):
                        if j == nbt - 1:
                            pev = TE.op("transpose", ptA[:, j * 128:(j + 1) * 128],
                                        in_=p_bf[pp][:, j * 128:(j + 1) * 128],
                                        identity=idnb[:], inc=s_pe)
                        else:
                            TE.call("transpose", ptA[:, j * 128:(j + 1) * 128],
                                    in_=p_bf[pp][:, j * 128:(j + 1) * 128],
                                    identity=idnb[:])
                    if nkt == 9:
                        TE.wait(s_prX)
                        pev = TE.op("transpose", aux[:, 0:128], in_=p_dg[:],
                                    identity=idnf[:], inc=s_pe)
                    SC.wait(s_pe, pev)
                    SC.call("copy", out=pT_sb[pp][:, 0:nbt * 128], in_=ptA[:, 0:nbt * 128])
                    SC.op("copy", out=tiny[:1, 0:1], in_=tiny[:1, 4:5], inc=s_prT)
                    if nkt == 9:
                        SC.call("copy", out=pT_sb[pp][:, 1024:1152], in_=aux[:, 0:128])
                        SC.op("copy", out=tiny[:1, 0:1], in_=tiny[:1, 4:5],
                              inc=s_prX)
                    sev = SC.op("copy", out=tiny[:1, 2:3], in_=tiny[:1, 4:5],
                                inc=s_se)
                    TE.wait(s_se, sev)
                    for j in range(nkt):
                        if j == nkt - 1:
                            pev = TE.op("matmul", buf[:, 0:128],
                                        lhsT=pT_sb[pp][:, j * 128:(j + 1) * 128],
                                        rhs=vcc[:, (gi * NKT + kt0 + j) * DH:
                                                (gi * NKT + kt0 + j + 1) * DH],
                                        start=(j == 0), stop=True, inc=s_pe)
                        else:
                            TE.call("matmul", buf[:, 0:128],
                                    lhsT=pT_sb[pp][:, j * 128:(j + 1) * 128],
                                    rhs=vcc[:, (gi * NKT + kt0 + j) * DH:
                                            (gi * NKT + kt0 + j + 1) * DH],
                                    start=(j == 0), stop=(j == nkt - 1))
                    VE.wait(s_pe, pev)
                    VE.call("tensor_scalar", y_grp[:, hg * 128:(hg + 1) * 128],
                            buf[:, 0:128], scalar1=ilv[pp][:, 0:1], scalar2=None,
                            op0=ALU.mult)
                    VE.op("tensor_copy", tiny[:1, 0:2], y_grp[:1, 0:2], inc=s_prS[pp])
                    airot += 1
                if hg != GRP - 1:
                    continue
                vev = VE.op("tensor_mul", y2_t[:], y_grp[:],
                            sgb[:, tt * D + gi * 512: tt * D + (gi + 1) * 512],
                            inc=s_ve)
                SC.wait(s_ve, vev)
                sev = SC.op("activation", out=sqt[:], in_=y2_t[:], func=AF.Square,
                            inc=s_se)
                VE.wait(s_se, sev)
                VE.call("tensor_reduce",
                        red16[:, 8:12].rearrange("p (g o) -> p g o", o=1),
                        sqt[:].rearrange("p (g e) -> p g e", e=DH), axis=AX.X,
                        op=ALU.add)
                VE.call("drain")
                vev = VE.op("tensor_copy", tiny[:1, 0:2], red16[:1, 8:10], inc=s_ve)
                sqt_free_v = vev
                SC.wait(s_ve, vev)
                SC.call("activation", out=lnt16[:, 8:12], in_=red16[:, 8:12],
                        func=AF.Ln, scale=1.0 / DH, bias=eps6[:, 0:1])
                sev = SC.op("activation", out=rstd16[:, 8:12], in_=lnt16[:, 8:12],
                            func=AF.Exp, scale=-0.5, inc=s_se)
                VE.wait(s_se, sev)
                VE.call("tensor_mul",
                        y2_t[:].rearrange("p (g e) -> p g e", e=DH),
                        y2_t[:].rearrange("p (g e) -> p g e", e=DH),
                        rstd16[:, 8:12].rearrange("p (g o) -> p g o", o=1)
                        .broadcast_to([128, 4, DH]))
                VE.call("tensor_mul",
                        sgb[:].rearrange("p (t g e) -> p t g e", t=TT, e=DH)[
                            :, tt, gi * GRP:(gi + 1) * GRP, :],
                        y2_t[:].rearrange("p (g e) -> p g e", e=DH),
                        gnb[:].rearrange("p (o e) -> p o e", o=1).broadcast_to([128, 4, DH]))

        VE.wait(s_se, sev_gg)
        for tt in range(TT):
            sl = tt % 2
            GP.wait(s_xf[sl])
            xv = GP.dma(xo[sl][:],
                        x_in[seq, tok0 + tt * 128: tok0 + (tt + 1) * 128, :], s_x[sl])
            VE.wait(s_x[sl], xv)
            VE.call("tensor_mul", prodt[:], sgb[:, tt * D:(tt + 1) * D],
                    ggf[:, tt * D:(tt + 1) * D])
            VE.call("drain")
            vev = VE.op("tensor_add", xo[sl][:], xo[sl][:], prodt[:], inc=s_ve)
            comb_done_v = vev
            GP.wait(s_ve, vev)
            ov = GP.dma(out_d[seq, tok0 + tt * 128: tok0 + (tt + 1) * 128, :],
                        xo[sl][:], s_out)
            GP.wait(s_out, ov)
            GP.call("sem_inc", s_xf[sl].h, 1)
            s_xf[sl].inc()

    if dbg:
        GP.dma(dbg_sg, sgb[:], s_out)
        GP.dma(dbg_gg, ggf[:], s_out)
        GP.dma(dbg_xnT, xnT[:], s_out)
        GP.dma(dbg_kT, kTc[:], s_out)
        GP.dma(dbg_v, vcc[:], s_out)
        GP.wait(s_out)

    GP.wait(s_out)

    with nc.Block() as block:
        block.sync(lambda e: SY.replay(e))
        block.scalar(lambda e: SC.replay(e))
        block.vector(lambda e: VE.replay(e))
        block.tensor(lambda e: TE.replay(e))
        block.gpsimd(lambda e: GP.replay(e))
    ctx.close()
    psp.close()
    return nc


_NC_CACHE = {}


def kernel(L0_attn_hidden_states, L0_attn_Wq, L0_attn_Wk, L0_attn_Wv, L0_attn_Wg,
           L0_attn_softmax_scaler, L0_attn_gn_weight, L0_ffn_w_gate_up,
           L0_ffn_w_proj, n_chunks=2 * (T // CHUNK)):
    x = np.ascontiguousarray(np.asarray(L0_attn_hidden_states, dtype=np.float32))
    wq = np.ascontiguousarray(np.asarray(L0_attn_Wq, dtype=np.float32))
    wk = np.ascontiguousarray(np.asarray(L0_attn_Wk, dtype=np.float32))
    wv = np.ascontiguousarray(np.asarray(L0_attn_Wv, dtype=np.float32))
    wg = np.ascontiguousarray(np.asarray(L0_attn_Wg, dtype=np.float32))
    scl = np.ascontiguousarray(np.asarray(L0_attn_softmax_scaler,
                                          dtype=np.float32)).reshape(1, H)
    gnw = np.ascontiguousarray(np.asarray(L0_attn_gn_weight,
                                          dtype=np.float32)).reshape(1, DH)
    wgu = np.ascontiguousarray(np.asarray(L0_ffn_w_gate_up, dtype=np.float32))
    wpr = np.ascontiguousarray(np.asarray(L0_ffn_w_proj, dtype=np.float32))

    key = (n_chunks, False)
    if key not in _NC_CACHE:
        _NC_CACHE[key] = build_program(n_chunks)
    nc = _NC_CACHE[key]

    in_maps = []
    for core in range(NCORES):
        xs = np.ascontiguousarray(x[core * SEQ_PER_CORE:(core + 1) * SEQ_PER_CORE])
        in_maps.append({"x": xs, "wq": wq, "wk": wk, "wv": wv, "wg": wg,
                        "scaler": scl, "gnw": gnw, "wgu": wgu, "wpr": wpr})
    res = run_bass_kernel_spmd(nc, in_maps, core_ids=list(range(NCORES)))
    out = np.concatenate([r["out"] for r in res.results], axis=0)
    return out.reshape(B, T, D)


# version 04
# speedup vs baseline: 1.0608x; 1.0500x over previous
import math
import contextlib
import numpy as np
import ml_dtypes
import concourse.bass as bass
import concourse.mybir as mybir
from concourse.bass_utils import run_bass_kernel_spmd

F32 = mybir.dt.float32
BF16 = mybir.dt.bfloat16
AF = mybir.ActivationFunctionType
ALU = mybir.AluOpType
AX = mybir.AxisListType

B, T, D = 16, 2048, 2048
H, KVH, DH = 16, 4, 128
GRP = H // KVH
HID = 8192
NCORES = 8
SEQ_PER_CORE = B // NCORES
CHUNK = 512
TT = CHUNK // 128
DK = D // 128
NKT = T // 128
MASK_NEG = -30000.0
FFN_EPS = float(np.finfo(np.float32).eps)


class Sem:
    def __init__(self, handle):
        self.h = handle
        self.v = 0

    def inc(self, n=1):
        self.v += n
        return self.v


class Rec:
    def __init__(self, name):
        self.name = name
        self.ops = []

    def wait(self, sem, val=None):
        self.ops.append(("w", sem.h, sem.v if val is None else val))

    def call(self, meth, *args, **kwargs):
        self.ops.append(("c", meth, args, kwargs))

    def op(self, meth, *args, inc=None, incn=1, **kwargs):
        v = inc.inc(incn) if inc is not None else None
        self.ops.append(("o", meth, args, kwargs,
                         None if inc is None else (inc.h, incn)))
        return v

    def dma(self, out, in_, inc, incn=16):
        v = inc.inc(incn)
        self.ops.append(("o", "dma_start", (), dict(out=out, in_=in_),
                         (inc.h, incn)))
        return v

    def replay(self, eng):
        for it in self.ops:
            k = it[0]
            if k == "w":
                if it[2] > 0:
                    eng.wait_ge(it[1], it[2])
            elif k == "c":
                getattr(eng, it[1])(*it[2], **it[3])
            else:
                r = getattr(eng, it[1])(*it[2], **it[3])
                if it[4] is not None:
                    r.then_inc(it[4][0], it[4][1])


def build_program(n_chunks=2 * (T // CHUNK), dbg=False):
    nc = bass.Bass("TRN2", target_bir_lowering=False,
                   detect_race_conditions=False)

    x_in = nc.dram_tensor("x", [SEQ_PER_CORE, T, D], F32, kind="ExternalInput").ap()
    wq_in = nc.dram_tensor("wq", [D, H * DH], F32, kind="ExternalInput").ap()
    wk_in = nc.dram_tensor("wk", [D, KVH * DH], F32, kind="ExternalInput").ap()
    wv_in = nc.dram_tensor("wv", [D, KVH * DH], F32, kind="ExternalInput").ap()
    wg_in = nc.dram_tensor("wg", [D, D], F32, kind="ExternalInput").ap()
    scaler_in = nc.dram_tensor("scaler", [1, H], F32, kind="ExternalInput").ap()
    gnw_in = nc.dram_tensor("gnw", [1, DH], F32, kind="ExternalInput").ap()
    wgu_in = nc.dram_tensor("wgu", [2 * HID, D], F32, kind="ExternalInput").ap()
    wpr_in = nc.dram_tensor("wpr", [D, HID], F32, kind="ExternalInput").ap()
    out_d = nc.dram_tensor("out", [SEQ_PER_CORE, T, D], F32, kind="ExternalOutput").ap()
    if dbg:
        dbg_sg = nc.dram_tensor("dbg_sg", [128, TT * D], BF16, kind="ExternalOutput").ap()
        dbg_gg = nc.dram_tensor("dbg_gg", [128, TT * D], BF16, kind="ExternalOutput").ap()
        dbg_xnT = nc.dram_tensor("dbg_xnT", [128, DK * CHUNK], BF16, kind="ExternalOutput").ap()
        dbg_kT = nc.dram_tensor("dbg_kT", [128, KVH * T], BF16, kind="ExternalOutput").ap()
        dbg_v = nc.dram_tensor("dbg_v", [128, KVH * NKT * DH], BF16, kind="ExternalOutput").ap()

    theta = 10000.0
    inv_freq = 1.0 / theta ** (np.arange(0, DH, 2, dtype=np.float64) / DH)
    freqs = np.outer(np.arange(T, dtype=np.float64), inv_freq)
    cos_tiled = np.cos(freqs).astype(np.float32).reshape(NKT, 128, 64) \
        .transpose(1, 0, 2).reshape(128, -1)
    sin_tiled = np.sin(freqs).astype(np.float32).reshape(NKT, 128, 64) \
        .transpose(1, 0, 2).reshape(128, -1)
    lp_np = (np.log(np.arange(1, T + 1, dtype=np.float64)) / math.sqrt(DH)).astype(np.float32)
    lp_tiled = np.ascontiguousarray(lp_np.reshape(NKT, 128).T)
    qi = np.arange(128)[:, None]
    ki = np.arange(128)[None, :]
    m_causal = np.where(ki <= qi, 0.0, MASK_NEG).astype(np.float32)
    m_window = np.where(ki >= qi, 0.0, MASK_NEG).astype(np.float32)

    c_cos = nc.inline_tensor(cos_tiled, name="c_cos")
    c_sin = nc.inline_tensor(sin_tiled, name="c_sin")
    c_lp = nc.inline_tensor(lp_tiled, name="c_lp")
    c_mc = nc.inline_tensor(m_causal, name="c_mc")
    c_mw = nc.inline_tensor(m_window, name="c_mw")
    c_iden = nc.inline_tensor(np.eye(128, dtype=ml_dtypes.bfloat16), name="c_iden")
    c_idenf = nc.inline_tensor(np.eye(128, dtype=np.float32), name="c_idenf")
    c_onec = nc.inline_tensor(np.ones((128, 1), dtype=ml_dtypes.bfloat16), name="c_onec")
    c_oner = nc.inline_tensor(np.ones((1, 128), dtype=np.float32), name="c_oner")

    wq_s = nc.dram_tensor("wq_s", [D, H * DH], BF16, kind="Internal")
    wk_s = nc.dram_tensor("wk_s", [D, KVH * DH], BF16, kind="Internal")
    wv_s = nc.dram_tensor("wv_s", [D, KVH * DH], BF16, kind="Internal")
    wg_s = nc.dram_tensor("wg_s", [D, D], BF16, kind="Internal")
    gut_s = nc.dram_tensor("gut_s", [2 * HID, D], BF16, kind="Internal")
    prt_s = nc.dram_tensor("prt_s", [HID, D], BF16, kind="Internal")

    ctx = contextlib.ExitStack()
    sb = lambda n, s, d=F32: ctx.enter_context(nc.sbuf_tensor(n, s, d))
    mksem = lambda n: Sem(ctx.enter_context(nc.semaphore(n)))

    xnT = sb("xnT", [128, DK * CHUNK], BF16)
    ggf = sb("ggf", [128, TT * D], BF16)
    sgb = sb("sgb", [128, TT * D], BF16)
    kTc = sb("kTc", [128, KVH * T], BF16)
    vcc = sb("vcc", [128, KVH * NKT * DH], BF16)
    hpc = sb("hpc", [128, 16 * CHUNK], BF16)
    xo = [sb(f"xo{i}", [128, D]) for i in range(2)]
    xn_f = sb("xn_f", [128, D])
    fx = sb("fx", [128, D])
    xn_b = sb("xn_b", [128, D], BF16)
    qf = sb("qf", [128, 512])
    rt1 = sb("rt1", [128, 64]); rt2 = sb("rt2", [128, 64])
    qgb = sb("qgb", [128, GRP * 512], BF16)
    ktm = sb("ktm", [128, KVH * DH])
    sqt = sb("sqt", [128, GRP * DH])
    red16 = sb("red16", [128, 16])
    rstd16 = sb("rstd16", [128, 16])
    lnt16 = sb("lnt16", [128, 16])
    p_bf = [sb(f"p_bf{i}", [128, 1152], BF16) for i in range(2)]
    p_dg = sb("p_dg", [128, 128])
    pT_sb = [sb(f"pT_sb{i}", [128, 1152], BF16) for i in range(2)]
    y_grp = sb("y_grp", [128, GRP * DH])
    y2_t = sb("y2_t", [128, GRP * DH], BF16)
    nmx = [sb(f"nmx{i}", [128, 1]) for i in range(2)]
    lsum = [sb(f"lsum{i}", [128, 1]) for i in range(2)]
    ilv = [sb(f"ilv{i}", [128, 1]) for i in range(2)]
    lntA = [sb(f"lntA{i}", [128, 1]) for i in range(2)]
    sums2 = sb("sums2", [128, 2])
    meanv = sb("meanv", [128, 1]); negmean = sb("negmean", [128, 1])
    m2v = sb("m2v", [128, 1]); negm2 = sb("negm2", [128, 1]); varv = sb("varv", [128, 1])
    lnt1 = sb("lnt1", [128, 1]); rstd1 = sb("rstd1", [128, 1])
    rstd_t = sb("rstd_t", [128, TT]); lnt_t = sb("lnt_t", [128, TT])
    h2t = sb("h2t", [128, CHUNK], BF16)
    silut = sb("silut", [128, CHUNK])
    prodt = sb("prodt", [128, D], BF16)
    cosb = sb("cosb", [128, NKT * 64])
    sinb = sb("sinb", [128, NKT * 64])
    lpb = sb("lpb", [128, NKT])
    mcb = sb("mcb", [128, 128]); mwb = sb("mwb", [128, 128])
    idnb = sb("idnb", [128, 128], BF16)
    idnf = sb("idnf", [128, 128])
    onec = sb("onec", [128, 1], BF16)
    oner = sb("oner", [1, 128])
    gnb = sb("gnb", [128, DH], BF16)
    sclb = sb("sclb", [128, H])
    gn_row = sb("gn_row", [1, DH]); scl_row = sb("scl_row", [1, H])
    eps6 = sb("eps6", [128, 1]); eps5 = sb("eps5", [128, 1]); epsF = sb("epsF", [128, 1])
    neghalf = sb("neghalf", [128, 1])
    tiny = sb("tiny", [128, 8])
    wsl = [sb(f"wsl{i}", [128, DK * 256], BF16) for i in range(2)]
    gsl = [sb(f"gsl{i}", [128, D], BF16) for i in range(2)]
    psl = [sb(f"psl{i}", [128, 512], BF16) for i in range(2)]
    st_am = sb("st_am", [128, 32]); st_am3 = sb("st_am3", [128, 32])
    st_sc = sb("st_sc", [128, 32], BF16)
    st_ln = sb("st_ln", [128, 32]); st_rs = sb("st_rs", [128, 32])
    stf = xo
    st_A = xn_f
    st_B = fx
    stb = xn_b
    st_qb = prodt
    st_what = (ggf, 0)
    st_dd = (ggf, 2048)
    st_q = [(sgb, 0), (sgb, 2048)]
    st_tb = [(sgb, 4096), (sgb, 6144)]

    def carve(spec, w=D):
        t, off = spec
        return t[:, off:off + w]

    psp = contextlib.ExitStack()
    psA = psp.enter_context(nc.psum_tensor("psA", [128, 1152], F32))
    psB = psp.enter_context(nc.psum_tensor("psB", [128, 1152], F32))
    ptA = psp.enter_context(nc.psum_tensor("ptA", [128, 1024], BF16))
    aux = psp.enter_context(nc.psum_tensor("aux", [128, 512], F32))

    def pm(i):
        return (psA if i < 2 else psB)[:, (i % 2) * 512:(i % 2) * 512 + 512]

    s_ld = mksem("s_ld")
    s_x = [mksem(f"s_x{i}") for i in range(2)]
    s_xf = [mksem(f"s_xf{i}") for i in range(2)]
    s_wa = [mksem(f"s_wa{i}") for i in range(2)]
    s_waf = [mksem(f"s_waf{i}") for i in range(2)]
    s_wg = [mksem(f"s_wg{i}") for i in range(2)]
    s_wgf = [mksem(f"s_wgf{i}") for i in range(2)]
    s_wp = [mksem(f"s_wp{i}") for i in range(2)]
    s_wpf = [mksem(f"s_wpf{i}") for i in range(2)]
    s_sf = [mksem(f"s_sf{i}") for i in range(2)]
    s_sff = [mksem(f"s_sff{i}") for i in range(2)]
    s_sq = [mksem(f"s_sq{i}") for i in range(2)]
    s_sqf = [mksem(f"s_sqf{i}") for i in range(2)]
    s_stb = [mksem(f"s_stb{i}") for i in range(2)]
    s_stbf = [mksem(f"s_stbf{i}") for i in range(2)]
    s_scr = mksem("s_scr")
    s_pe = mksem("s_pe")
    s_ve = mksem("s_ve")
    s_se = mksem("s_se")
    s_out = mksem("s_out")
    s_prT = mksem("s_prT")
    s_prM = [mksem(f"s_prM{i}") for i in range(4)]
    s_prS = [mksem(f"s_prS{i}") for i in range(2)]
    s_prX = mksem("s_prX")

    SY, SC, VE, TE, GP = Rec("sync"), Rec("scalar"), Rec("vector"), Rec("tensor"), Rec("gpsimd")

    for dst, src in ((cosb, c_cos), (sinb, c_sin), (lpb, c_lp), (mcb, c_mc),
                     (mwb, c_mw), (idnb, c_iden), (idnf, c_idenf),
                     (onec, c_onec), (oner, c_oner)):
        SY.dma(dst[:], src.ap(), s_ld)
    SY.dma(gn_row[:], gnw_in, s_ld)
    SY.dma(scl_row[:], scaler_in, s_ld)

    VE.call("memset", tiny[:], 0.0)
    VE.call("memset", eps6[:], 1e-6)
    VE.call("memset", eps5[:], 1e-5)
    VE.call("memset", epsF[:], FFN_EPS)
    VE.call("memset", neghalf[:], -0.5)

    TE.wait(s_ld)
    TE.call("matmul", aux[:, 0:DH], lhsT=oner[:], rhs=gn_row[:], start=True, stop=True)
    pev = TE.op("matmul", aux[:, 128:128 + H], lhsT=oner[:], rhs=scl_row[:],
                start=True, stop=True, inc=s_pe)
    SC.wait(s_pe, pev)
    SC.call("copy", out=gnb[:], in_=aux[:, 0:DH])
    sev0 = SC.op("copy", out=sclb[:], in_=aux[:, 128:128 + H], inc=s_se)
    SC.op("copy", out=tiny[:1, 0:2], in_=tiny[:1, 4:6], inc=s_prX)

    jobs = []
    for (src, dst, dout) in ((wq_in, wq_s, H * DH), (wk_in, wk_s, KVH * DH),
                             (wv_in, wv_s, KVH * DH), (wg_in, wg_s, D)):
        for rb in range(D // 128):
            jobs.append(("cast", src, dst, rb, dout))
    for rb in range(2 * HID // 128):
        jobs.append(("qgu", wgu_in, gut_s, rb, D))
    for rb in range(D // 128):
        for cb in range(HID // D):
            jobs.append(("qpr", wpr_in, prt_s, (rb, cb), D))

    g64 = lambda ap, w: ap[:, 0:w].rearrange("p (g e) -> p g e", e=64)
    g1 = lambda ap, w: ap[:, 0:w // 64].rearrange("p (g o) -> p g o", o=1)

    for ji, (mode, src, dst, rb, w) in enumerate(jobs):
        sl = ji % 2
        if ji >= 2:
            SY.wait(s_sff[sl])
        if mode == "qpr":
            r, cbb = rb
            src_ap = src[r * 128:(r + 1) * 128, cbb * D:(cbb + 1) * D]
        else:
            src_ap = src[rb * 128:(rb + 1) * 128, 0:w]
        ldv = SY.dma(stf[sl][:, 0:w], src_ap, s_sf[sl])

        if mode == "cast":
            SC.wait(s_sf[sl], ldv)
            if ji >= 2:
                SC.wait(s_sqf[sl])
            qv = SC.op("copy", out=carve(st_q[sl], w), in_=stf[sl][:, 0:w],
                       inc=s_sq[sl])
            SC.op("copy", out=tiny[:1, 0:2], in_=tiny[:1, 4:6], inc=s_sff[sl])
            SY.wait(s_sq[sl], qv)
            scv = SY.dma(dst.ap()[rb * 128:(rb + 1) * 128, 0:w],
                         carve(st_q[sl], w), s_scr)
            SY.wait(s_scr, scv)
            SY.call("sem_inc", s_sqf[sl].h, 1)
            s_sqf[sl].inc()
        else:
            SC.wait(s_sf[sl], ldv)
            sev = SC.op("copy", out=stb[:, 0:w], in_=stf[sl][:, 0:w], inc=s_se)
            SC.op("copy", out=tiny[:1, 0:2], in_=tiny[:1, 4:6], inc=s_sff[sl])
            VE.wait(s_se, sev)
            VE.call("tensor_reduce", g1(st_am[:], w), g64(stb[:], w), axis=AX.X,
                    op=ALU.add, apply_absolute_value=True)
            VE.call("drain")
            VE.call("tensor_scalar", st_am3[:, 0:w // 64], st_am[:, 0:w // 64],
                    scalar1=1.0 / 64.0, scalar2=1e-8, op0=ALU.mult, op1=ALU.max)
            VE.call("drain")
            vev = VE.op("tensor_copy", st_sc[:, 0:w // 64], st_am3[:, 0:w // 64],
                        inc=s_ve)
            SC.wait(s_ve, vev)
            SC.call("activation", out=st_ln[:, 0:w // 64], in_=st_sc[:, 0:w // 64],
                    func=AF.Ln)
            sev = SC.op("activation", out=st_rs[:, 0:w // 64], in_=st_ln[:, 0:w // 64],
                        func=AF.Exp, scale=-1.0, inc=s_se)
            VE.wait(s_se, sev)
            vev = VE.op("tensor_mul", g64(st_qb[:], w), g64(stb[:], w),
                        g1(st_rs[:], w).broadcast_to([128, w // 64, 64]), inc=s_ve)
            SC.wait(s_ve, vev)
            SC.call("activation", out=st_A[:, 0:w], in_=st_qb[:, 0:w], func=AF.Abs)
            SC.call("activation", out=st_A[:, 0:w], in_=st_A[:, 0:w], func=AF.Relu,
                    bias=neghalf[:, 0:1])
            SC.call("activation", out=st_A[:, 0:w], in_=st_A[:, 0:w], func=AF.Sign)
            sev = SC.op("activation", out=st_B[:, 0:w], in_=st_qb[:, 0:w],
                        func=AF.Sign, inc=s_se)
            VE.wait(s_se, sev)
            if ji >= 2:
                VE.wait(s_sqf[sl])
            VE.call("tensor_mul", st_B[:, 0:w], st_B[:, 0:w], st_A[:, 0:w])
            VE.call("tensor_mul", g64(carve(st_what), w), g64(st_B[:], w),
                    g1(st_sc[:], w).broadcast_to([128, w // 64, 64]))
            VE.call("tensor_sub", carve(st_dd, w), carve(st_what, w), stb[:, 0:w])
            qv = VE.op("tensor_add", carve(st_q[sl], w), stb[:, 0:w],
                       carve(st_dd, w), inc=s_sq[sl])
            TE.wait(s_sq[sl], qv)
            tsl = sl
            for hhf in range(2):
                TE.wait(s_prT)
                for k in range(8):
                    kk = hhf * 8 + k
                    if k == 7:
                        pev = TE.op("transpose", ptA[:, k * 128:(k + 1) * 128],
                                    in_=carve(st_q[sl], w)[:, kk * 128:(kk + 1) * 128],
                                    identity=idnb[:], inc=s_pe)
                    else:
                        TE.call("transpose", ptA[:, k * 128:(k + 1) * 128],
                                in_=carve(st_q[sl], w)[:, kk * 128:(kk + 1) * 128],
                                identity=idnb[:])
                SC.wait(s_pe, pev)
                if hhf == 0 and ji >= 2:
                    SC.wait(s_stbf[tsl])
                ev = SC.op("copy",
                           out=carve(st_tb[tsl], w)[:, hhf * 1024:(hhf + 1) * 1024],
                           in_=ptA[:], inc=s_stb[tsl])
                SC.op("copy", out=tiny[:1, 0:2], in_=tiny[:1, 4:6], inc=s_prT)
            SC.call("sem_inc", s_sqf[sl].h, 1)
            s_sqf[sl].inc()
            SY.wait(s_stb[tsl])
            if mode == "qgu":
                scv = SY.dma(dst.ap()[rb * 128:(rb + 1) * 128, 0:D],
                             carve(st_tb[tsl], D), s_scr)
            else:
                r, cbb = rb
                dv = dst.ap()[cbb * D:(cbb + 1) * D, r * 128:(r + 1) * 128] \
                    .rearrange("(k i) j -> i k j", i=128)
                sv = carve(st_tb[tsl], D).rearrange("p (k j) -> p k j", k=16)
                scv = SY.dma(dv, sv, s_scr)
            SY.wait(s_scr, scv)
            SY.call("sem_inc", s_stbf[tsl].h, 1)
            s_stbf[tsl].inc()

    SY.wait(s_scr)
    SY.call("nop")


    airot = 0
    sqt_free_v = 0
    comb_done_v = 0
    ln_apply_v = 0

    def pe_wait_releases():
        for s in (s_prT, s_prS[0], s_prS[1], s_prX, *s_prM):
            TE.wait(s)

    GP.wait(s_sff[0])
    GP.wait(s_sff[1])

    for c in range(n_chunks):
        seq = c // (T // CHUNK)
        half = c % (T // CHUNK)
        tok0 = half * CHUNK
        gt0 = tok0 // 128

        for tt in range(TT):
            sl = tt % 2
            GP.wait(s_xf[sl])
            xv = GP.dma(xo[sl][:],
                        x_in[seq, tok0 + tt * 128: tok0 + (tt + 1) * 128, :], s_x[sl])
            SC.wait(s_x[sl], xv)
            if ln_apply_v:
                SC.wait(s_ve, ln_apply_v)
            SC.call("activation", out=xn_f[:], in_=xo[sl][:], func=AF.Identity,
                    accum_out=sums2[:, 0:1])
            SC.call("activation", out=xn_f[:], in_=xo[sl][:], func=AF.Square,
                    accum_out=sums2[:, 1:2])
            SC.call("mul", out=meanv[:], in_=sums2[:, 0:1], mul=1.0 / D)
            SC.call("mul", out=negmean[:], in_=meanv[:], mul=-1.0)
            SC.call("activation", out=m2v[:], in_=meanv[:], func=AF.Square)
            SC.call("mul", out=negm2[:], in_=m2v[:], mul=-1.0)
            SC.call("drain")
            SC.call("activation", out=varv[:], in_=sums2[:, 1:2], func=AF.Identity,
                    scale=1.0 / D, bias=negm2[:, 0:1])
            SC.call("drain")
            SC.call("activation", out=lnt1[:], in_=varv[:], func=AF.Ln, bias=eps5[:, 0:1])
            sev = SC.op("activation", out=rstd1[:], in_=lnt1[:], func=AF.Exp,
                        scale=-0.5, inc=s_se)
            VE.wait(s_se, sev)
            vev = VE.op("tensor_scalar", xn_f[:], xo[sl][:], scalar1=negmean[:, 0:1],
                        scalar2=rstd1[:, 0:1], op0=ALU.add, op1=ALU.mult, inc=s_ve)
            ln_apply_v = vev
            VE.op("tensor_copy", tiny[:1, 6:8], xn_f[:1, 0:2], inc=s_xf[sl])
            SC.wait(s_ve, vev)
            sev = SC.op("copy", out=xn_b[:], in_=xn_f[:], inc=s_se)
            TE.wait(s_se, sev)
            for hhf in range(2):
                TE.wait(s_prT)
                for k in range(8):
                    kk = hhf * 8 + k
                    if k == 7:
                        pev = TE.op("transpose", ptA[:, k * 128:(k + 1) * 128],
                                    in_=xn_b[:, kk * 128:(kk + 1) * 128],
                                    identity=idnb[:], inc=s_pe)
                    else:
                        TE.call("transpose", ptA[:, k * 128:(k + 1) * 128],
                                in_=xn_b[:, kk * 128:(kk + 1) * 128], identity=idnb[:])
                SC.wait(s_pe, pev)
                ov = xnT[:].rearrange("p (dk t) -> p dk t", t=CHUNK)[
                    :, hhf * 8:(hhf + 1) * 8, tt * 128:(tt + 1) * 128]
                iv = ptA[:].rearrange("p (dk t) -> p dk t", t=128)
                SC.call("copy", out=ov, in_=iv)
                SC.op("copy", out=tiny[:1, 0:2], in_=tiny[:1, 4:6], inc=s_prT)
        sev_xnT = SC.op("copy", out=tiny[:1, 2:4], in_=tiny[:1, 4:6], inc=s_se)

        mirot = 0
        pev_ss_prev = 0
        silut_ev_v = 0
        for p in range(4):
            for hc in range(16):
                mg, mu = mirot % 4, (mirot + 1) % 4
                mirot += 2
                wgver = {}
                for which, mi in (("g", mg), ("u", mu)):
                    hb = (0 if which == "g" else 64) + p * 16 + hc
                    sl = (2 * (p * 16 + hc) + (0 if which == "g" else 1)) % 2
                    SY.wait(s_wgf[sl])
                    wv_ = SY.dma(gsl[sl][:], gut_s.ap()[hb * 128:(hb + 1) * 128, :],
                                 s_wg[sl])
                    TE.wait(s_wg[sl], wv_)
                    if p == 0 and hc == 0 and which == "g":
                        TE.wait(s_prX)
                        TE.wait(s_se, sev_xnT)
                        pe_wait_releases()
                    TE.wait(s_prM[mi])
                    for dk in range(DK):
                        if dk == DK - 1:
                            wgver[which] = TE.op(
                                "matmul", pm(mi),
                                lhsT=gsl[sl][:, dk * 128:(dk + 1) * 128],
                                rhs=xnT[:, dk * CHUNK:(dk + 1) * CHUNK],
                                start=False, stop=True, inc=s_wgf[sl])
                        else:
                            TE.call("matmul", pm(mi),
                                    lhsT=gsl[sl][:, dk * 128:(dk + 1) * 128],
                                    rhs=xnT[:, dk * CHUNK:(dk + 1) * CHUNK],
                                    start=(dk == 0), stop=False)
                    wgver[which + "sl"] = sl
                SC.wait(s_wgf[wgver["gsl"]], wgver["g"])
                sev = SC.op("activation", out=silut[:], in_=pm(mg), func=AF.Silu,
                            inc=s_se)
                SC.op("copy", out=tiny[:1, 0:2], in_=tiny[:1, 4:6], inc=s_prM[mg])
                VE.wait(s_se, sev)
                VE.wait(s_wgf[wgver["usl"]], wgver["u"])
                hv = VE.op("tensor_mul", hpc[:, hc * CHUNK:(hc + 1) * CHUNK],
                           silut[:], pm(mu), inc=s_ve)
                VE.op("tensor_copy", tiny[:1, 2:4], silut[:1, 0:2], inc=s_prM[mu])
                SC.wait(s_ve, hv)
                if pev_ss_prev:
                    SC.wait(s_pe, pev_ss_prev)
                sev = SC.op("activation", out=h2t[:],
                            in_=hpc[:, hc * CHUNK:(hc + 1) * CHUNK],
                            func=AF.Square, inc=s_se)
                TE.wait(s_se, sev)
                for tt in range(TT):
                    first = (p == 0 and hc == 0)
                    last = (p == 3 and hc == 15)
                    if tt == TT - 1:
                        pev_ss = TE.op("matmul", aux[:, tt:tt + 1],
                                       lhsT=h2t[:, tt * 128:(tt + 1) * 128],
                                       rhs=onec[:], start=first, stop=last, inc=s_pe)
                    else:
                        TE.call("matmul", aux[:, tt:tt + 1],
                                lhsT=h2t[:, tt * 128:(tt + 1) * 128], rhs=onec[:],
                                start=first, stop=last)
                pev_ss_prev = pev_ss
            if p == 3:
                SC.wait(s_pe, pev_ss)
                SC.call("activation", out=lnt_t[:], in_=aux[:, 0:TT], func=AF.Ln,
                        scale=1.0 / HID, bias=epsF[:, 0:1])
                sev_rstd = SC.op("activation", out=rstd_t[:], in_=lnt_t[:],
                                 func=AF.Exp, scale=-0.5, inc=s_se)
                SC.op("copy", out=tiny[:1, 0:2], in_=tiny[:1, 4:6], inc=s_prX)
            for dc in range(4):
                for tt in range(TT):
                    TE.wait(s_prM[tt])
                wpver = [None, None]
                for hc in range(16):
                    sl = hc % 2
                    SY.wait(s_wpf[sl])
                    wv_ = SY.dma(psl[sl][:],
                                 prt_s.ap()[p * 2048 + hc * 128:
                                            p * 2048 + (hc + 1) * 128,
                                            dc * 512:(dc + 1) * 512], s_wp[sl])
                    TE.wait(s_wp[sl], wv_)
                    for tt in range(TT):
                        if tt == TT - 1:
                            wpver[sl] = TE.op(
                                "matmul", pm(tt),
                                lhsT=hpc[:, hc * CHUNK + tt * 128:
                                         hc * CHUNK + (tt + 1) * 128],
                                rhs=psl[sl][:], start=(hc == 0), stop=(hc == 15),
                                inc=s_wpf[sl])
                        else:
                            TE.call("matmul", pm(tt),
                                    lhsT=hpc[:, hc * CHUNK + tt * 128:
                                             hc * CHUNK + (tt + 1) * 128],
                                    rhs=psl[sl][:], start=(hc == 0), stop=(hc == 15))
                for sl in range(2):
                    if wpver[sl] is not None:
                        SC.wait(s_wpf[sl], wpver[sl])
                        VE.wait(s_wpf[sl], wpver[sl])
                for tt in range(TT):
                    dst = ggf[:, tt * D + dc * 512: tt * D + (dc + 1) * 512]
                    if p == 0:
                        if dc == 0 and tt == 0 and comb_done_v:
                            SC.wait(s_ve, comb_done_v)
                        SC.call("copy", out=dst, in_=pm(tt))
                        SC.op("copy", out=tiny[:1, 0:2], in_=tiny[:1, 4:6],
                              inc=s_prM[tt])
                    elif p < 3:
                        VE.call("tensor_add", dst, dst, pm(tt))
                        VE.op("tensor_copy", tiny[:1, 0:2], silut[:1, 0:2],
                              inc=s_prM[tt])
                    else:
                        VE.wait(s_se, sev_rstd)
                        if silut_ev_v:
                            VE.wait(s_se, silut_ev_v)
                        VE.call("tensor_add", silut[:], dst, pm(tt))
                        VE.op("tensor_copy", tiny[:1, 0:2], silut[:1, 0:2],
                              inc=s_prM[tt])
                        VE.call("drain")
                        VE.call("tensor_scalar", silut[:], silut[:],
                                scalar1=rstd_t[:, tt:tt + 1], scalar2=None,
                                op0=ALU.mult)
                        vev = VE.op("tensor_copy", tiny[:1, 4:6], silut[:1, 0:2],
                                    inc=s_ve)
                        SC.wait(s_ve, vev)
                        silut_ev_v = SC.op("activation", out=dst, in_=silut[:],
                                           func=AF.Silu, inc=s_se)
        sev_gg = SC.op("copy", out=tiny[:1, 2:4], in_=tiny[:1, 4:6], inc=s_se)

        awrot = 0
        mirot = 0
        sqt_free_v = sqt_free_v if c > 0 else 0

        def attn_piece(src_s, colblk, width=256):
            nonlocal awrot
            sl = awrot % 2
            awrot += 1
            SY.wait(s_waf[sl])
            v = SY.dma(wsl[sl][:, 0:DK * width].rearrange("p (k n) -> p k n", n=width),
                       src_s.ap().rearrange("(k p) n -> p k n", p=128)[
                           :, :, colblk * width:(colblk + 1) * width], s_wa[sl])
            return sl, v

        def proj_tm(src_s, colblk, tgt, tt, width=256, last_inc=None):
            sl, wv_ = attn_piece(src_s, colblk, width)
            TE.wait(s_wa[sl], wv_)
            ver = None
            for dk in range(DK):
                if dk == DK - 1:
                    ver = TE.op("matmul", tgt,
                                lhsT=xnT[:, dk * CHUNK + tt * 128:
                                         dk * CHUNK + (tt + 1) * 128],
                                rhs=wsl[sl][:, dk * width:(dk + 1) * width],
                                start=False, stop=True, inc=s_waf[sl])
                else:
                    TE.call("matmul", tgt,
                            lhsT=xnT[:, dk * CHUNK + tt * 128:
                                     dk * CHUNK + (tt + 1) * 128],
                            rhs=wsl[sl][:, dk * width:(dk + 1) * width],
                            start=(dk == 0), stop=False)
            return sl, ver

        for tt in range(TT):
            gq = gt0 + tt
            mik = mirot % 4
            mirot += 1
            TE.wait(s_prM[mik])
            vers = []
            for cb2 in range(2):
                sl, ver = proj_tm(wk_s, cb2, pm(mik)[:, cb2 * 256:(cb2 + 1) * 256], tt)
                vers.append((sl, ver))
            for sl, ver in vers:
                SC.wait(s_waf[sl], ver)
            SC.call("copy", out=ktm[:], in_=pm(mik))
            if sqt_free_v:
                SC.wait(s_ve, sqt_free_v)
            sev = SC.op("activation", out=sqt[:], in_=ktm[:], func=AF.Square, inc=s_se)
            VE.wait(s_se, sev)
            VE.call("tensor_reduce",
                    red16[:, 0:4].rearrange("p (g o) -> p g o", o=1),
                    sqt[:].rearrange("p (g e) -> p g e", e=DH), axis=AX.X, op=ALU.add)
            VE.call("drain")
            vev = VE.op("tensor_copy", tiny[:1, 0:2], red16[:1, 0:2], inc=s_ve)
            SC.wait(s_ve, vev)
            SC.call("activation", out=lnt16[:, 0:4], in_=red16[:, 0:4], func=AF.Ln,
                    scale=1.0 / DH, bias=eps6[:, 0:1])
            SC.op("copy", out=tiny[:1, 0:2], in_=tiny[:1, 4:6], inc=s_prM[mik])
            sev = SC.op("activation", out=rstd16[:, 0:4], in_=lnt16[:, 0:4],
                        func=AF.Exp, scale=-0.5, inc=s_se)
            VE.wait(s_se, sev)
            VE.call("tensor_mul",
                    ktm[:].rearrange("p (g e) -> p g e", e=DH),
                    ktm[:].rearrange("p (g e) -> p g e", e=DH),
                    rstd16[:, 0:4].rearrange("p (g o) -> p g o", o=1)
                    .broadcast_to([128, 4, DH]))
            for kv in range(KVH):
                x1 = ktm[:, kv * DH: kv * DH + 64]
                x2 = ktm[:, kv * DH + 64: kv * DH + DH]
                ct = cosb[:, gq * 64:(gq + 1) * 64]
                st = sinb[:, gq * 64:(gq + 1) * 64]
                VE.call("tensor_mul", rt1[:, 0:64], x1, st)
                VE.call("tensor_mul", rt2[:, 0:64], x2, st)
                VE.call("tensor_mul", x1, x1, ct)
                VE.call("tensor_mul", x2, x2, ct)
                VE.call("tensor_add", x1, x1, rt2[:, 0:64])
                VE.call("tensor_sub", x2, x2, rt1[:, 0:64])
            vev = VE.op("tensor_copy", tiny[:1, 0:2], ktm[:1, 0:2], inc=s_ve)
            SC.wait(s_ve, vev)
            kb = prodt[:, 0:512]
            sev = SC.op("copy", out=kb, in_=ktm[:], inc=s_se)
            TE.wait(s_se, sev)
            TE.wait(s_prT)
            for kv in range(KVH):
                if kv == KVH - 1:
                    pev = TE.op("transpose", ptA[:, kv * 128:(kv + 1) * 128],
                                in_=kb[:, kv * 128:(kv + 1) * 128],
                                identity=idnb[:], inc=s_pe)
                else:
                    TE.call("transpose", ptA[:, kv * 128:(kv + 1) * 128],
                            in_=kb[:, kv * 128:(kv + 1) * 128], identity=idnb[:])
            SC.wait(s_pe, pev)
            ovk = kTc[:].rearrange("p (kv t) -> p kv t", t=T)[
                :, :, tok0 + tt * 128: tok0 + (tt + 1) * 128]
            SC.call("copy", out=ovk,
                    in_=ptA[:, 0:512].rearrange("p (kv t) -> p kv t", t=128))
            SC.op("copy", out=tiny[:1, 0:2], in_=tiny[:1, 4:6], inc=s_prT)
            miv = mirot % 4
            mirot += 1
            TE.wait(s_prM[miv])
            vers = []
            for cb2 in range(2):
                sl, ver = proj_tm(wv_s, cb2, pm(miv)[:, cb2 * 256:(cb2 + 1) * 256], tt)
                vers.append((sl, ver))
            for sl, ver in vers:
                SC.wait(s_waf[sl], ver)
            ovv = vcc[:].rearrange("p (kv kt e) -> p kv kt e", kv=KVH, e=DH)[
                :, :, gq, :]
            SC.call("copy", out=ovv,
                    in_=pm(miv).rearrange("p (kv e) -> p kv e", e=DH))
            SC.op("copy", out=tiny[:1, 0:2], in_=tiny[:1, 4:6], inc=s_prM[miv])
            for cbp in range(4):
                mig = mirot % 4
                mirot += 1
                TE.wait(s_prM[mig])
                vers = []
                for cb2 in range(2):
                    sl, ver = proj_tm(wg_s, cbp * 2 + cb2,
                                      pm(mig)[:, cb2 * 256:(cb2 + 1) * 256], tt)
                    vers.append((sl, ver))
                for sl, ver in vers:
                    SC.wait(s_waf[sl], ver)
                if tt == 0 and cbp == 0 and comb_done_v:
                    SC.wait(s_ve, comb_done_v)
                SC.call("activation",
                        out=sgb[:, tt * D + cbp * 512: tt * D + (cbp + 1) * 512],
                        in_=pm(mig), func=AF.Silu)
                SC.op("copy", out=tiny[:1, 0:2], in_=tiny[:1, 4:6], inc=s_prM[mig])

        for s_ in s_prM:
            TE.wait(s_)
        for gi in range(KVH):
            for hg in range(GRP):
                h = gi * GRP + hg
                sl, wv_ = attn_piece(wq_s, h, 128)
                TE.wait(s_wa[sl], wv_)
                TE.wait(s_prX)
                qver = None
                for tt in range(TT):
                    for dk in range(DK):
                        lastmm = (tt == TT - 1 and dk == DK - 1)
                        if lastmm:
                            qver = TE.op(
                                "matmul", aux[:, tt * 128:(tt + 1) * 128],
                                lhsT=xnT[:, dk * CHUNK + tt * 128:
                                         dk * CHUNK + (tt + 1) * 128],
                                rhs=wsl[sl][:, dk * 128:(dk + 1) * 128],
                                start=False, stop=True, inc=s_waf[sl])
                        else:
                            TE.call("matmul", aux[:, tt * 128:(tt + 1) * 128],
                                    lhsT=xnT[:, dk * CHUNK + tt * 128:
                                             dk * CHUNK + (tt + 1) * 128],
                                    rhs=wsl[sl][:, dk * 128:(dk + 1) * 128],
                                    start=(dk == 0), stop=(dk == DK - 1))
                SC.wait(s_waf[sl], qver)
                SC.call("copy", out=qf[:], in_=aux[:])
                if sqt_free_v:
                    SC.wait(s_ve, sqt_free_v)
                sev = SC.op("activation", out=sqt[:], in_=qf[:], func=AF.Square,
                            inc=s_se)
                VE.wait(s_se, sev)
                VE.call("tensor_reduce",
                        red16[:, 4:8].rearrange("p (g o) -> p g o", o=1),
                        sqt[:].rearrange("p (g e) -> p g e", e=DH), axis=AX.X,
                        op=ALU.add)
                VE.call("drain")
                vev = VE.op("tensor_copy", tiny[:1, 0:2], red16[:1, 4:6], inc=s_ve)
                SC.wait(s_ve, vev)
                SC.call("activation", out=lnt16[:, 4:8], in_=red16[:, 4:8],
                        func=AF.Ln, scale=1.0 / DH, bias=eps6[:, 0:1])
                SC.op("copy", out=tiny[:1, 0:2], in_=tiny[:1, 4:6], inc=s_prX)
                sev = SC.op("activation", out=rstd16[:, 4:8], in_=lnt16[:, 4:8],
                            func=AF.Exp, scale=-0.5, inc=s_se)
                VE.wait(s_se, sev)
                VE.call("tensor_mul",
                        qf[:].rearrange("p (g e) -> p g e", e=DH),
                        qf[:].rearrange("p (g e) -> p g e", e=DH),
                        rstd16[:, 4:8].rearrange("p (g o) -> p g o", o=1)
                        .broadcast_to([128, 4, DH]))
                for tt in range(TT):
                    gq = gt0 + tt
                    x1 = qf[:, tt * 128: tt * 128 + 64]
                    x2 = qf[:, tt * 128 + 64: tt * 128 + 128]
                    ct = cosb[:, gq * 64:(gq + 1) * 64]
                    st = sinb[:, gq * 64:(gq + 1) * 64]
                    VE.call("tensor_mul", rt1[:, 0:64], x1, st)
                    VE.call("tensor_mul", rt2[:, 0:64], x2, st)
                    VE.call("tensor_mul", x1, x1, ct)
                    VE.call("tensor_mul", x2, x2, ct)
                    VE.call("tensor_add", x1, x1, rt2[:, 0:64])
                    VE.call("tensor_sub", x2, x2, rt1[:, 0:64])
                    VE.call("tensor_scalar", qf[:, tt * 128:(tt + 1) * 128],
                            qf[:, tt * 128:(tt + 1) * 128],
                            scalar1=sclb[:, h:h + 1],
                            scalar2=lpb[:, gq:gq + 1], op0=ALU.mult, op1=ALU.mult)
                vev = VE.op("tensor_copy", tiny[:1, 0:2], qf[:1, 0:2], inc=s_ve)
                SC.wait(s_ve, vev)
                qcb = prodt[:, 512:1024]
                sev = SC.op("copy", out=qcb, in_=qf[:], inc=s_se)
                TE.wait(s_se, sev)
                TE.wait(s_prT)
                for tt in range(TT):
                    if tt == TT - 1:
                        pev = TE.op("transpose", ptA[:, tt * 128:(tt + 1) * 128],
                                    in_=qcb[:, tt * 128:(tt + 1) * 128],
                                    identity=idnb[:], inc=s_pe)
                    else:
                        TE.call("transpose", ptA[:, tt * 128:(tt + 1) * 128],
                                in_=qcb[:, tt * 128:(tt + 1) * 128], identity=idnb[:])
                SC.wait(s_pe, pev)
                SC.call("copy", out=qgb[:, hg * 512:(hg + 1) * 512], in_=ptA[:, 0:512])
                sev_q = SC.op("copy", out=tiny[:1, 0:2], in_=tiny[:1, 4:6], inc=s_prT)
            sev_qg = SC.op("copy", out=tiny[:1, 2:4], in_=tiny[:1, 4:6], inc=s_se)

            pairs = [(t_, h_) for t_ in range(TT) for h_ in range(GRP)]
            airot0 = airot
            sc_pev = {}

            def emit_scores(idx):
                tt_, hg_ = pairs[idx]
                gq_ = gt0 + tt_
                kt0_ = max(0, gq_ - 8)
                nkt_ = gq_ + 1 - kt0_
                pp_ = (airot0 + idx) % 2
                buf_ = psA if pp_ == 0 else psB
                TE.wait(s_prS[pp_])
                if idx == 0:
                    TE.wait(s_se, sev_qg)
                done_ = 0
                nspans_ = (nkt_ + 3) // 4
                for sp_ in range(nspans_):
                    ln2 = min(4, nkt_ - done_)
                    ds_ = buf_[:, done_ * 128:(done_ + ln2) * 128]
                    lh_ = qgb[:, hg_ * 512 + tt_ * 128:hg_ * 512 + (tt_ + 1) * 128]
                    rh_ = kTc[:, gi * T + (kt0_ + done_) * 128:
                              gi * T + (kt0_ + done_ + ln2) * 128]
                    if sp_ == nspans_ - 1:
                        sc_pev[idx] = TE.op("matmul", ds_, lhsT=lh_, rhs=rh_,
                                            start=True, stop=True, inc=s_pe)
                    else:
                        TE.call("matmul", ds_, lhsT=lh_, rhs=rh_,
                                start=True, stop=True)
                    done_ += ln2

            mn_vev = {}

            def emit_mask_negmax(idx):
                tt_, hg_ = pairs[idx]
                gq_ = gt0 + tt_
                nkt_ = gq_ + 1 - max(0, gq_ - 8)
                pp_ = (airot0 + idx) % 2
                buf_ = psA if pp_ == 0 else psB
                VE.wait(s_pe, sc_pev[idx])
                VE.call("tensor_add", buf_[:, (nkt_ - 1) * 128: nkt_ * 128],
                        buf_[:, (nkt_ - 1) * 128: nkt_ * 128], mcb[:])
                if nkt_ == 9:
                    VE.call("tensor_add", buf_[:, 0:128], buf_[:, 0:128], mwb[:])
                VE.call("reduce_max", nmx[pp_][:], buf_[:, 0:nkt_ * 128],
                        axis=AX.X, negate=True)
                mn_vev[idx] = VE.op("tensor_copy", tiny[:1, 0:1],
                                    nmx[pp_][:1, 0:1], inc=s_ve)

            emit_scores(0)
            emit_mask_negmax(0)
            for idx in range(len(pairs)):
                tt, hg = pairs[idx]
                gq = gt0 + tt
                kt0 = max(0, gq - 8)
                nkt = gq + 1 - kt0
                if True:
                    pp = airot % 2
                    buf = psA if pp == 0 else psB
                    SC.wait(s_ve, mn_vev[idx])
                    SC.call("activation", out=p_bf[pp][:, 0:nkt * 128],
                            in_=buf[:, 0:nkt * 128], func=AF.Exp,
                            bias=nmx[pp][:, 0:1], accum_out=lsum[pp][:])
                    if nkt == 9:
                        SC.call("activation", out=p_dg[:], in_=buf[:, 1024:1152],
                                func=AF.Exp, bias=nmx[pp][:, 0:1])
                    SC.call("activation", out=lntA[pp][:], in_=lsum[pp][:], func=AF.Ln)
                    sev = SC.op("activation", out=ilv[pp][:], in_=lntA[pp][:],
                                func=AF.Exp, scale=-1.0, inc=s_se)
                    if idx + 1 < len(pairs):
                        emit_scores(idx + 1)
                        emit_mask_negmax(idx + 1)
                    TE.wait(s_se, sev)
                    TE.wait(s_prT)
                    nbt = min(nkt, 8)
                    for j in range(nbt):
                        if j == nbt - 1:
                            pev = TE.op("transpose", ptA[:, j * 128:(j + 1) * 128],
                                        in_=p_bf[pp][:, j * 128:(j + 1) * 128],
                                        identity=idnb[:], inc=s_pe)
                        else:
                            TE.call("transpose", ptA[:, j * 128:(j + 1) * 128],
                                    in_=p_bf[pp][:, j * 128:(j + 1) * 128],
                                    identity=idnb[:])
                    if nkt == 9:
                        TE.wait(s_prX)
                        pev = TE.op("transpose", aux[:, 0:128], in_=p_dg[:],
                                    identity=idnf[:], inc=s_pe)
                    SC.wait(s_pe, pev)
                    SC.call("copy", out=pT_sb[pp][:, 0:nbt * 128], in_=ptA[:, 0:nbt * 128])
                    SC.op("copy", out=tiny[:1, 0:1], in_=tiny[:1, 4:5], inc=s_prT)
                    if nkt == 9:
                        SC.call("copy", out=pT_sb[pp][:, 1024:1152], in_=aux[:, 0:128])
                        SC.op("copy", out=tiny[:1, 0:1], in_=tiny[:1, 4:5],
                              inc=s_prX)
                    sev = SC.op("copy", out=tiny[:1, 2:3], in_=tiny[:1, 4:5],
                                inc=s_se)
                    TE.wait(s_se, sev)
                    for j in range(nkt):
                        if j == nkt - 1:
                            pev = TE.op("matmul", buf[:, 0:128],
                                        lhsT=pT_sb[pp][:, j * 128:(j + 1) * 128],
                                        rhs=vcc[:, (gi * NKT + kt0 + j) * DH:
                                                (gi * NKT + kt0 + j + 1) * DH],
                                        start=(j == 0), stop=True, inc=s_pe)
                        else:
                            TE.call("matmul", buf[:, 0:128],
                                    lhsT=pT_sb[pp][:, j * 128:(j + 1) * 128],
                                    rhs=vcc[:, (gi * NKT + kt0 + j) * DH:
                                            (gi * NKT + kt0 + j + 1) * DH],
                                    start=(j == 0), stop=(j == nkt - 1))
                    VE.wait(s_pe, pev)
                    VE.call("tensor_scalar", y_grp[:, hg * 128:(hg + 1) * 128],
                            buf[:, 0:128], scalar1=ilv[pp][:, 0:1], scalar2=None,
                            op0=ALU.mult)
                    VE.op("tensor_copy", tiny[:1, 0:2], y_grp[:1, 0:2], inc=s_prS[pp])
                    airot += 1
                if hg != GRP - 1:
                    continue
                vev = VE.op("tensor_mul", y2_t[:], y_grp[:],
                            sgb[:, tt * D + gi * 512: tt * D + (gi + 1) * 512],
                            inc=s_ve)
                SC.wait(s_ve, vev)
                sev = SC.op("activation", out=sqt[:], in_=y2_t[:], func=AF.Square,
                            inc=s_se)
                VE.wait(s_se, sev)
                VE.call("tensor_reduce",
                        red16[:, 8:12].rearrange("p (g o) -> p g o", o=1),
                        sqt[:].rearrange("p (g e) -> p g e", e=DH), axis=AX.X,
                        op=ALU.add)
                VE.call("drain")
                vev = VE.op("tensor_copy", tiny[:1, 0:2], red16[:1, 8:10], inc=s_ve)
                sqt_free_v = vev
                SC.wait(s_ve, vev)
                SC.call("activation", out=lnt16[:, 8:12], in_=red16[:, 8:12],
                        func=AF.Ln, scale=1.0 / DH, bias=eps6[:, 0:1])
                sev = SC.op("activation", out=rstd16[:, 8:12], in_=lnt16[:, 8:12],
                            func=AF.Exp, scale=-0.5, inc=s_se)
                VE.wait(s_se, sev)
                VE.call("tensor_mul",
                        y2_t[:].rearrange("p (g e) -> p g e", e=DH),
                        y2_t[:].rearrange("p (g e) -> p g e", e=DH),
                        rstd16[:, 8:12].rearrange("p (g o) -> p g o", o=1)
                        .broadcast_to([128, 4, DH]))
                VE.call("tensor_mul",
                        sgb[:].rearrange("p (t g e) -> p t g e", t=TT, e=DH)[
                            :, tt, gi * GRP:(gi + 1) * GRP, :],
                        y2_t[:].rearrange("p (g e) -> p g e", e=DH),
                        gnb[:].rearrange("p (o e) -> p o e", o=1).broadcast_to([128, 4, DH]))

        VE.wait(s_se, sev_gg)
        for tt in range(TT):
            sl = tt % 2
            GP.wait(s_xf[sl])
            xv = GP.dma(xo[sl][:],
                        x_in[seq, tok0 + tt * 128: tok0 + (tt + 1) * 128, :], s_x[sl])
            VE.wait(s_x[sl], xv)
            VE.call("tensor_mul", prodt[:], sgb[:, tt * D:(tt + 1) * D],
                    ggf[:, tt * D:(tt + 1) * D])
            VE.call("drain")
            vev = VE.op("tensor_add", xo[sl][:], xo[sl][:], prodt[:], inc=s_ve)
            comb_done_v = vev
            GP.wait(s_ve, vev)
            ov = GP.dma(out_d[seq, tok0 + tt * 128: tok0 + (tt + 1) * 128, :],
                        xo[sl][:], s_out)
            GP.wait(s_out, ov)
            GP.call("sem_inc", s_xf[sl].h, 1)
            s_xf[sl].inc()

    if dbg:
        GP.dma(dbg_sg, sgb[:], s_out)
        GP.dma(dbg_gg, ggf[:], s_out)
        GP.dma(dbg_xnT, xnT[:], s_out)
        GP.dma(dbg_kT, kTc[:], s_out)
        GP.dma(dbg_v, vcc[:], s_out)
        GP.wait(s_out)

    GP.wait(s_out)

    with nc.Block() as block:
        block.sync(lambda e: SY.replay(e))
        block.scalar(lambda e: SC.replay(e))
        block.vector(lambda e: VE.replay(e))
        block.tensor(lambda e: TE.replay(e))
        block.gpsimd(lambda e: GP.replay(e))
    ctx.close()
    psp.close()
    return nc


_NC_CACHE = {}


def kernel(L0_attn_hidden_states, L0_attn_Wq, L0_attn_Wk, L0_attn_Wv, L0_attn_Wg,
           L0_attn_softmax_scaler, L0_attn_gn_weight, L0_ffn_w_gate_up,
           L0_ffn_w_proj, n_chunks=2 * (T // CHUNK)):
    x = np.ascontiguousarray(np.asarray(L0_attn_hidden_states, dtype=np.float32))
    wq = np.ascontiguousarray(np.asarray(L0_attn_Wq, dtype=np.float32))
    wk = np.ascontiguousarray(np.asarray(L0_attn_Wk, dtype=np.float32))
    wv = np.ascontiguousarray(np.asarray(L0_attn_Wv, dtype=np.float32))
    wg = np.ascontiguousarray(np.asarray(L0_attn_Wg, dtype=np.float32))
    scl = np.ascontiguousarray(np.asarray(L0_attn_softmax_scaler,
                                          dtype=np.float32)).reshape(1, H)
    gnw = np.ascontiguousarray(np.asarray(L0_attn_gn_weight,
                                          dtype=np.float32)).reshape(1, DH)
    wgu = np.ascontiguousarray(np.asarray(L0_ffn_w_gate_up, dtype=np.float32))
    wpr = np.ascontiguousarray(np.asarray(L0_ffn_w_proj, dtype=np.float32))

    key = (n_chunks, False)
    if key not in _NC_CACHE:
        _NC_CACHE[key] = build_program(n_chunks)
    nc = _NC_CACHE[key]

    in_maps = []
    for core in range(NCORES):
        xs = np.ascontiguousarray(x[core * SEQ_PER_CORE:(core + 1) * SEQ_PER_CORE])
        in_maps.append({"x": xs, "wq": wq, "wk": wk, "wv": wv, "wg": wg,
                        "scaler": scl, "gnw": gnw, "wgu": wgu, "wpr": wpr})
    res = run_bass_kernel_spmd(nc, in_maps, core_ids=list(range(NCORES)))
    out = np.concatenate([r["out"] for r in res.results], axis=0)
    return out.reshape(B, T, D)
